# version 15
# baseline (speedup 1.0000x reference)
import numpy as np
import ml_dtypes
import concourse.bass as bass
import concourse.mybir as mybir
from concourse.bass_utils import run_bass_kernel_spmd

F32 = mybir.dt.float32
BF16 = mybir.dt.bfloat16
U8 = mybir.dt.uint8
AF = mybir.ActivationFunctionType
ALU = mybir.AluOpType
AXX = mybir.AxisListType.X

EPS = 1e-5
DM = 1024
DI = 2048
DS = 128
NH = 32
HD = 64
CDIM = 2304
SEG = 2048
Q = 128
NCH = SEG // Q
NT = 16
XROWS = SEG + 128
CORES = 8
NH2 = 16

ENGINES = ("tensor", "vector", "scalar", "sync", "gpsimd")


class Op:
    __slots__ = ("idx", "engine", "fn", "deps", "is_dma", "tag", "waits", "incs", "flagged", "late", "dma_idx")

    def __init__(self, idx, engine, fn, deps, is_dma, tag):
        self.idx = idx
        self.engine = engine
        self.fn = fn
        self.deps = deps
        self.is_dma = is_dma
        self.tag = tag
        self.waits = []
        self.incs = []
        self.flagged = False
        self.late = False
        self.dma_idx = -1


class Plan:
    NSLOT = 10

    def __init__(self):
        self.ops = []
        self.last_write = {}
        self.readers = {}
        self.dma_counts = {}

    def op(self, engine, fn, reads=(), writes=(), dma=False, tag="", after=(), late=False):
        deps = set()
        for k in reads:
            if k in self.last_write:
                deps.add(self.last_write[k])
        for k in writes:
            if k in self.last_write:
                deps.add(self.last_write[k])
            for r in self.readers.get(k, ()):
                deps.add(r)
        for o in after:
            deps.add(o.idx)
        o = Op(len(self.ops), engine, fn, deps, dma, tag)
        o.late = late
        if dma:
            o.dma_idx = self.dma_counts.get(engine, 0)
            self.dma_counts[engine] = o.dma_idx + 1
        self.ops.append(o)
        for k in reads:
            self.readers.setdefault(k, []).append(o.idx)
        for k in writes:
            self.last_write[k] = o.idx
            self.readers[k] = []
        return o

    def barrier(self, tag="bar"):
        lasts = {}
        last_dma = {}
        for o in self.ops:
            lasts[o.engine] = o
            if o.is_dma:
                last_dma[(o.engine, o.dma_idx % self.NSLOT)] = o
        deps = list(lasts.values()) + list(last_dma.values())
        for e in ENGINES:
            self.op(e, _mknop(e), after=deps, tag=tag + "_" + e)


    def schedule(self):
        n = len(self.ops)
        dur = [0.0] * n
        for o in self.ops:
            t = o.tag
            if o.is_dma:
                dur[o.idx] = 4.0 if t.startswith(("xsT", "gfT", "uft")) else 2.5
            elif o.engine == "tensor":
                if t.startswith(("xmm", "zmm", "opmm", "dtmm", "yint", "smm", "corr")):
                    dur[o.idx] = 0.25
                elif t.startswith("cnv"):
                    dur[o.idx] = 0.5
                elif t.startswith("hmm"):
                    dur[o.idx] = 0.12
                elif t.startswith("yin"):
                    dur[o.idx] = 0.12
                else:
                    dur[o.idx] = 0.35
            elif o.engine == "vector":
                if t.startswith(("E", "yie", "yadd", "supd", "stmp", "ycorr", "yfull", "sjw", "sacc")):
                    dur[o.idx] = 2.2
                elif t.startswith(("xdt", "xf", "mp", "gate", "dxs", "yD", "ycast", "srb", "Emin")):
                    dur[o.idx] = 1.2
                else:
                    dur[o.idx] = 0.5
            elif o.engine == "scalar":
                if t.startswith(("expL", "nrelu")):
                    dur[o.idx] = 3.6
                elif t.startswith(("zsilu", "ycast", "srb")):
                    dur[o.idx] = 1.8
                elif t.startswith(("silu", "castw", "uhat", "sq", "sp_", "evac", "evh", "foevac", "Tu3e", "g2c", "Tg2e")):
                    dur[o.idx] = 1.0
                else:
                    dur[o.idx] = 0.3
            elif t == "allgather":
                dur[o.idx] = 60.0
            else:
                dur[o.idx] = 0.3
        succ = [[] for _ in range(n)]
        for o in self.ops:
            for d in o.deps:
                succ[d].append(o.idx)
        prio = [0.0] * n
        for i in range(n - 1, -1, -1):
            m = 0.0
            for s in succ[i]:
                if prio[s] > m:
                    m = prio[s]
            prio[i] = dur[i] + m
        regions = []
        i = 0
        ops = self.ops
        while i < n:
            isbar = ops[i].tag.startswith("bar")
            j = i
            while j < n and ops[j].tag.startswith("bar") == isbar:
                j += 1
            regions.append(("bar" if isbar else "work", list(range(i, j))))
            i = j
        import heapq
        order = []
        fin = [0.0] * n
        t_now = 0.0
        done = [False] * n
        for kind, idxs in regions:
            if kind == "bar":
                for i2 in idxs:
                    order.append(i2)
                    fin[i2] = t_now
                    done[i2] = True
                continue
            remaining = set(idxs)
            indeg = {}
            ready = []
            for i2 in idxs:
                k = sum(1 for d in self.ops[i2].deps if d in remaining)
                indeg[i2] = k
                if k == 0:
                    heapq.heappush(ready, (-prio[i2], i2))
            eng_free = {e: t_now for e in ENGINES}
            while remaining:
                progressed = False
                for e in ENGINES:
                    if not ready:
                        break
                    best = None
                    tmp = []
                    while ready:
                        p, i2 = heapq.heappop(ready)
                        if self.ops[i2].engine == e and best is None:
                            best = (p, i2)
                        else:
                            tmp.append((p, i2))
                    for x in tmp:
                        heapq.heappush(ready, x)
                    if best is None:
                        continue
                    p, i2 = best
                    dep_fin = t_now
                    for d in self.ops[i2].deps:
                        if fin[d] > dep_fin:
                            dep_fin = fin[d]
                    st = max(eng_free[e], dep_fin)
                    if self.ops[i2].is_dma:
                        eng_free[e] = st + 0.08
                        fin[i2] = st + dur[i2]
                    else:
                        eng_free[e] = st + dur[i2]
                        fin[i2] = eng_free[e]
                    done[i2] = True
                    order.append(i2)
                    remaining.discard(i2)
                    progressed = True
                    for s2 in succ[i2]:
                        if s2 in indeg:
                            indeg[s2] -= 1
                            if indeg[s2] == 0:
                                heapq.heappush(ready, (-prio[s2], s2))
                if not progressed:
                    i2 = min(remaining)
                    if indeg.get(i2, 0) != 0:
                        raise RuntimeError("scheduler stuck")
                    order.append(i2)
                    done[i2] = True
                    remaining.discard(i2)
            t_now = max(fin[i2] for i2 in idxs) if idxs else t_now
        assert len(order) == n
        self.order = order
        self.est_span = t_now

    def resolve(self):
        self.schedule()
        seq = [self.ops[i] for i in self.order]
        lasts = {}
        last_dma_all = []
        for o in seq:
            if o.tag.startswith("bar"):
                o.deps = set(x.idx for x in lasts.values()) | set(x.idx for x in last_dma_all[-80:])
            else:
                lasts[o.engine] = o
                if o.is_dma:
                    last_dma_all.append(o)
        for o in self.ops:
            for d in o.deps:
                do = self.ops[d]
                if not do.is_dma:
                    do.flagged = True
        comp_count = {e: 0 for e in ENGINES}
        slot_count = {}
        events = [None] * len(self.ops)
        pre_waits = [None] * len(self.ops)
        dma_seq = {}
        for o in seq:
            if o.is_dma:
                o.dma_idx = dma_seq.get(o.engine, 0)
                dma_seq[o.engine] = o.dma_idx + 1
        for o in seq:
            if o.is_dma:
                slot = o.dma_idx % self.NSLOT
                sk = f"dmaq_{o.engine}{slot}"
                n = slot_count.get(sk, 0)
                pre_waits[o.idx] = (sk, n * 16)
                slot_count[sk] = n + 1
                o.incs.append((sk, 16))
                events[o.idx] = (sk, (n + 1) * 16)
            elif o.flagged:
                comp_count[o.engine] += 1
                o.incs.append(("prog_" + o.engine, 1))
                events[o.idx] = ("prog_" + o.engine, comp_count[o.engine])
            else:
                events[o.idx] = ("prog_" + o.engine, 0)
        water = {e: {} for e in ENGINES}
        self.dma_slot_sems = sorted(slot_count)
        self.dma_slot_finals = dict(slot_count)
        for o in seq:
            need = {}
            if pre_waits[o.idx] is not None:
                sk, v = pre_waits[o.idx]
                if v > 0:
                    need[sk] = v
            for d in sorted(o.deps):
                do = self.ops[d]
                sk, v = events[d]
                if v <= 0:
                    continue
                need[sk] = max(need.get(sk, 0), v)
            for sk, v in sorted(need.items()):
                if water[o.engine].get(sk, 0) >= v:
                    continue
                o.waits.append((sk, v))
                water[o.engine][sk] = v


def _mknop(ename):
    def f(eng):
        if hasattr(eng, "engine_nop"):
            return eng.engine_nop()
        return eng.nop()
    return f


def rep3(ap_small, reps):
    p, n = ap_small.shape
    return ap_small.unsqueeze(2).broadcast_to([p, n, reps])


def build_program(debug=()):
    nc = bass.Bass("TRN2", target_bir_lowering=False, debug=False)
    dbg = set(debug)

    def okind(name):
        return "ExternalOutput" if name in dbg else "Internal"

    T = {}
    for name, shape, dt_, kind in [
        ("x", [XROWS, DM], F32, "ExternalInput"),
        ("wipT", [DM, 4384], F32, "ExternalInput"),
        ("wipA3", [18 * 128, 1024], F32, "ExternalInput"),
        ("wopT", [DI, DM], F32, "ExternalInput"),
        ("conv_w", [CDIM, 4], F32, "ExternalInput"),
        ("conv_wT", [4, CDIM], F32, "ExternalInput"),
        ("conv_b", [CDIM], F32, "ExternalInput"),
        ("A_log", [NH], F32, "ExternalInput"),
        ("Dv", [NH], F32, "ExternalInput"),
        ("dt_bias", [NH], F32, "ExternalInput"),
        ("gamma", [DI], F32, "ExternalInput"),
        ("ln_w", [DM], F32, "ExternalInput"),
        ("ln_b", [DM], F32, "ExternalInput"),
        ("Win", [DM, 64], F32, "ExternalInput"),
        ("bin", [64], F32, "ExternalInput"),
        ("W1", [64, 16], F32, "ExternalInput"),
        ("b1", [64, 16], F32, "ExternalInput"),
        ("W2", [64, 16], F32, "ExternalInput"),
        ("b2", [64], F32, "ExternalInput"),
        ("Wout2", [64, DM], F32, "ExternalInput"),
        ("bout", [DM], F32, "ExternalInput"),
        ("ident_bf", [128, 128], BF16, "ExternalInput"),
        ("ident_f32", [128, 128], F32, "ExternalInput"),
        ("caus", [128, 128], F32, "ExternalInput"),
        ("sel", [8, 1], F32, "ExternalInput"),
        ("m2t", [8, 8], F32, "ExternalInput"),
        ("hmask", [1, 1], F32, "ExternalInput"),
        ("out", [SEG, DM], F32, "ExternalOutput"),
    ]:
        T[name] = nc.dram_tensor(name, shape, dt_, kind=kind).ap()
    T["cc_out"] = nc.dram_tensor("cc_out", [8 * 129, DI], F32, kind=okind("cc_out"), addr_space="Shared").ap()
    for name, shape, dt_ in [
        ("u_dram", [XROWS, DM], BF16),
        ("xs_dram", [DI, SEG], BF16),
        ("yl_dram", [SEG, DI], BF16),
        ("sz_dram", [SEG, DI], BF16),
        ("g_dram", [SEG, DI], BF16),
        ("lc_dram", [NCH * NH, 128], F32),
        ("pf_dram", [NCH, NH], F32),
        ("al_dram", [8, NH], F32),
        ("cc_in", [129, DI], F32),

    ]:
        T[name] = nc.dram_tensor(name, shape, dt_, kind=okind(name)).ap()

    if "dbg_a3" in dbg:
        T["dbg_a3"] = nc.dram_tensor("dbg_a3", [1024, 516], F32, kind="ExternalOutput").ap()

    SB = {}

    def sball(name, shape, dtype):
        SB[name] = nc.alloc_sbuf_tensor(name, shape, dtype).ap()
        return SB[name]

    sball("cpool", [128, 6], F32)
    sball("ones_c", [NH, 1], F32)
    sball("zcol", [NH, 1], F32)
    sball("A_sb", [NH, 1], F32)
    sball("dtb_sb", [NH, 1], F32)
    sball("negoff", [NH, NCH], F32)
    sball("PF", [NH, NCH], F32)
    sball("dt_tok", [128, NT * NH], F32)
    sball("dtf_tok", [128, NT * NH], F32)
    sball("e_tok", [128, NT * NH], F32)
    sball("eseg_tok", [128, NT * NH], F32)
    sball("lc_tok", [128, NT * NH], F32)
    sball("Bf", [128, SEG], BF16)
    sball("Cf", [128, SEG], BF16)
    sball("caus_sb", [128, 128], F32)
    sball("identb", [128, 128], BF16)
    sball("identf", [128, 128], F32)
    sball("Drep", [128, NH], F32)
    sball("Pb", [128, NH], F32)
    sball("rstd0", [128, NT + 1], F32)
    sball("nm0r0", [128, NT + 1], F32)
    sball("convw", [128, 18 * 4], F32)
    sball("cwr", [72, 128], F32)
    sball("cbr", [18, 128], F32)
    sball("gmr", [16, 128], F32)
    sball("pf_t", [16, NH], F32)
    sball("convb", [128, 18], F32)
    sball("hm", [128, 1], F32)
    sball("gamma_sb", [128, 16], F32)
    sball("b2row", [128, 64], F32)
    sball("binrow", [1, 64], BF16)
    sball("boutrow", [1, DM], BF16)
    sball("onesK", [1, 128], BF16)
    sball("Win_bf", [128, 8 * 64], BF16)
    sball("Wout2_bf", [64, DM], BF16)
    sball("lnsc", [128, 16], F32)
    sball("m2t_sb", [8, 8], F32)
    sball("sel_sb", [8, 1], F32)
    sball("lp_sb", [8, NH], F32)
    sball("alpha", [8, NH], F32)
    sball("stat1", [128, 1], F32)
    sball("stat2", [128, 1], F32)
    sball("stat3", [128, 1], F32)
    sball("stat4", [128, 1], F32)
    sball("bs0", [128, 12], F32)
    sball("bs1", [128, 12], F32)
    sball("ag0", [128, 2], F32)
    sball("ag1", [128, 2], F32)
    sball("scr", [128, DM], F32)
    sball("wstage", [128, 8 * 128], F32)
    sball("wstage2", [128, 8 * 128], F32)
    sball("wbufA", [128, 8 * 128], BF16)
    sball("wbufB", [128, 8 * 128], BF16)
    sball("halo3", [128, 18 * 4], BF16)
    sball("S_run", [128, DI], F32)
    sball("S_j", [128, DI], F32)
    sball("S_tmp", [128, DI], F32)
    sball("Btok0", [128, 128], BF16)
    sball("Btok1", [128, 128], BF16)
    sball("zrow", [1, DI], F32)
    sball("Gm0", [128, 128], BF16)
    sball("Gm1", [128, 128], BF16)

    ARN = 133120
    AR = nc.alloc_sbuf_tensor("arena", [128, ARN], U8).ap()

    def ar(off, p, f, dtype):
        nbytes = f * (2 if dtype == BF16 else 4)
        assert off % 4 == 0 and off + nbytes <= ARN, (off, nbytes)
        return AR[0:p, off : off + nbytes].bitcast(dtype)

    v_uf = ar(0, 128, 8 * XROWS, BF16)
    v_xsb = [ar(34816, 128, 1024, F32), ar(38912, 128, 1024, F32)]
    v_usr = [ar(43008, 128, 1024, F32), ar(47104, 128, 1024, F32)]
    v_utok = [ar(51200, 128, 1024, BF16), ar(53248, 128, 1024, BF16)]
    v_dtF = ar(34816, NH, SEG, F32)
    v_lcF = ar(43008, NH, SEG, F32)
    v_wF = ar(51200, NH, SEG, F32)
    v_work = [ar(59392, 128, 516, BF16), ar(60448, 128, 516, BF16)]
    v_cvo = [ar(61504, 128, 512, BF16), ar(62528, 128, 512, BF16)]
    v_silu = [ar(63616, 128, 512, BF16), ar(64640, 128, 512, BF16)]
    v_wrow = ar(69760, 128, DM, F32)
    v_brow = ar(73856, 128, DM, F32)
    v_xst = [ar(34816, 128, 2048, BF16), ar(38912, 128, 2048, BF16)]
    v_xdt = [ar(43008, 128, 2048, BF16), ar(47104, 128, 2048, BF16)]
    v_xf = [ar(51200, 128, 2048, BF16), ar(55296, 128, 2048, BF16)]
    v_ysb = [ar(59392, 128, 2048, F32), ar(67584, 128, 2048, F32)]
    v_tmpb = ar(75776, 128, 2048, BF16)
    v_ylb = ar(79872, 128, 2048, BF16)
    v_L = [ar(83968, 128, 2048, BF16), ar(88064, 128, 2048, BF16)]
    v_Mp = [ar(92160, 128, 2048, BF16), ar(96256, 128, 2048, BF16),
            ar(100352, 128, 2048, BF16), ar(104448, 128, 2048, BF16)]
    v_lcb = [ar(108544, 128, 2048, F32), ar(116736, 128, 2048, F32)]
    v_diag = ar(108544, 128, 72 * 128, BF16)
    v_srunb = ar(124928, 128, 2048, BF16)
    v_wz = ar(34816, 128, 8 * DI, BF16)
    v_yl2 = [ar(67584, 128, 2048, BF16), ar(71680, 128, 2048, BF16)]
    v_y1 = ar(75776, 128, 2048, F32)
    v_g0 = ar(83968, 128, 2048, BF16)
    v_saccb = ar(88064, 128, 2048, F32) if False else ar(88064, 128, 2048, BF16)
    v_sz = [ar(92160, 128, 2048, BF16), ar(96256, 128, 2048, BF16)]
    v_gfh = [ar(j * 2048, 128, 1024, BF16) for j in range(16)]
    v_wo = ar(32768, 128, 16 * 1024, BF16)
    v_W1r = ar(65536, 128, 1024, F32)
    v_b1r = ar(69632, 128, 1024, F32)
    v_W2r = ar(73728, 128, 1024, F32)
    v_xfin = [ar(77824, 128, 1024, F32), ar(81920, 128, 1024, F32)]
    v_x2 = [ar(86016, 128, 1024, F32), ar(108928, 128, 1024, F32)]
    v_moh = [ar(90112, 128, 1024, F32), ar(113024, 128, 1024, F32)]
    v_u3 = [ar(94208, 128, 1024, BF16), ar(117120, 128, 1024, BF16)]
    v_u3f = [ar(96256, 128, 1024, BF16), ar(119168, 128, 1024, BF16)]
    v_hb = [ar(98304, 128, 1024, BF16), ar(121216, 128, 1024, BF16)]
    v_g2 = [ar(100352, 128, 64, F32), ar(123264, 128, 64, F32)]
    v_g2b = [ar(100608, 128, 64, BF16), ar(123520, 128, 64, BF16)]
    v_o = [ar(100736, 128, 1024, F32), ar(104832, 128, 1024, F32)]

    psA = nc.alloc_psum_tensor("psA", [128, 2048], F32).ap()
    psB = nc.alloc_psum_tensor("psB", [128, 1024], F32).ap()
    psT = nc.alloc_psum_tensor("psT", [128, 512], F32).ap()
    psTb = nc.alloc_psum_tensor("psTb", [128, 1024], BF16).ap()
    PSA = ["psA0", "psA1", "psA2", "psA3"]

    P = Plan()

    def dma(eng, out, in_, reads, writes, tag="", transpose=False):
        return P.op(
            eng,
            (lambda o_, i_, tr: (lambda e: e.dma_start(out=o_, in_=i_, transpose=tr)))(out, in_, transpose),
            reads=reads, writes=writes, dma=True, tag=tag,
        )

    def vop(fn, reads, writes, tag="", late=False):
        return P.op("vector", fn, reads=reads, writes=writes, tag=tag, late=late)

    def sop(fn, reads, writes, tag="", late=False):
        return P.op("scalar", fn, reads=reads, writes=writes, tag=tag, late=late)

    def top(fn, reads, writes, tag=""):
        return P.op("tensor", fn, reads=reads, writes=writes, tag=tag)

    cp = SB["cpool"]
    _cvals = [(EPS, 0), (-0.5, 1), (-1.0, 2), (1.0 / DI, 3), (1.0 + EPS, 4), (EPS * EPS, 5)]
    for val, col in _cvals:
        nc.const_aps.register(F32, val, cp[:, col : col + 1])
        vop((lambda v_, c_: lambda e: e.memset(cp[:, c_ : c_ + 1], v_))(val, col), [], ["constap"], f"cst{col}", late=True)

    dma("sync", SB["caus_sb"][:], T["caus"][:], [], ["caus"], "ld_caus")
    dma("sync", SB["identb"][:], T["ident_bf"][:], [], ["identb"], "ld_idb")
    dma("sync", SB["identf"][:], T["ident_f32"][:], [], ["identf"], "ld_idf")
    dma("sync", SB["Drep"][:], T["Dv"].unsqueeze(0).broadcast_to([128, NH]), [], ["Drep"], "ld_D")
    dma("sync", v_wrow[:], T["ln_w"].unsqueeze(0).broadcast_to([128, DM]), [], ["wrow"], "ld_w")
    dma("sync", v_brow[:], T["ln_b"].unsqueeze(0).broadcast_to([128, DM]), [], ["brow"], "ld_b")
    dma("sync", SB["cwr"][:], T["conv_wT"].rearrange("k (ct p) -> (k ct) p", p=128), [], ["cwr"], "ld_cw")
    dma("sync", SB["cbr"][:], T["conv_b"].rearrange("(ct p) -> ct p", p=128), [], ["cbr"], "ld_cb")
    dma("sync", SB["hm"][:], T["hmask"].broadcast_to([128, 1]), [], ["hm"], "ld_hm")
    dma("sync", SB["A_sb"][:], T["A_log"].unsqueeze(1), [], ["A_sb"], "ld_A")
    dma("sync", SB["dtb_sb"][:], T["dt_bias"].unsqueeze(1), [], ["dtb"], "ld_dtb")
    dma("sync", SB["gmr"][:], T["gamma"].rearrange("(j p) -> j p", p=128), [], ["gmr"], "ld_gm")
    dma("sync", SB["b2row"][:], T["b2"].unsqueeze(0).broadcast_to([128, 64]), [], ["b2r"], "ld_b2")
    dma("sync", SB["m2t_sb"][:], T["m2t"][:], [], ["m2t"], "ld_m2t")
    dma("sync", SB["sel_sb"][:], T["sel"][:], [], ["sel"], "ld_sel")

    vop(lambda e: e.memset(SB["ones_c"][:], 1.0), [], ["ones_c"], "ones_c", late=True)
    vop(lambda e: e.memset(SB["zcol"][:], 0.0), [], ["zcol"], "zcol", late=True)
    vop(lambda e: e.memset(SB["onesK"][:], 1.0), [], ["onesK"], "onesK", late=True)
    sop(lambda e: e.activation(SB["A_sb"][:], SB["A_sb"][:], AF.Exp), ["A_sb", "constap"], ["A_sb"], "A_exp")
    vop(lambda e: e.tensor_single_scalar(SB["A_sb"][:], SB["A_sb"][:], -1.0, ALU.mult), ["A_sb"], ["A_sb"], "A_neg")
    dma("sync", SB["wstage"][:, 0:512].rearrange("p (j m) -> p j m", m=64), T["Win"].rearrange("(j p) m -> p j m", p=128), [], ["wstage"], "ld_win")
    sop(lambda e: e.activation(SB["Win_bf"][:], SB["wstage"][:, 0:512], AF.Copy), ["wstage"], ["win_bf"], "cast_win")
    dma("sync", SB["wstage2"][0:64, 0:1024], T["Wout2"][:], [], ["wstage2"], "ld_wo2")
    sop(lambda e: e.activation(SB["Wout2_bf"][:], SB["wstage2"][0:64, 0:1024], AF.Copy), ["wstage2"], ["wo2_bf"], "cast_wo2")
    dma("sync", SB["wstage"][0:1, 512:576], T["bin"].unsqueeze(0), ["win_bf"], ["wstage"], "ld_bin")
    sop(lambda e: e.activation(SB["binrow"][:], SB["wstage"][0:1, 512:576], AF.Copy), ["wstage"], ["binrow"], "cast_bin")
    dma("sync", SB["wstage2"][0:1, 0:1024], T["bout"].unsqueeze(0), ["wo2_bf"], ["wstage2"], "ld_bout")
    sop(lambda e: e.activation(SB["boutrow"][:], SB["wstage2"][0:1, 0:1024], AF.Copy), ["wstage2"], ["boutrow"], "cast_bout")

    top(lambda e: e.transpose(psT[:, 0:72], SB["cwr"][:], SB["identf"][0:72, 0:72]),
        ["cwr", "identf"], ["psT"], "Tcw")
    vop(lambda e: e.tensor_copy(SB["convw"][:].rearrange("p (k ct) -> p k ct", ct=18), psT[:, 0:72].rearrange("p (k ct) -> p k ct", ct=18)),
        ["psT"], ["convw"], "Tcwe")
    top(lambda e: e.transpose(psT[:, 0:18], SB["cbr"][:], SB["identf"][0:18, 0:18]),
        ["cbr", "identf"], ["psT"], "Tcb")
    vop(lambda e: e.tensor_copy(SB["convb"][:], psT[:, 0:18]), ["psT"], ["convb"], "Tcbe")
    for ct_ in range(18):
        for k_ in range(4):
            vop((lambda c2, k2: lambda e: e.tensor_scalar_mul(
                    v_diag[:, (c2 * 4 + k2) * 128 : (c2 * 4 + k2 + 1) * 128],
                    SB["identb"][:], SB["convw"][:, k2 * 18 + c2 : k2 * 18 + c2 + 1]))(ct_, k_),
                ["identb", "convw"], ["cdiag"], f"dg{ct_}_{k_}")
    top(lambda e: e.transpose(psT[:, 0:16], SB["gmr"][:], SB["identf"][0:16, 0:16]),
        ["gmr", "identf"], ["psT"], "Tgm")
    vop(lambda e: e.tensor_copy(SB["gamma_sb"][:], psT[:, 0:16]), ["psT"], ["gamma"], "Tgme")
    P.barrier("barC0")

    for r in range(17):
        bi = r % 2
        L = SB["lnsc"][:, bi * 8 : bi * 8 + 8]
        bsk, agk = f"bs{bi}", f"ag{bi}"
        bs, ag = SB[bsk], SB[agk]
        xk, uk, utk = f"xsb{bi}", f"usr{bi}", f"utok{bi}"
        dma("sync", v_xsb[bi][:], T["x"][r * 128 : (r + 1) * 128, :], [], [xk], f"ldx{r}")
        vop((lambda b_, bs_: lambda e: e.bn_stats(bs_[:, 0:6], v_xsb[b_][:, 0:512]))(bi, bs),
            [xk], [bsk], f"bna{r}", late=True)
        vop((lambda b_, bs_: lambda e: e.bn_stats(bs_[:, 6:12], v_xsb[b_][:, 512:1024]))(bi, bs),
            [xk], [bsk], f"bnb{r}", late=True)
        vop((lambda bs_, ag_: lambda e: e.bn_aggr(ag_[:], bs_[:]))(bs, ag),
            [bsk], [agk], f"bng{r}", late=True)
        sop((lambda L_, ag_: lambda e: e.activation(L_[:, 3:4], ag_[:, 1:2], AF.Ln, bias=EPS))(L, ag),
            [agk], ["lnt1" + str(bi)], f"t1{r}")
        sop((lambda r_, L_: lambda e: e.activation(SB["rstd0"][:, r_ : r_ + 1], L_[:, 3:4], AF.Exp, scale=-0.5))(r, L),
            ["lnt1" + str(bi)], ["rstd0"], f"r0{r}")
        sop((lambda L_, ag_: lambda e: e.activation(L_[:, 4:5], ag_[:, 1:2], AF.Ln, scale=1.0 + EPS, bias=EPS * EPS))(L, ag),
            [agk], ["lnt12" + str(bi)], f"t12{r}")
        sop((lambda L_: lambda e: e.activation(L_[:, 5:6], L_[:, 4:5], AF.Exp, scale=-0.5))(L),
            ["lnt12" + str(bi)], ["lnq" + str(bi)], f"q{r}")
        vop((lambda r_, L_, ag_: lambda e: e.scalar_tensor_tensor(SB["nm0r0"][:, r_ : r_ + 1], SB["rstd0"][:, r_ : r_ + 1], -1.0, ag_[:, 0:1], ALU.mult, ALU.mult))(r, L, ag),
            ["rstd0", agk], ["nm0r0"], f"nm0r{r}", late=True)
        vop((lambda b_, L_, ag_: lambda e: e.tensor_scalar(v_usr[b_][:], v_xsb[b_][:], ag_[:, 0:1], L_[:, 5:6], ALU.subtract, ALU.mult))(bi, L, ag),
            [xk, agk, "lnq" + str(bi)], [uk], f"uhat{r}")
        vop((lambda b_: lambda e: e.tensor_mul(v_usr[b_][:], v_usr[b_][:], v_wrow[:]))(bi),
            [uk, "wrow"], [uk], f"uw{r}")
        vop((lambda b_: lambda e: e.tensor_add(v_utok[b_][:], v_usr[b_][:], v_brow[:]))(bi),
            [uk, "brow"], [utk], f"ub{r}")
        dma("sync", T["u_dram"][r * 128 : (r + 1) * 128, :], v_utok[bi][:], [utk], [f"u_dram{r}"], f"stu{r}")

    u_deps = [f"u_dram{r}" for r in range(17)]
    for j in range(8):
        dma("sync", v_uf[:, j * XROWS : (j + 1) * XROWS],
            T["u_dram"][:, j * 128 : (j + 1) * 128], u_deps, [f"uf{j}"], f"uft{j}", transpose=True)
    UFK = [f"uf{j}" for j in range(8)]

    def uf(j, t0, n):
        return v_uf[:, j * XROWS + t0 : j * XROWS + t0 + n]

    dma("sync", SB["wstage"][:, 0:256].rearrange("p (j c) -> p j c", c=32),
        T["wipT"][:, 4352:4384].rearrange("(j p) c -> p j c", p=128), [], ["wstage"], "ld_wdt")
    sop(lambda e: e.activation(SB["wbufA"][:, 0:256], SB["wstage"][:, 0:256], AF.Copy), ["wstage"], ["wbufA"], "cast_wdt")
    for d in range(8):
        for t4 in range(4):
            top((lambda d_, t_: lambda e: e.matmul(
                    psA[0:NH, t_ * 512 : (t_ + 1) * 512],
                    SB["wbufA"][:, d_ * 32 : (d_ + 1) * 32],
                    uf(d_, 128 + t_ * 512, 512),
                    start=(d_ == 0), stop=(d_ == 7), skip_group_check=True))(d, t4),
                ["wbufA"] + UFK, [f"psA{t4}"], f"dtmm{d}_{t4}")
    if "dbg_dt" in dbg:
        T["dbg_dt"] = nc.dram_tensor("dbg_dt", [4 * NH, SEG], F32, kind="ExternalOutput").ap()
    sop(lambda e: e.activation(v_wF[:], psA[0:NH, 0:SEG], AF.Abs, bias=SB["dtb_sb"][:]),
        PSA + ["dtb"], ["wF"], "sp_ab")
    sop(lambda e: e.activation(v_wF[:], v_wF[:], AF.Exp, scale=-1.0), ["wF"], ["wF"], "sp_en")
    sop(lambda e: e.activation(v_wF[:], v_wF[:], AF.Ln, bias=1.0), ["wF"], ["wF"], "sp_l1")
    sop(lambda e: e.activation(v_dtF[:], psA[0:NH, 0:SEG], AF.Relu, bias=SB["dtb_sb"][:]),
        PSA + ["dtb"], ["dtF"], "sp_rl")
    vop(lambda e: e.tensor_add(v_dtF[:], v_dtF[:], v_wF[:]), ["dtF", "wF"], ["dtF"], "dtFadd")
    if "dbg_dt" in dbg:
        dma("sync", T["dbg_dt"][NH : 2 * NH], v_dtF[:], ["dtF"], ["dbgdt1"], "dbg_st1")
        dma("sync", T["dbg_dt"][2 * NH : 3 * NH], v_wF[:], ["wF"], ["dbgdt2"], "dbg_st2")
    for c in range(NCH):
        top((lambda c_: lambda e: e.transpose(psT[:, c_ * NH : (c_ + 1) * NH], v_dtF[:, c_ * Q : (c_ + 1) * Q], SB["identf"][0:NH, 0:NH]))(c),
            ["dtF", "identf"], ["psT"], f"Tdt{c}")
    vop(lambda e: e.tensor_copy(SB["dt_tok"][:], psT[:, 0:512]), ["psT"], ["dt_tok"], "Tdte")
    vop(lambda e: e.tensor_scalar_mul(v_dtF[:], v_dtF[:], SB["A_sb"][:]), ["dtF", "A_sb"], ["dtF"], "dtA")
    vop(lambda e: e.memset(v_wF[:], 1.0), ["wF"], ["wF"], "wFones", late=True)
    vop(lambda e: e.tensor_tensor_scan(v_lcF[:], v_wF[:], v_dtF[:], 0.0, ALU.mult, ALU.add),
        ["wF", "dtF"], ["lcF"], "lcscan", late=True)
    vop(lambda e: e.memset(SB["negoff"][:, 0:1], 0.0), [], ["negoff"], "no0", late=True)
    vop(lambda e: e.tensor_single_scalar(
            SB["negoff"][:, 1:NCH],
            v_lcF.rearrange("h (c q) -> h c q", q=Q)[:, 0 : NCH - 1, Q - 1 : Q].squeeze(2), -1.0, ALU.mult),
        ["lcF"], ["negoff"], "nooff")
    for c in range(NCH):
        top((lambda c_: lambda e: e.transpose(psT[:, c_ * NH : (c_ + 1) * NH], v_lcF[:, c_ * Q : (c_ + 1) * Q], SB["identf"][0:NH, 0:NH]))(c),
            ["lcF", "identf"], ["psT"], f"Tlc{c}")
    vop(lambda e: e.tensor_copy(SB["lc_tok"][:], psT[:, 0:512]), ["psT"], ["lc_tok"], "Tlce")
    if "dbg_dt" in dbg:
        dma("sync", T["dbg_dt"][3 * NH : 4 * NH], v_dtF[:], ["dtF"], ["dbgdt3"], "dbg_st3")
    dma("sync", T["lc_dram"].rearrange("(c h) q -> h c q", h=NH), v_lcF.rearrange("h (c q) -> h c q", q=Q),
        ["lcF"], ["lc_dram"], "st_lc")
    for c in range(NCH):
        sop((lambda c_: lambda e: e.activation(v_wF[:, c_ * Q : (c_ + 1) * Q], v_lcF[:, c_ * Q : (c_ + 1) * Q], AF.Exp, bias=SB["negoff"][:, c_ : c_ + 1]))(c),
            ["lcF", "negoff"], ["wF"], f"eF{c}")
        sop((lambda c_: lambda e: e.activation(SB["PF"][:, c_ : c_ + 1], v_lcF[:, (c_ + 1) * Q - 1 : (c_ + 1) * Q], AF.Exp, bias=SB["negoff"][:, c_ : c_ + 1]))(c),
            ["lcF", "negoff"], ["PF"], f"PF{c}")
    for c in range(NCH):
        top((lambda c_: lambda e: e.transpose(psT[:, c_ * NH : (c_ + 1) * NH], v_wF[:, c_ * Q : (c_ + 1) * Q], SB["identf"][0:NH, 0:NH]))(c),
            ["wF", "identf"], ["psT"], f"Te{c}")
    vop(lambda e: e.tensor_copy(SB["e_tok"][:], psT[:, 0:512]), ["psT"], ["e_tok"], "Tee")
    for c in range(NCH):
        sop((lambda c_: lambda e: e.activation(v_wF[:, c_ * Q : (c_ + 1) * Q], v_lcF[:, c_ * Q : (c_ + 1) * Q], AF.Exp, scale=-1.0, bias=v_lcF[:, (c_ + 1) * Q - 1 : (c_ + 1) * Q]))(c),
            ["lcF"], ["wF"], f"fF{c}")
    for c in range(NCH):
        top((lambda c_: lambda e: e.transpose(psT[:, c_ * NH : (c_ + 1) * NH], v_wF[:, c_ * Q : (c_ + 1) * Q], SB["identf"][0:NH, 0:NH]))(c),
            ["wF", "identf"], ["psT"], f"Tf{c}")
    vop(lambda e: e.tensor_copy(SB["dtf_tok"][:], psT[:, 0:512]), ["psT"], ["dtf_tok"], "Tfe")
    vop(lambda e: e.tensor_mul(SB["dtf_tok"][:], SB["dt_tok"][:], SB["dtf_tok"][:]),
        ["dt_tok", "dtf_tok"], ["dtf_tok"], "dtf")
    sop(lambda e: e.activation(v_wF[:], v_lcF[:], AF.Exp), ["lcF"], ["wF"], "esegF")
    for c in range(NCH):
        top((lambda c_: lambda e: e.transpose(psT[:, c_ * NH : (c_ + 1) * NH], v_wF[:, c_ * Q : (c_ + 1) * Q], SB["identf"][0:NH, 0:NH]))(c),
            ["wF", "identf"], ["psT"], f"Tes{c}")
    vop(lambda e: e.tensor_copy(SB["eseg_tok"][:], psT[:, 0:512]), ["psT"], ["eseg_tok"], "Tese")
    top(lambda e: e.transpose(psT[0:NCH, 0:NH], SB["PF"][:], SB["identf"][0:NH, 0:NH]),
        ["PF", "identf"], ["psT"], "Tpf")
    vop(lambda e: e.tensor_copy(SB["pf_t"][:], psT[0:NCH, 0:NH]), ["psT"], ["pf_t"], "Tpfe")
    dma("sync", T["pf_dram"][:], SB["pf_t"][:], ["pf_t"], ["pf_dram"], "st_pf")

    wk = [("wbufA", SB["wbufA"]), ("wbufB", SB["wbufB"])]
    stg = [("wstage", SB["wstage"]), ("wstage2", SB["wstage2"])]
    for ct in range(18):
        wkey, wbuf = wk[ct % 2]
        skey, sbuf = stg[ct % 2]
        col0 = 2048 + ct * 128
        dma("sync", sbuf[:, 0:1024],
            T["wipA3"][ct * 128 : (ct + 1) * 128, :],
            [], [skey], f"ldw{ct}")
        sop((lambda s_, w_: lambda e: e.activation(w_[:], s_[:, 0:1024], AF.Copy))(sbuf, wbuf),
            [skey], [wkey], f"castw{ct}")
        hbk = f"psB{ct % 2}"
        hb0 = (ct % 2) * 512
        for ts in range(4):
            for d in range(8):
                top((lambda d_, t_, w_, o_: lambda e: e.matmul(
                        psB[:, o_ + t_ * 4 : o_ + t_ * 4 + 3], w_[:, d_ * 128 : (d_ + 1) * 128],
                        uf(d_, 125 + t_ * 512, 3), start=(d_ == 0), stop=(d_ == 7), skip_group_check=True))(d, ts, wbuf, hb0),
                    [wkey] + UFK, [hbk], f"hmm{ct}_{ts}_{d}")
        for ts in range(4):
            pk = f"psA{ts % 2}"
            psl = psA[:, (ts % 2) * 512 : (ts % 2) * 512 + 512]
            for d in range(8):
                top((lambda d_, t_, w_, p_: lambda e: e.matmul(
                        p_, w_[:, d_ * 128 : (d_ + 1) * 128],
                        uf(d_, 128 + t_ * 512, 512),
                        start=(d_ == 0), stop=(d_ == 7)))(d, ts, wbuf, psl),
                    [wkey] + UFK, [pk], f"xmm{ct}_{ts}_{d}")
            wbk = f"work{ts % 2}"
            wbuf2 = v_work[ts % 2]
            if ts == 0:
                sop((lambda w_, o_: lambda e: e.activation(w_[:, 0:3], psB[:, o_ : o_ + 3], AF.Identity, scale=SB["hm"][:]))(wbuf2, hb0),
                    [hbk, "hm"], [wbk], f"evh{ct}_{ts}")
            else:
                sop((lambda w_, t_, o_: lambda e: e.activation(w_[:, 0:3], psB[:, o_ + t_ * 4 : o_ + t_ * 4 + 3], AF.Copy))(wbuf2, ts, hb0),
                    [hbk], [wbk], f"evh{ct}_{ts}")
            sop((lambda w_, p_: lambda e: e.activation(w_[:, 3:515], p_, AF.Copy))(wbuf2, psl),
                [pk], [wbk], f"evac{ct}_{ts}")
            cpk = f"psA{2 + ts % 2}"
            psc = psA[:, (2 + ts % 2) * 512 : (2 + ts % 2) * 512 + 512]
            for k in range(4):
                top((lambda w_, p_, ct_, k_: lambda e: e.matmul(
                        p_, v_diag[:, (ct_ * 4 + k_) * 128 : (ct_ * 4 + k_ + 1) * 128],
                        w_[:, k_ : 512 + k_], start=(k_ == 0), stop=(k_ == 3)))(wbuf2, psc, ct, k),
                    [wbk, "cdiag"], [cpk], f"cnv{ct}_{ts}_{k}")
            if ct < 16:
                so = v_silu[ts % 2]
                sok = f"silu{ts % 2}"
                sop((lambda p_, s_, ct_: lambda e: e.activation(s_[:], p_, AF.Silu, bias=SB["convb"][:, ct_ : ct_ + 1]))(psc, so, ct),
                    [cpk, "convb"], [sok], f"silu{ct}_{ts}")
                dma("scalar", T["xs_dram"][ct * 128 : (ct + 1) * 128, ts * 512 : (ts + 1) * 512],
                    so[:], [sok], [f"xs{ct}_{ts}"], f"stxs{ct}_{ts}")
            else:
                dstk = ("Bf" if ct == 16 else "Cf") + str(ts)
                dst = SB["Bf" if ct == 16 else "Cf"]
                sop((lambda p_, d_, ct_, t_: lambda e: e.activation(d_[:, t_ * 512 : (t_ + 1) * 512], p_, AF.Silu, bias=SB["convb"][:, ct_ : ct_ + 1]))(psc, dst, ct, ts),
                    [cpk, "convb"], [dstk], f"siluBC{ct}_{ts}")

    if "dbg_a3" in dbg:
        pass
    XSK = [f"xs{ct}_{ts}" for ct in range(16) for ts in range(4)]
    P.barrier("barA")

    vop(lambda e: e.memset(SB["S_run"][:], 0.0), [], ["S_run"], "S0", late=True)
    for c in range(NCH):
        bi = c % 2
        xk = f"xst{bi}"
        dma("sync", v_xst[bi][:], T["xs_dram"][:, c * 128 : (c + 1) * 128],
            [f"xs{ct_}_{c // 4}" for ct_ in range(16)], [xk], f"xsT{c}", transpose=True)
        top((lambda c_: lambda e: e.matmul(psT[:, 0:128], SB["Bf"][:, c_ * 128 : (c_ + 1) * 128],
                                           SB["Cf"][:, c_ * 128 : (c_ + 1) * 128], start=True, stop=True))(c),
            [f"Bf{c // 4}", f"Cf{c // 4}"], ["psT"], f"gmm{c}")
        vop((lambda b_: lambda e: e.tensor_tensor(SB["Gm" + str(b_)][:], psT[:, 0:128], SB["caus_sb"][:], ALU.mult))(bi),
            ["psT", "caus"], [f"Gm{bi}"], f"gm{c}")
        for hh in range(2):
            lk = f"lcb{hh}"
            h0 = hh * NH2
            mi = 2 * bi + hh
            dma("scalar", v_lcb[hh][:],
                T["lc_dram"].rearrange("(c h) q -> c (h q)", h=NH)[c : c + 1, h0 * 128 : (h0 + NH2) * 128].broadcast_to([128, NH2 * 128]),
                ["lc_dram"], [lk], f"lcbd{c}_{hh}")
            vop((lambda hh_, c_, h0_: lambda e: e.tensor_tensor(
                    v_lcb[hh_][:].rearrange("p (a b) -> p a b", b=128),
                    v_lcb[hh_][:].rearrange("p (a b) -> p a b", b=128),
                    rep3(SB["lc_tok"][:, c_ * NH + h0_ : c_ * NH + h0_ + NH2], 128),
                    ALU.subtract))(hh, c, h0),
                [lk, "lc_tok"], [lk], f"E{c}_{hh}")
            vop((lambda hh_: lambda e: e.tensor_scalar_min(v_lcb[hh_][:], v_lcb[hh_][:], 0.0))(hh),
                [lk], [lk], f"Emin{c}_{hh}")
            sop((lambda hh_, b_: lambda e: e.activation(v_L[b_][:], v_lcb[hh_][:], AF.Exp))(hh, bi),
                [lk], [f"L{bi}"], f"expL{c}_{hh}")
            vop((lambda mi_, b_: lambda e: e.tensor_tensor(
                    v_Mp[mi_][:].rearrange("p (a b) -> p a b", b=128),
                    v_L[b_][:].rearrange("p (a b) -> p a b", b=128),
                    SB["Gm" + str(b_)][:].unsqueeze(1).broadcast_to([128, NH2, 128]),
                    ALU.mult))(mi, bi),
                [f"L{bi}", f"Gm{bi}"], [f"Mp{mi}"], f"mp{c}_{hh}")
        vop((lambda b_, c_: lambda e: e.tensor_tensor(
                v_xdt[b_][:].rearrange("p (a b) -> p a b", b=HD),
                v_xst[b_][:].rearrange("p (a b) -> p a b", b=HD),
                rep3(SB["dt_tok"][:, c_ * NH : (c_ + 1) * NH], HD),
                ALU.mult))(bi, c),
            [xk, "dt_tok"], [f"xdt{bi}"], f"xdt{c}")
        vop((lambda b_, c_: lambda e: e.tensor_tensor(
                v_xf[b_][:].rearrange("p (a b) -> p a b", b=HD),
                v_xst[b_][:].rearrange("p (a b) -> p a b", b=HD),
                rep3(SB["dtf_tok"][:, c_ * NH : (c_ + 1) * NH], HD),
                ALU.mult))(bi, c),
            [xk, "dtf_tok"], [f"xf{bi}"], f"xf{c}")
        top((lambda c_: lambda e: e.transpose(psTb[:, 0:128], SB["Bf"][:, c_ * 128 : (c_ + 1) * 128], SB["identb"][:]))(c),
            [f"Bf{c // 4}", "identb"], ["psTb"], f"TB{c}")
        vop((lambda b_: lambda e: e.tensor_copy(SB["Btok" + str(b_)][:], psTb[:, 0:128]))(bi),
            ["psTb"], [f"Btok{bi}"], f"TBe{c}")
        sop(lambda e: e.activation(v_srunb[:], SB["S_run"][:], AF.Copy), ["S_run"], ["S_runb"], f"srb{c}")
        for yh in range(2):
            for j2 in range(2):
                top((lambda c_, y_, j_: lambda e: e.matmul(
                        psB[:, j_ * 512 : (j_ + 1) * 512], SB["Cf"][:, c_ * 128 : (c_ + 1) * 128],
                        v_srunb[:, y_ * 1024 + j_ * 512 : y_ * 1024 + (j_ + 1) * 512], start=True, stop=True))(c, yh, j2),
                    [f"Cf{c // 4}", "S_runb"], [f"psB{j2}"], f"yint{c}_{yh}_{j2}")
            vop((lambda b_, c_, y_: lambda e: e.tensor_tensor(
                    v_ysb[b_][:, y_ * 1024 : (y_ + 1) * 1024].rearrange("p (a b) -> p a b", b=HD),
                    psB[:, 0:1024].rearrange("p (a b) -> p a b", b=HD),
                    rep3(SB["e_tok"][:, c_ * NH + y_ * NH2 : c_ * NH + (y_ + 1) * NH2], HD),
                    ALU.mult))(bi, c, yh),
                ["psB0", "psB1", "e_tok"], [f"ysb{bi}_{yh}"], f"yie{c}_{yh}")
            for h in range(yh * NH2, (yh + 1) * NH2):
                top((lambda h_, mi_, b_, y_: lambda e: e.matmul(
                        psB[:, (h_ % NH2) * HD : (h_ % NH2 + 1) * HD], v_Mp[mi_][:, (h_ % NH2) * 128 : (h_ % NH2 + 1) * 128],
                        v_xdt[b_][:, h_ * HD : (h_ + 1) * HD], start=True, stop=True))(h, 2 * bi + yh, bi, yh),
                    [f"Mp{2 * bi + yh}", f"xdt{bi}"], [f"psB{(h % NH2) // 8}"], f"yin{c}_{h}")
            vop((lambda b_, y_: lambda e: e.tensor_add(
                    v_ysb[b_][:, y_ * 1024 : (y_ + 1) * 1024],
                    v_ysb[b_][:, y_ * 1024 : (y_ + 1) * 1024], psB[:, 0:1024]))(bi, yh),
                [f"ysb{bi}_{yh}", "psB0", "psB1"], [f"ysb{bi}_{yh}"], f"yadd{c}_{yh}")
        for j4 in range(4):
            top((lambda b_, j_: lambda e: e.matmul(
                    psA[:, j_ * 512 : (j_ + 1) * 512], SB["Btok" + str(b_)][:],
                    v_xf[b_][:, j_ * 512 : (j_ + 1) * 512], start=True, stop=True))(bi, j4),
                [f"Btok{bi}", f"xf{bi}"], [f"psA{j4}"], f"smm{c}_{j4}")
        dma("scalar", SB["Pb"][:],
            T["pf_dram"][c : c + 1, :].broadcast_to([128, NH]),
            ["pf_dram"], ["Pb"], f"pb{c}")
        vop(lambda e: e.tensor_tensor(
                SB["S_tmp"][:].rearrange("p (a b) -> p a b", b=HD),
                SB["S_run"][:].rearrange("p (a b) -> p a b", b=HD),
                rep3(SB["Pb"][:], HD), ALU.mult),
            ["S_run", "Pb"], ["S_tmp"], f"stmp{c}")
        vop(lambda e: e.tensor_add(SB["S_run"][:], psA[:, 0:2048], SB["S_tmp"][:]),
            PSA + ["S_tmp"], ["S_run"], f"supd{c}")
        vop((lambda b_: lambda e: e.tensor_tensor(
                v_tmpb[:].rearrange("p (a b) -> p a b", b=HD),
                v_xst[b_][:].rearrange("p (a b) -> p a b", b=HD),
                rep3(SB["Drep"][:], HD), ALU.mult))(bi),
            [xk, "Drep"], ["tmpb"], f"dxs{c}")
        vop((lambda b_: lambda e: e.tensor_add(v_ysb[b_][:], v_ysb[b_][:], v_tmpb[:]))(bi),
            [f"ysb{bi}_0", f"ysb{bi}_1", "tmpb"], [f"ysb{bi}_0", f"ysb{bi}_1"], f"yD{c}")
        sop((lambda b_: lambda e: e.activation(v_ylb[:], v_ysb[b_][:], AF.Copy))(bi),
            [f"ysb{bi}_0", f"ysb{bi}_1"], ["ylb"], f"ycast{c}")
        dma("gpsimd", T["yl_dram"][c * 128 : (c + 1) * 128, :], v_ylb[:], ["ylb"], [f"yl{c}"], f"styl{c}")

    vop(lambda e: e.memset(SB["zrow"][:], 0.0), [], ["zrow"], "zrow", late=True)
    dma("gpsimd", T["cc_in"][128:129, :], SB["zrow"][:], ["zrow"], ["cc_in"], "st_zrow")
    dma("gpsimd", T["cc_in"][0:128, :], SB["S_run"][:], ["S_run"], ["cc_in"], "st_sseg")
    dma("gpsimd", T["cc_in"][128:129, 0:NH], SB["lc_tok"][127:128, 15 * NH : 16 * NH], ["lc_tok"], ["cc_in"], "st_lp")
    P.op("gpsimd",
         lambda e: e.collective_compute(
             "AllGather", ALU.bypass, replica_groups=[list(range(CORES))],
             ins=[T["cc_in"].opt()], outs=[T["cc_out"].opt()]),
         reads=["cc_in"], writes=["cc_out"], tag="allgather")
    CDCLR = ["xst0", "xst1", "xdt0", "xdt1", "xf0", "xf1", "ysb0_0", "ysb0_1"]
    for d in range(8):
        for half in range(2):
            skey, sbuf = stg[half]
            dma("sync", sbuf[:, 0:1024],
                T["wipT"][:, 0:2048][d * 128 : (d + 1) * 128, half * 1024 : (half + 1) * 1024],
                [], [skey], f"ldwz{d}_{half}")
            sop((lambda d_, h_, s_: lambda e: e.activation(v_wz[:, d_ * 2048 + h_ * 1024 : d_ * 2048 + (h_ + 1) * 1024], s_[:, 0:1024], AF.Copy))(d, half, sbuf),
                [skey], ["wz", CDCLR[d]], f"castwz{d}_{half}")
    for c in range(NCH):
        bi = c % 2
        for d in range(8):
            for j4 in range(4):
                top((lambda c_, d_, j_: lambda e: e.matmul(
                        psA[:, j_ * 512 : (j_ + 1) * 512],
                        uf(d_, 128 + c_ * 128, 128),
                        v_wz[:, d_ * 2048 + j_ * 512 : d_ * 2048 + (j_ + 1) * 512],
                        start=(d_ == 0), stop=(d_ == 7), skip_group_check=True))(c, d, j4),
                    ["wz"] + UFK, [f"psA{j4}"], f"zmm{c}_{d}_{j4}")
        szk = f"sz{bi}"
        sop((lambda b_, c_: lambda e: e.activation(v_sz[b_][:], psA[:, 0:2048], AF.Silu))(bi, c),
            PSA, [szk, f"Mp{bi}"], f"zsilu{c}")
        dma("scalar", T["sz_dram"][c * 128 : (c + 1) * 128, :], v_sz[bi][:], [szk], [f"szd{c}"], f"stsz{c}")
    for j in range(8):
        dma("gpsimd", SB["lp_sb"][j : j + 1, :], T["cc_out"][j * 129 + 128 : j * 129 + 129, 0:NH],
            ["cc_out"], ["lp_sb"], f"ldlp{j}")
    top(lambda e: e.matmul(psB[0:8, 0:NH], SB["m2t_sb"][:], SB["lp_sb"][:], start=True, stop=True),
        ["m2t", "lp_sb"], ["psB0"], "alphamm")
    sop(lambda e: e.activation(SB["alpha"][:], psB[0:8, 0:NH], AF.Exp), ["psB0"], ["alpha"], "alphaexp")
    vop(lambda e: e.tensor_scalar_mul(SB["alpha"][:], SB["alpha"][:], SB["sel_sb"][:]),
        ["alpha", "sel"], ["alpha"], "alphasel", late=True)
    dma("gpsimd", T["al_dram"][:], SB["alpha"][:], ["alpha"], ["al_dram"], "st_alpha")
    vop(lambda e: e.memset(SB["S_run"][:], 0.0), ["cc_in"], ["S_run"], "sacc0", late=True)
    for j in range(8):
        dma("gpsimd", SB["S_j"][:], T["cc_out"][j * 129 : j * 129 + 128, :], ["cc_out"], ["S_j"], f"ldsj{j}")
        dma("scalar", SB["Pb"][:], T["al_dram"][j : j + 1, :].broadcast_to([128, NH]), ["al_dram"], ["Pb"], f"ldal{j}")
        vop(lambda e: e.tensor_tensor(
                SB["S_tmp"][:].rearrange("p (a b) -> p a b", b=HD),
                SB["S_j"][:].rearrange("p (a b) -> p a b", b=HD),
                rep3(SB["Pb"][:], HD), ALU.mult),
            ["S_j", "Pb"], ["S_tmp"], f"sjw{j}")
        vop(lambda e: e.tensor_add(SB["S_run"][:], SB["S_run"][:], SB["S_tmp"][:]),
            ["S_run", "S_tmp"], ["S_run"], f"sacc{j}")
    vop(lambda e: e.tensor_copy(v_saccb[:], SB["S_run"][:]), ["S_run"], ["S_accb", "L1"], "saccb")

    for c in range(NCH):
        bi = c % 2
        ylk = f"yl2{bi}"
        dma("gpsimd", v_yl2[bi][:], T["yl_dram"][c * 128 : (c + 1) * 128, :],
            [f"yl{c}"], [ylk, f"ysb1_{bi}"], f"ldyl{c}")
        dma("sync", v_sz[bi][:], T["sz_dram"][c * 128 : (c + 1) * 128, :], [f"szd{c}"], [f"szr{bi}", f"sz{bi}", f"Mp{bi}"], f"ldsz{c}")
        for j4 in range(4):
            top((lambda c_, j_: lambda e: e.matmul(
                    psA[:, j_ * 512 : (j_ + 1) * 512], SB["Cf"][:, c_ * 128 : (c_ + 1) * 128],
                    v_saccb[:, j_ * 512 : (j_ + 1) * 512], start=True, stop=True))(c, j4),
                [f"Cf{c // 4}", "S_accb"], [f"psA{j4}"], f"corr{c}_{j4}")
        vop((lambda c_: lambda e: e.tensor_tensor(
                v_y1[:].rearrange("p (a b) -> p a b", b=HD),
                psA[:, 0:2048].rearrange("p (a b) -> p a b", b=HD),
                rep3(SB["eseg_tok"][:, c_ * NH : (c_ + 1) * NH], HD),
                ALU.mult))(c),
            PSA + ["eseg_tok"], ["y1", "tmpb", "ylb"], f"ycorr{c}")
        vop((lambda b_: lambda e: e.tensor_add(v_y1[:], v_y1[:], v_yl2[b_][:]))(bi),
            ["y1", ylk], ["y1"], f"yfull{c}")
        vop((lambda b_: lambda e: e.tensor_tensor(v_g0[:], v_y1[:], v_sz[b_][:], ALU.mult))(bi),
            ["y1", f"szr{bi}"], ["g0", "L0"], f"gate{c}")
        sop(lambda e: e.activation(v_y1[:], v_g0[:], AF.Square, accum_out=SB["stat1"][:]),
            ["g0"], ["stat1", "y1"], f"rmssq{c}", late=True)
        sop(lambda e: e.activation(SB["stat2"][:], SB["stat1"][:], AF.Ln, scale=1.0 / DI, bias=EPS),
            ["stat1"], ["stat2"], f"rmsln{c}")
        sop(lambda e: e.activation(SB["stat2"][:], SB["stat2"][:], AF.Exp, scale=-0.5),
            ["stat2"], ["stat2"], f"rmsexp{c}")
        sop(lambda e: e.activation(v_g0[:], v_g0[:], AF.Identity, scale=SB["stat2"][:]),
            ["g0", "stat2"], ["g0"], f"rmsapp{c}")
        dma("gpsimd", T["g_dram"][c * 128 : (c + 1) * 128, :], v_g0[:], ["g0"], [f"g{c}"], f"stg{c}")

    GK = [f"g{c}" for c in range(NCH)]
    P.barrier("barPOST")

    for j in range(16):
        skey, sbuf = stg[j % 2]
        dma("sync", sbuf[:, 0:1024], T["wopT"][j * 128 : (j + 1) * 128, :], [], [skey], f"ldwo{j}")
        sop((lambda j_, s_: lambda e: e.activation(v_wo[:, j_ * 1024 : (j_ + 1) * 1024], s_[:, 0:1024], AF.Identity, scale=SB["gamma_sb"][:, j_ : j_ + 1]))(j, sbuf),
            [skey, "gamma"], ["wo"], f"castwo{j}")
    dma("sync", v_W1r[:], T["W1"].rearrange("a b -> (a b)").unsqueeze(0).broadcast_to([128, 1024]), [], ["W1r"], "ld_W1")
    dma("sync", v_b1r[:], T["b1"].rearrange("a b -> (a b)").unsqueeze(0).broadcast_to([128, 1024]), [], ["b1r"], "ld_b1")
    dma("sync", v_W2r[:], T["W2"].rearrange("a b -> (a b)").unsqueeze(0).broadcast_to([128, 1024]), [], ["W2r"], "ld_W2")

    GFK = [f"gfh{j}" for j in range(16)]

    def plain_ln(src_ap, src_keys, dst_ap, dst_key, tag, co, bi):
        Lx = SB["lnsc"]
        bsk, agk = f"bs{bi}", f"ag{bi}"
        bs, ag = SB[bsk], SB[agk]
        vop((lambda s_, bs_: lambda e: e.bn_stats(bs_[:, 0:6], s_[:, 0:512]))(src_ap, bs),
            src_keys, [bsk], f"{tag}bna", late=True)
        vop((lambda s_, bs_: lambda e: e.bn_stats(bs_[:, 6:12], s_[:, 512:1024]))(src_ap, bs),
            src_keys, [bsk], f"{tag}bnb", late=True)
        vop((lambda bs_, ag_: lambda e: e.bn_aggr(ag_[:], bs_[:]))(bs, ag),
            [bsk], [agk], f"{tag}bng", late=True)
        sop((lambda c_, ag_: lambda e: e.activation(Lx[:, c_ + 3 : c_ + 4], ag_[:, 1:2], AF.Ln, bias=EPS))(co, ag),
            [agk], [f"{tag}lnv"], f"{tag}lnv")
        sop((lambda c_: lambda e: e.activation(Lx[:, c_ + 4 : c_ + 5], Lx[:, c_ + 3 : c_ + 4], AF.Exp, scale=-0.5))(co),
            [f"{tag}lnv"], [f"{tag}r"], f"{tag}r")
        vop((lambda s_, d_, c_, ag_: lambda e: e.tensor_scalar(d_[:], s_[:], ag_[:, 0:1], Lx[:, c_ + 4 : c_ + 5], ALU.subtract, ALU.mult))(src_ap, dst_ap, co, ag),
            src_keys + [f"{tag}r", agk], [dst_key], f"{tag}app", late=True)

    for th in range(2):
        for j in range(16):
            dma("sync", v_gfh[j][:],
                T["g_dram"][th * 1024 : (th + 1) * 1024, j * 128 : (j + 1) * 128],
                [f"g{c_}" for c_ in range(th * 8, th * 8 + 8)], [f"gfh{j}"], f"gfT{th}_{j}", transpose=True)
        for r in range(th * 8, th * 8 + 8):
            bi = r % 2
            rr = r - th * 8
            xk = f"xfin{bi}"
            ok = f"vout{bi}"
            co = bi * 8
            st1 = "stat1" if bi == 0 else "stat3"
            st2 = "stat2" if bi == 0 else "stat4"
            dma("sync", v_xfin[bi][:], T["x"][128 + r * 128 : 128 + (r + 1) * 128, :], [], [xk], f"ldx2_{r}")
            for j in range(16):
                for d2 in range(2):
                    top((lambda rr_, j_, d_: lambda e: e.matmul(
                            psA[:, d_ * 512 : (d_ + 1) * 512], v_gfh[j_][:, rr_ * 128 : (rr_ + 1) * 128],
                            v_wo[:, j_ * 1024 + d_ * 512 : j_ * 1024 + (d_ + 1) * 512],
                            start=(j_ == 0), stop=(j_ == 15), skip_group_check=True))(rr, j, d2),
                        GFK + ["wo"], [f"psA{d2}"], f"opmm{r}_{j}_{d2}")
            sop((lambda b_, r_: lambda e: e.activation(v_moh[b_][:], v_xfin[b_][:], AF.Identity, scale=SB["rstd0"][:, r_ + 1 : r_ + 2], bias=SB["nm0r0"][:, r_ + 1 : r_ + 2]))(bi, r),
                [xk, "rstd0", "nm0r0"], [f"moh{bi}"], f"u0_{r}")
            vop((lambda b_: lambda e: e.tensor_add(v_moh[b_][:], v_moh[b_][:], psA[:, 0:1024]))(bi),
                [f"moh{bi}", "psA0", "psA1"], [f"moh{bi}"], f"moadd{r}")
            plain_ln(v_moh[bi], [f"moh{bi}"], v_moh[bi], f"moh{bi}", f"ln2_{r}_", co, bi)
            vop((lambda b_: lambda e: e.tensor_add(v_x2[b_][:], v_xfin[b_][:], v_moh[b_][:]))(bi),
                [xk, f"moh{bi}"], [f"x2{bi}"], f"x2_{r}")
            plain_ln(v_x2[bi], [f"x2{bi}"], v_u3[bi], f"u3{bi}", f"ln3_{r}_", co, bi)
            for j in range(8):
                top((lambda j_, b_: lambda e: e.transpose(psTb[:, j_ * 128 : (j_ + 1) * 128], v_u3[b_][:, j_ * 128 : (j_ + 1) * 128], SB["identb"][:]))(j, bi),
                    [f"u3{bi}", "identb"], ["psTb"], f"Tu3_{r}_{j}")
            sop((lambda b_: lambda e: e.activation(v_u3f[b_][:], psTb[:], AF.Copy))(bi),
                ["psTb"], [f"u3f{bi}"], f"Tu3e_{r}", late=True)
            for d in range(8):
                top((lambda d_, b_: lambda e: e.matmul(psB[:, 0:64], v_u3f[b_][:, d_ * 128 : (d_ + 1) * 128],
                                                       SB["Win_bf"][:, d_ * 64 : (d_ + 1) * 64], start=(d_ == 0), stop=False))(d, bi),
                    [f"u3f{bi}", "win_bf"], ["psB0"], f"ffs{r}_{d}")
            top(lambda e: e.matmul(psB[:, 0:64], SB["onesK"][:], SB["binrow"][:], start=False, stop=True),
                ["onesK", "binrow"], ["psB0"], f"ffsb{r}")
            vop((lambda b_: lambda e: e.tensor_tensor(
                    v_moh[b_][:].rearrange("p (a b) -> p a b", b=16),
                    psB[:, 0:64].unsqueeze(2).broadcast_to([128, 64, 16]),
                    v_W1r[:].rearrange("p (a b) -> p a b", b=16), ALU.mult))(bi),
                ["psB0", "W1r"], [f"moh{bi}"], f"h1_{r}")
            vop((lambda b_: lambda e: e.tensor_add(v_moh[b_][:], v_moh[b_][:], v_b1r[:]))(bi),
                [f"moh{bi}", "b1r"], [f"moh{bi}"], f"h2_{r}")
            sop((lambda b_: lambda e: e.activation(v_hb[b_][:], v_moh[b_][:], AF.Relu))(bi),
                [f"moh{bi}"], [f"hb{bi}"], f"h3_{r}")
            vop((lambda b_: lambda e: e.tensor_tensor(v_moh[b_][:], v_hb[b_][:], v_W2r[:], ALU.mult))(bi),
                [f"hb{bi}", "W2r"], [f"moh{bi}"], f"hw_{r}")
            vop((lambda b_: lambda e: e.tensor_reduce(v_g2[b_][:], v_moh[b_][:].rearrange("p (a b) -> p a b", b=16), AXX, ALU.add))(bi),
                [f"moh{bi}"], [f"g2{bi}"], f"g2_{r}", late=True)
            vop((lambda b_: lambda e: e.tensor_add(v_g2[b_][:], v_g2[b_][:], SB["b2row"][:]))(bi),
                [f"g2{bi}", "b2r"], [f"g2{bi}"], f"g2b_{r}")
            sop((lambda b_: lambda e: e.activation(v_g2b[b_][:], v_g2[b_][:], AF.Copy))(bi),
                [f"g2{bi}"], [f"g2b{bi}"], f"g2c_{r}", late=True)
            top((lambda b_: lambda e: e.transpose(psTb[0:64, 0:128], v_g2b[b_][:], SB["identb"][:]))(bi),
                [f"g2b{bi}", "identb"], ["psTb"], f"Tg2_{r}")
            sop((lambda b_: lambda e: e.activation(v_u3f[b_][0:64, 0:128], psTb[0:64, 0:128], AF.Copy))(bi),
                ["psTb"], [f"u3f{bi}"], f"Tg2e_{r}")
            for d2 in range(2):
                top((lambda d_, b_: lambda e: e.matmul(psB[:, d_ * 512 : (d_ + 1) * 512], v_u3f[b_][0:64, 0:128],
                                                       SB["Wout2_bf"][:, d_ * 512 : (d_ + 1) * 512], start=True, stop=False, skip_group_check=True))(d2, bi),
                    [f"u3f{bi}", "wo2_bf"], [f"psB{d2}"], f"fo{r}_{d2}")
            for d2 in range(2):
                top((lambda d_: lambda e: e.matmul(psB[:, d_ * 512 : (d_ + 1) * 512], SB["onesK"][:],
                                                   SB["boutrow"][:, d_ * 512 : (d_ + 1) * 512], start=False, stop=True, skip_group_check=True))(d2),
                    ["onesK", "boutrow"], [f"psB{d2}"], f"fob{r}_{d2}")
            sop((lambda b_: lambda e: e.activation(v_moh[b_][:], psB[:, 0:1024], AF.Copy))(bi),
                ["psB0", "psB1"], [f"moh{bi}"], f"foevac{r}")
            plain_ln(v_moh[bi], [f"moh{bi}"], v_moh[bi], f"moh{bi}", f"ln4_{r}_", co, bi)
            vop((lambda b_: lambda e: e.tensor_add(v_o[b_][:], v_x2[b_][:], v_moh[b_][:]))(bi),
                [f"x2{bi}", f"moh{bi}"], [ok], f"out_{r}")
            dma("sync", T["out"][r * 128 : (r + 1) * 128, :], v_o[bi][:], [ok], [f"outd{r}"], f"stout{r}")

    P.barrier("barEND")

    P.resolve()
    global LAST_PLAN
    LAST_PLAN = P
    from contextlib import ExitStack
    es = ExitStack()
    semmap = {}
    for e in ENGINES:
        semmap["prog_" + e] = es.enter_context(nc.semaphore("prog_" + e))
    for sk in P.dma_slot_sems:
        semmap[sk] = es.enter_context(nc.semaphore(sk))
    block = es.enter_context(nc.Block())

    SEQ = [P.ops[i] for i in P.order]

    def emit_engine(ename):
        def run(eng_obj):
            for o in SEQ:
                if o.engine != ename:
                    continue
                for sk, v in o.waits:
                    eng_obj.wait_ge(semmap[sk], v)
                inst = o.fn(eng_obj)
                for sk, v in o.incs:
                    if inst is None:
                        raise RuntimeError(f"op {o.tag} has incs but returned no instruction")
                    inst = inst.then_inc(semmap[sk], v)
        return run

    block.tensor(emit_engine("tensor"))
    block.vector(emit_engine("vector"))
    block.scalar(emit_engine("scalar"))
    block.sync(emit_engine("sync"))
    block.gpsimd(emit_engine("gpsimd"))
    es.close()
    return nc


def make_in_maps(inputs):
    f32 = np.float32
    x = np.asarray(inputs["L0_attn_x"], f32).reshape(16384, DM)
    xg = np.concatenate([np.zeros((128, DM), f32), x], axis=0)
    wipT = np.ascontiguousarray(np.asarray(inputs["L0_attn_in_proj_w"], f32).T)
    wipA3 = np.ascontiguousarray(
        np.asarray(inputs["L0_attn_in_proj_w"], f32)[2048:4352]
        .reshape(18, 128, 8, 128)
        .transpose(0, 3, 2, 1)
        .reshape(18 * 128, 1024))
    wopT = np.ascontiguousarray(np.asarray(inputs["L0_attn_out_proj_w"], f32).T)
    ident_bf = np.eye(128, dtype=ml_dtypes.bfloat16)
    ident_f32 = np.eye(128, dtype=f32)
    caus = np.triu(np.ones((128, 128), f32))
    in_maps = []
    for i in range(CORES):
        sel = (np.arange(8)[:, None] < i).astype(f32)
        m2t = np.zeros((8, 8), f32)
        for k in range(8):
            for j in range(8):
                m2t[k, j] = 1.0 if (j < k < i) else 0.0
        in_maps.append({
            "x": np.ascontiguousarray(xg[i * SEG : i * SEG + XROWS]),
            "wipT": wipT,
            "wipA3": wipA3,
            "wopT": wopT,
            "conv_w": np.asarray(inputs["L0_attn_conv_w"], f32),
            "conv_wT": np.ascontiguousarray(np.asarray(inputs["L0_attn_conv_w"], f32).T),
            "conv_b": np.asarray(inputs["L0_attn_conv_b"], f32),
            "A_log": np.asarray(inputs["L0_attn_A_log"], f32),
            "Dv": np.asarray(inputs["L0_attn_D"], f32),
            "dt_bias": np.asarray(inputs["L0_attn_dt_bias"], f32),
            "gamma": np.asarray(inputs["L0_attn_gamma"], f32),
            "ln_w": np.asarray(inputs["L0_attn_ln_w"], f32),
            "ln_b": np.asarray(inputs["L0_attn_ln_b"], f32),
            "Win": np.asarray(inputs["L0_ffn_Win"], f32),
            "bin": np.asarray(inputs["L0_ffn_bin_"], f32),
            "W1": np.asarray(inputs["L0_ffn_W1"], f32),
            "b1": np.asarray(inputs["L0_ffn_b1"], f32),
            "W2": np.asarray(inputs["L0_ffn_W2"], f32),
            "b2": np.asarray(inputs["L0_ffn_b2"], f32),
            "Wout2": np.asarray(inputs["L0_ffn_Wout"], f32),
            "bout": np.asarray(inputs["L0_ffn_bout"], f32),
            "ident_bf": ident_bf,
            "ident_f32": ident_f32,
            "caus": caus,
            "sel": sel,
            "m2t": m2t,
            "hmask": np.full((1, 1), 0.0 if i == 0 else 1.0, f32),
        })
    return in_maps


def kernel(**inputs):
    nc = build_program()
    in_maps = make_in_maps(inputs)
    res = run_bass_kernel_spmd(nc, in_maps, core_ids=list(range(CORES)))
    out = np.concatenate([r["out"] for r in res.results], axis=0)
    return out[None]


# version 16
# speedup vs baseline: 1.0567x; 1.0567x over previous
import numpy as np
import ml_dtypes
import concourse.bass as bass
import concourse.mybir as mybir
from concourse.bass_utils import run_bass_kernel_spmd

F32 = mybir.dt.float32
BF16 = mybir.dt.bfloat16
U8 = mybir.dt.uint8
AF = mybir.ActivationFunctionType
ALU = mybir.AluOpType
AXX = mybir.AxisListType.X

EPS = 1e-5
DM = 1024
DI = 2048
DS = 128
NH = 32
HD = 64
CDIM = 2304
SEG = 2048
Q = 128
NCH = SEG // Q
NT = 16
XROWS = SEG + 128
CORES = 8
NH2 = 16

ENGINES = ("tensor", "vector", "scalar", "sync", "gpsimd")


class Op:
    __slots__ = ("idx", "engine", "fn", "deps", "is_dma", "tag", "waits", "incs", "flagged", "late", "dma_idx")

    def __init__(self, idx, engine, fn, deps, is_dma, tag):
        self.idx = idx
        self.engine = engine
        self.fn = fn
        self.deps = deps
        self.is_dma = is_dma
        self.tag = tag
        self.waits = []
        self.incs = []
        self.flagged = False
        self.late = False
        self.dma_idx = -1


class Plan:
    NSLOT = 10

    def __init__(self):
        self.ops = []
        self.last_write = {}
        self.readers = {}
        self.dma_counts = {}

    def op(self, engine, fn, reads=(), writes=(), dma=False, tag="", after=(), late=False):
        deps = set()
        for k in reads:
            if k in self.last_write:
                deps.add(self.last_write[k])
        for k in writes:
            if k in self.last_write:
                deps.add(self.last_write[k])
            for r in self.readers.get(k, ()):
                deps.add(r)
        for o in after:
            deps.add(o.idx)
        o = Op(len(self.ops), engine, fn, deps, dma, tag)
        o.late = late
        if dma:
            o.dma_idx = self.dma_counts.get(engine, 0)
            self.dma_counts[engine] = o.dma_idx + 1
        self.ops.append(o)
        for k in reads:
            self.readers.setdefault(k, []).append(o.idx)
        for k in writes:
            self.last_write[k] = o.idx
            self.readers[k] = []
        return o

    def barrier(self, tag="bar"):
        lasts = {}
        last_dma = {}
        for o in self.ops:
            lasts[o.engine] = o
            if o.is_dma:
                last_dma[(o.engine, o.dma_idx % self.NSLOT)] = o
        deps = list(lasts.values()) + list(last_dma.values())
        for e in ENGINES:
            self.op(e, _mknop(e), after=deps, tag=tag + "_" + e)


    def schedule(self):
        n = len(self.ops)
        dur = [0.0] * n
        for o in self.ops:
            t = o.tag
            if o.is_dma:
                dur[o.idx] = 4.0 if t.startswith(("xsT", "gfT", "uft")) else 2.5
            elif o.engine == "tensor":
                if t.startswith(("xmm", "zmm", "opmm", "dtmm", "yint", "smm", "corr")):
                    dur[o.idx] = 0.25
                elif t.startswith("cnv"):
                    dur[o.idx] = 0.5
                elif t.startswith("hmm"):
                    dur[o.idx] = 0.12
                elif t.startswith("yin"):
                    dur[o.idx] = 0.12
                else:
                    dur[o.idx] = 0.35
            elif o.engine == "vector":
                if t.startswith(("E", "yie", "yadd", "supd", "stmp", "ycorr", "yfull", "sjw", "sacc")):
                    dur[o.idx] = 2.2
                elif t.startswith(("xdt", "xf", "mp", "gate", "dxs", "yD", "ycast", "srb", "Emin")):
                    dur[o.idx] = 1.2
                else:
                    dur[o.idx] = 0.5
            elif o.engine == "scalar":
                if t.startswith(("expL", "nrelu")):
                    dur[o.idx] = 3.6
                elif t.startswith(("zsilu", "ycast", "srb")):
                    dur[o.idx] = 1.8
                elif t.startswith(("silu", "castw", "uhat", "sq", "sp_", "evac", "evh", "foevac", "Tu3e", "g2c", "Tg2e")):
                    dur[o.idx] = 1.0
                else:
                    dur[o.idx] = 0.3
            elif t == "allgather":
                dur[o.idx] = 60.0
            else:
                dur[o.idx] = 0.3
        succ = [[] for _ in range(n)]
        for o in self.ops:
            for d in o.deps:
                succ[d].append(o.idx)
        prio = [0.0] * n
        for i in range(n - 1, -1, -1):
            m = 0.0
            for s in succ[i]:
                if prio[s] > m:
                    m = prio[s]
            prio[i] = dur[i] + m
        regions = []
        i = 0
        ops = self.ops
        while i < n:
            isbar = ops[i].tag.startswith("bar")
            j = i
            while j < n and ops[j].tag.startswith("bar") == isbar:
                j += 1
            regions.append(("bar" if isbar else "work", list(range(i, j))))
            i = j
        import heapq
        order = []
        fin = [0.0] * n
        t_now = 0.0
        done = [False] * n
        for kind, idxs in regions:
            if kind == "bar":
                for i2 in idxs:
                    order.append(i2)
                    fin[i2] = t_now
                    done[i2] = True
                continue
            remaining = set(idxs)
            indeg = {}
            ready = []
            for i2 in idxs:
                k = sum(1 for d in self.ops[i2].deps if d in remaining)
                indeg[i2] = k
                if k == 0:
                    heapq.heappush(ready, (-prio[i2], i2))
            eng_free = {e: t_now for e in ENGINES}
            while remaining:
                progressed = False
                for e in ENGINES:
                    if not ready:
                        break
                    best = None
                    tmp = []
                    while ready:
                        p, i2 = heapq.heappop(ready)
                        if self.ops[i2].engine == e and best is None:
                            best = (p, i2)
                        else:
                            tmp.append((p, i2))
                    for x in tmp:
                        heapq.heappush(ready, x)
                    if best is None:
                        continue
                    p, i2 = best
                    dep_fin = t_now
                    for d in self.ops[i2].deps:
                        if fin[d] > dep_fin:
                            dep_fin = fin[d]
                    st = max(eng_free[e], dep_fin)
                    if self.ops[i2].is_dma:
                        eng_free[e] = st + 0.08
                        fin[i2] = st + dur[i2]
                    else:
                        eng_free[e] = st + dur[i2]
                        fin[i2] = eng_free[e]
                    done[i2] = True
                    order.append(i2)
                    remaining.discard(i2)
                    progressed = True
                    for s2 in succ[i2]:
                        if s2 in indeg:
                            indeg[s2] -= 1
                            if indeg[s2] == 0:
                                heapq.heappush(ready, (-prio[s2], s2))
                if not progressed:
                    i2 = min(remaining)
                    if indeg.get(i2, 0) != 0:
                        raise RuntimeError("scheduler stuck")
                    order.append(i2)
                    done[i2] = True
                    remaining.discard(i2)
            t_now = max(fin[i2] for i2 in idxs) if idxs else t_now
        assert len(order) == n
        self.order = order
        self.est_span = t_now

    def resolve(self):
        self.schedule()
        seq = [self.ops[i] for i in self.order]
        lasts = {}
        last_dma_all = []
        for o in seq:
            if o.tag.startswith("bar"):
                o.deps = set(x.idx for x in lasts.values()) | set(x.idx for x in last_dma_all[-80:])
            else:
                lasts[o.engine] = o
                if o.is_dma:
                    last_dma_all.append(o)
        for o in self.ops:
            for d in o.deps:
                do = self.ops[d]
                if not do.is_dma:
                    do.flagged = True
        comp_count = {e: 0 for e in ENGINES}
        slot_count = {}
        events = [None] * len(self.ops)
        pre_waits = [None] * len(self.ops)
        dma_seq = {}
        for o in seq:
            if o.is_dma:
                o.dma_idx = dma_seq.get(o.engine, 0)
                dma_seq[o.engine] = o.dma_idx + 1
        for o in seq:
            if o.is_dma:
                slot = o.dma_idx % self.NSLOT
                sk = f"dmaq_{o.engine}{slot}"
                n = slot_count.get(sk, 0)
                pre_waits[o.idx] = (sk, n * 16)
                slot_count[sk] = n + 1
                o.incs.append((sk, 16))
                events[o.idx] = (sk, (n + 1) * 16)
            elif o.flagged:
                comp_count[o.engine] += 1
                o.incs.append(("prog_" + o.engine, 1))
                events[o.idx] = ("prog_" + o.engine, comp_count[o.engine])
            else:
                events[o.idx] = ("prog_" + o.engine, 0)
        water = {e: {} for e in ENGINES}
        self.dma_slot_sems = sorted(slot_count)
        self.dma_slot_finals = dict(slot_count)
        for o in seq:
            need = {}
            if pre_waits[o.idx] is not None:
                sk, v = pre_waits[o.idx]
                if v > 0:
                    need[sk] = v
            for d in sorted(o.deps):
                do = self.ops[d]
                sk, v = events[d]
                if v <= 0:
                    continue
                need[sk] = max(need.get(sk, 0), v)
            for sk, v in sorted(need.items()):
                if water[o.engine].get(sk, 0) >= v:
                    continue
                o.waits.append((sk, v))
                water[o.engine][sk] = v


def _mknop(ename):
    def f(eng):
        if hasattr(eng, "engine_nop"):
            return eng.engine_nop()
        return eng.nop()
    return f


def rep3(ap_small, reps):
    p, n = ap_small.shape
    return ap_small.unsqueeze(2).broadcast_to([p, n, reps])


def build_program(debug=()):
    nc = bass.Bass("TRN2", target_bir_lowering=False, debug=False)
    dbg = set(debug)

    def okind(name):
        return "ExternalOutput" if name in dbg else "Internal"

    T = {}
    for name, shape, dt_, kind in [
        ("x", [XROWS, DM], F32, "ExternalInput"),
        ("wipT", [DM, 4384], F32, "ExternalInput"),
        ("wipA3", [18 * 128, 1024], F32, "ExternalInput"),
        ("wopT", [DI, DM], F32, "ExternalInput"),
        ("conv_w", [CDIM, 4], F32, "ExternalInput"),
        ("conv_wT", [4, CDIM], F32, "ExternalInput"),
        ("conv_b", [CDIM], F32, "ExternalInput"),
        ("A_log", [NH], F32, "ExternalInput"),
        ("Dv", [NH], F32, "ExternalInput"),
        ("dt_bias", [NH], F32, "ExternalInput"),
        ("gamma", [DI], F32, "ExternalInput"),
        ("ln_w", [DM], F32, "ExternalInput"),
        ("ln_b", [DM], F32, "ExternalInput"),
        ("Win", [DM, 64], F32, "ExternalInput"),
        ("bin", [64], F32, "ExternalInput"),
        ("W1", [64, 16], F32, "ExternalInput"),
        ("b1", [64, 16], F32, "ExternalInput"),
        ("W2", [64, 16], F32, "ExternalInput"),
        ("b2", [64], F32, "ExternalInput"),
        ("Wout2", [64, DM], F32, "ExternalInput"),
        ("bout", [DM], F32, "ExternalInput"),
        ("ident_bf", [128, 128], BF16, "ExternalInput"),
        ("ident_f32", [128, 128], F32, "ExternalInput"),
        ("caus", [128, 128], F32, "ExternalInput"),
        ("sel", [8, 1], F32, "ExternalInput"),
        ("m2t", [8, 8], F32, "ExternalInput"),
        ("hmask", [1, 1], F32, "ExternalInput"),
        ("out", [SEG, DM], F32, "ExternalOutput"),
    ]:
        T[name] = nc.dram_tensor(name, shape, dt_, kind=kind).ap()
    T["cc_out"] = nc.dram_tensor("cc_out", [8 * 129, DI], F32, kind=okind("cc_out"), addr_space="Shared").ap()
    for name, shape, dt_ in [
        ("u_dram", [XROWS, DM], BF16),
        ("xs_dram", [DI, SEG], BF16),
        ("yl_dram", [SEG, DI], BF16),
        ("sz_dram", [SEG, DI], BF16),
        ("g_dram", [SEG, DI], BF16),
        ("lc_dram", [NCH * NH, 128], F32),
        ("pf_dram", [NCH, NH], F32),
        ("al_dram", [8, NH], F32),
        ("cc_in", [129, DI], F32),

    ]:
        T[name] = nc.dram_tensor(name, shape, dt_, kind=okind(name)).ap()

    if "dbg_a3" in dbg:
        T["dbg_a3"] = nc.dram_tensor("dbg_a3", [1024, 516], F32, kind="ExternalOutput").ap()

    SB = {}

    def sball(name, shape, dtype):
        SB[name] = nc.alloc_sbuf_tensor(name, shape, dtype).ap()
        return SB[name]

    sball("cpool", [128, 6], F32)
    sball("ones_c", [NH, 1], F32)
    sball("zcol", [NH, 1], F32)
    sball("A_sb", [NH, 1], F32)
    sball("dtb_sb", [NH, 1], F32)
    sball("negoff", [NH, NCH], F32)
    sball("PF", [NH, NCH], F32)
    sball("dt_tok", [128, NT * NH], F32)
    sball("dtf_tok", [128, NT * NH], F32)
    sball("e_tok", [128, NT * NH], F32)
    sball("eseg_tok", [128, NT * NH], F32)
    sball("lc_tok", [128, NT * NH], F32)
    sball("Bf", [128, SEG], BF16)
    sball("Cf", [128, SEG], BF16)
    sball("caus_sb", [128, 128], F32)
    sball("identb", [128, 128], BF16)
    sball("identf", [128, 128], F32)
    sball("Drep", [128, NH], F32)
    sball("Pb", [128, NH], F32)
    sball("rstd0", [128, NT + 1], F32)
    sball("nm0r0", [128, NT + 1], F32)
    sball("convw", [128, 18 * 4], F32)
    sball("cwr", [72, 128], F32)
    sball("cbr", [18, 128], F32)
    sball("gmr", [16, 128], F32)
    sball("pf_t", [16, NH], F32)
    sball("convb", [128, 18], F32)
    sball("hm", [128, 1], F32)
    sball("gamma_sb", [128, 16], F32)
    sball("b2row", [128, 64], F32)
    sball("binrow", [1, 64], BF16)
    sball("boutrow", [1, DM], BF16)
    sball("onesK", [1, 128], BF16)
    sball("Win_bf", [128, 8 * 64], BF16)
    sball("Wout2_bf", [64, DM], BF16)
    sball("lnsc", [128, 16], F32)
    sball("m2t_sb", [8, 8], F32)
    sball("sel_sb", [8, 1], F32)
    sball("lp_sb", [8, NH], F32)
    sball("alpha", [8, NH], F32)
    sball("stat1", [128, 1], F32)
    sball("stat2", [128, 1], F32)
    sball("stat3", [128, 1], F32)
    sball("stat4", [128, 1], F32)
    sball("bs0", [128, 12], F32)
    sball("bs1", [128, 12], F32)
    sball("ag0", [128, 2], F32)
    sball("ag1", [128, 2], F32)
    sball("scr", [128, DM], F32)
    sball("wstage", [128, 8 * 128], F32)
    sball("wstage2", [128, 8 * 128], F32)
    sball("wbufA", [128, 8 * 128], BF16)
    sball("wbufB", [128, 8 * 128], BF16)
    sball("halo3", [128, 18 * 4], BF16)
    sball("S_run", [128, DI], F32)
    sball("S_j", [128, DI], F32)
    sball("S_tmp", [128, DI], F32)
    sball("Btok0", [128, 128], BF16)
    sball("Btok1", [128, 128], BF16)
    sball("zrow", [1, DI], F32)
    sball("Gm0", [128, 128], BF16)
    sball("Gm1", [128, 128], BF16)

    ARN = 133120
    AR = nc.alloc_sbuf_tensor("arena", [128, ARN], U8).ap()

    def ar(off, p, f, dtype):
        nbytes = f * (2 if dtype == BF16 else 4)
        assert off % 4 == 0 and off + nbytes <= ARN, (off, nbytes)
        return AR[0:p, off : off + nbytes].bitcast(dtype)

    v_uf = ar(0, 128, 8 * XROWS, BF16)
    v_xsb = [ar(34816, 128, 1024, F32), ar(38912, 128, 1024, F32)]
    v_usr = [ar(43008, 128, 1024, F32), ar(47104, 128, 1024, F32)]
    v_utok = [ar(51200, 128, 1024, BF16), ar(53248, 128, 1024, BF16)]
    v_dtF = ar(34816, NH, SEG, F32)
    v_lcF = ar(43008, NH, SEG, F32)
    v_wF = ar(51200, NH, SEG, F32)
    v_work = [ar(59392, 128, 516, BF16), ar(60448, 128, 516, BF16)]
    v_cvo = [ar(61504, 128, 512, BF16), ar(62528, 128, 512, BF16)]
    v_silu = [ar(63616, 128, 512, BF16), ar(64640, 128, 512, BF16)]
    v_wrow = ar(69760, 128, DM, F32)
    v_brow = ar(73856, 128, DM, F32)
    v_xst = [ar(34816, 128, 2048, BF16), ar(38912, 128, 2048, BF16)]
    v_xdt = [ar(43008, 128, 2048, BF16), ar(47104, 128, 2048, BF16)]
    v_xf = [ar(51200, 128, 2048, BF16), ar(55296, 128, 2048, BF16)]
    v_ysb = [ar(59392, 128, 2048, F32), ar(67584, 128, 2048, F32)]
    v_tmpb = ar(75776, 128, 2048, BF16)
    v_ylb = ar(79872, 128, 2048, BF16)
    v_L = [ar(83968, 128, 2048, BF16), ar(88064, 128, 2048, BF16)]
    v_Mp = [ar(92160, 128, 2048, BF16), ar(96256, 128, 2048, BF16),
            ar(100352, 128, 2048, BF16), ar(104448, 128, 2048, BF16)]
    v_lcb = [ar(108544, 128, 2048, F32), ar(116736, 128, 2048, F32)]
    v_diag = ar(108544, 128, 72 * 128, BF16)
    v_srunb = ar(124928, 128, 2048, BF16)
    v_wz = ar(34816, 128, 8 * DI, BF16)
    v_yl2 = [ar(67584, 128, 2048, BF16), ar(71680, 128, 2048, BF16)]
    v_y1 = [ar(75776, 128, 2048, F32), ar(100352, 128, 2048, F32)]
    v_g0 = [ar(83968, 128, 2048, BF16), ar(108544, 128, 2048, BF16)]
    v_saccb = ar(88064, 128, 2048, F32) if False else ar(88064, 128, 2048, BF16)
    v_sz = [ar(92160, 128, 2048, BF16), ar(96256, 128, 2048, BF16)]
    v_gfh = [ar(j * 2048, 128, 1024, BF16) for j in range(16)]
    v_wo = ar(32768, 128, 16 * 1024, BF16)
    v_W1r = ar(65536, 128, 1024, F32)
    v_b1r = ar(69632, 128, 1024, F32)
    v_W2r = ar(73728, 128, 1024, F32)
    v_xfin = [ar(77824, 128, 1024, F32), ar(81920, 128, 1024, F32)]
    v_x2 = [ar(86016, 128, 1024, F32), ar(108928, 128, 1024, F32)]
    v_moh = [ar(90112, 128, 1024, F32), ar(113024, 128, 1024, F32)]
    v_u3 = [ar(94208, 128, 1024, BF16), ar(117120, 128, 1024, BF16)]
    v_u3f = [ar(96256, 128, 1024, BF16), ar(119168, 128, 1024, BF16)]
    v_hb = [ar(98304, 128, 1024, BF16), ar(121216, 128, 1024, BF16)]
    v_g2 = [ar(100352, 128, 64, F32), ar(123264, 128, 64, F32)]
    v_g2b = [ar(100608, 128, 64, BF16), ar(123520, 128, 64, BF16)]
    v_o = [ar(100736, 128, 1024, F32), ar(104832, 128, 1024, F32)]

    psA = nc.alloc_psum_tensor("psA", [128, 2048], F32).ap()
    psB = nc.alloc_psum_tensor("psB", [128, 1024], F32).ap()
    psT = nc.alloc_psum_tensor("psT", [128, 512], F32).ap()
    psTb = nc.alloc_psum_tensor("psTb", [128, 1024], BF16).ap()
    PSA = ["psA0", "psA1", "psA2", "psA3"]

    P = Plan()

    def dma(eng, out, in_, reads, writes, tag="", transpose=False):
        return P.op(
            eng,
            (lambda o_, i_, tr: (lambda e: e.dma_start(out=o_, in_=i_, transpose=tr)))(out, in_, transpose),
            reads=reads, writes=writes, dma=True, tag=tag,
        )

    def vop(fn, reads, writes, tag="", late=False):
        return P.op("vector", fn, reads=reads, writes=writes, tag=tag, late=late)

    def sop(fn, reads, writes, tag="", late=False):
        return P.op("scalar", fn, reads=reads, writes=writes, tag=tag, late=late)

    def top(fn, reads, writes, tag=""):
        return P.op("tensor", fn, reads=reads, writes=writes, tag=tag)

    cp = SB["cpool"]
    _cvals = [(EPS, 0), (-0.5, 1), (-1.0, 2), (1.0 / DI, 3), (1.0 + EPS, 4), (EPS * EPS, 5)]
    for val, col in _cvals:
        nc.const_aps.register(F32, val, cp[:, col : col + 1])
        vop((lambda v_, c_: lambda e: e.memset(cp[:, c_ : c_ + 1], v_))(val, col), [], ["constap"], f"cst{col}", late=True)

    dma("sync", SB["caus_sb"][:], T["caus"][:], [], ["caus"], "ld_caus")
    dma("sync", SB["identb"][:], T["ident_bf"][:], [], ["identb"], "ld_idb")
    dma("sync", SB["identf"][:], T["ident_f32"][:], [], ["identf"], "ld_idf")
    dma("sync", SB["Drep"][:], T["Dv"].unsqueeze(0).broadcast_to([128, NH]), [], ["Drep"], "ld_D")
    dma("sync", v_wrow[:], T["ln_w"].unsqueeze(0).broadcast_to([128, DM]), [], ["wrow"], "ld_w")
    dma("sync", v_brow[:], T["ln_b"].unsqueeze(0).broadcast_to([128, DM]), [], ["brow"], "ld_b")
    dma("sync", SB["cwr"][:], T["conv_wT"].rearrange("k (ct p) -> (k ct) p", p=128), [], ["cwr"], "ld_cw")
    dma("sync", SB["cbr"][:], T["conv_b"].rearrange("(ct p) -> ct p", p=128), [], ["cbr"], "ld_cb")
    dma("sync", SB["hm"][:], T["hmask"].broadcast_to([128, 1]), [], ["hm"], "ld_hm")
    dma("sync", SB["A_sb"][:], T["A_log"].unsqueeze(1), [], ["A_sb"], "ld_A")
    dma("sync", SB["dtb_sb"][:], T["dt_bias"].unsqueeze(1), [], ["dtb"], "ld_dtb")
    dma("sync", SB["gmr"][:], T["gamma"].rearrange("(j p) -> j p", p=128), [], ["gmr"], "ld_gm")
    dma("sync", SB["b2row"][:], T["b2"].unsqueeze(0).broadcast_to([128, 64]), [], ["b2r"], "ld_b2")
    dma("sync", SB["m2t_sb"][:], T["m2t"][:], [], ["m2t"], "ld_m2t")
    dma("sync", SB["sel_sb"][:], T["sel"][:], [], ["sel"], "ld_sel")

    vop(lambda e: e.memset(SB["ones_c"][:], 1.0), [], ["ones_c"], "ones_c", late=True)
    vop(lambda e: e.memset(SB["zcol"][:], 0.0), [], ["zcol"], "zcol", late=True)
    vop(lambda e: e.memset(SB["onesK"][:], 1.0), [], ["onesK"], "onesK", late=True)
    sop(lambda e: e.activation(SB["A_sb"][:], SB["A_sb"][:], AF.Exp), ["A_sb", "constap"], ["A_sb"], "A_exp")
    vop(lambda e: e.tensor_single_scalar(SB["A_sb"][:], SB["A_sb"][:], -1.0, ALU.mult), ["A_sb"], ["A_sb"], "A_neg")
    dma("sync", SB["wstage"][:, 0:512].rearrange("p (j m) -> p j m", m=64), T["Win"].rearrange("(j p) m -> p j m", p=128), [], ["wstage"], "ld_win")
    sop(lambda e: e.activation(SB["Win_bf"][:], SB["wstage"][:, 0:512], AF.Copy), ["wstage"], ["win_bf"], "cast_win")
    dma("sync", SB["wstage2"][0:64, 0:1024], T["Wout2"][:], [], ["wstage2"], "ld_wo2")
    sop(lambda e: e.activation(SB["Wout2_bf"][:], SB["wstage2"][0:64, 0:1024], AF.Copy), ["wstage2"], ["wo2_bf"], "cast_wo2")
    dma("sync", SB["wstage"][0:1, 512:576], T["bin"].unsqueeze(0), ["win_bf"], ["wstage"], "ld_bin")
    sop(lambda e: e.activation(SB["binrow"][:], SB["wstage"][0:1, 512:576], AF.Copy), ["wstage"], ["binrow"], "cast_bin")
    dma("sync", SB["wstage2"][0:1, 0:1024], T["bout"].unsqueeze(0), ["wo2_bf"], ["wstage2"], "ld_bout")
    sop(lambda e: e.activation(SB["boutrow"][:], SB["wstage2"][0:1, 0:1024], AF.Copy), ["wstage2"], ["boutrow"], "cast_bout")

    top(lambda e: e.transpose(psT[:, 0:72], SB["cwr"][:], SB["identf"][0:72, 0:72]),
        ["cwr", "identf"], ["psT"], "Tcw")
    vop(lambda e: e.tensor_copy(SB["convw"][:].rearrange("p (k ct) -> p k ct", ct=18), psT[:, 0:72].rearrange("p (k ct) -> p k ct", ct=18)),
        ["psT"], ["convw"], "Tcwe")
    top(lambda e: e.transpose(psT[:, 0:18], SB["cbr"][:], SB["identf"][0:18, 0:18]),
        ["cbr", "identf"], ["psT"], "Tcb")
    vop(lambda e: e.tensor_copy(SB["convb"][:], psT[:, 0:18]), ["psT"], ["convb"], "Tcbe")
    for ct_ in range(18):
        for k_ in range(4):
            vop((lambda c2, k2: lambda e: e.tensor_scalar_mul(
                    v_diag[:, (c2 * 4 + k2) * 128 : (c2 * 4 + k2 + 1) * 128],
                    SB["identb"][:], SB["convw"][:, k2 * 18 + c2 : k2 * 18 + c2 + 1]))(ct_, k_),
                ["identb", "convw"], ["cdiag"], f"dg{ct_}_{k_}")
    top(lambda e: e.transpose(psT[:, 0:16], SB["gmr"][:], SB["identf"][0:16, 0:16]),
        ["gmr", "identf"], ["psT"], "Tgm")
    vop(lambda e: e.tensor_copy(SB["gamma_sb"][:], psT[:, 0:16]), ["psT"], ["gamma"], "Tgme")
    P.barrier("barC0")

    for r in range(17):
        bi = r % 2
        L = SB["lnsc"][:, bi * 8 : bi * 8 + 8]
        bsk, agk = f"bs{bi}", f"ag{bi}"
        bs, ag = SB[bsk], SB[agk]
        xk, uk, utk = f"xsb{bi}", f"usr{bi}", f"utok{bi}"
        dma("sync", v_xsb[bi][:], T["x"][r * 128 : (r + 1) * 128, :], [], [xk], f"ldx{r}")
        vop((lambda b_, bs_: lambda e: e.bn_stats(bs_[:, 0:6], v_xsb[b_][:, 0:512]))(bi, bs),
            [xk], [bsk], f"bna{r}", late=True)
        vop((lambda b_, bs_: lambda e: e.bn_stats(bs_[:, 6:12], v_xsb[b_][:, 512:1024]))(bi, bs),
            [xk], [bsk], f"bnb{r}", late=True)
        vop((lambda bs_, ag_: lambda e: e.bn_aggr(ag_[:], bs_[:]))(bs, ag),
            [bsk], [agk], f"bng{r}", late=True)
        sop((lambda L_, ag_: lambda e: e.activation(L_[:, 3:4], ag_[:, 1:2], AF.Ln, bias=EPS))(L, ag),
            [agk], ["lnt1" + str(bi)], f"t1{r}")
        sop((lambda r_, L_: lambda e: e.activation(SB["rstd0"][:, r_ : r_ + 1], L_[:, 3:4], AF.Exp, scale=-0.5))(r, L),
            ["lnt1" + str(bi)], ["rstd0"], f"r0{r}")
        sop((lambda L_, ag_: lambda e: e.activation(L_[:, 4:5], ag_[:, 1:2], AF.Ln, scale=1.0 + EPS, bias=EPS * EPS))(L, ag),
            [agk], ["lnt12" + str(bi)], f"t12{r}")
        sop((lambda L_: lambda e: e.activation(L_[:, 5:6], L_[:, 4:5], AF.Exp, scale=-0.5))(L),
            ["lnt12" + str(bi)], ["lnq" + str(bi)], f"q{r}")
        vop((lambda r_, L_, ag_: lambda e: e.scalar_tensor_tensor(SB["nm0r0"][:, r_ : r_ + 1], SB["rstd0"][:, r_ : r_ + 1], -1.0, ag_[:, 0:1], ALU.mult, ALU.mult))(r, L, ag),
            ["rstd0", agk], ["nm0r0"], f"nm0r{r}", late=True)
        vop((lambda b_, L_, ag_: lambda e: e.tensor_scalar(v_usr[b_][:], v_xsb[b_][:], ag_[:, 0:1], L_[:, 5:6], ALU.subtract, ALU.mult))(bi, L, ag),
            [xk, agk, "lnq" + str(bi)], [uk], f"uhat{r}")
        vop((lambda b_: lambda e: e.tensor_mul(v_usr[b_][:], v_usr[b_][:], v_wrow[:]))(bi),
            [uk, "wrow"], [uk], f"uw{r}")
        vop((lambda b_: lambda e: e.tensor_add(v_utok[b_][:], v_usr[b_][:], v_brow[:]))(bi),
            [uk, "brow"], [utk], f"ub{r}")
        dma("sync", T["u_dram"][r * 128 : (r + 1) * 128, :], v_utok[bi][:], [utk], [f"u_dram{r}"], f"stu{r}")

    u_deps = [f"u_dram{r}" for r in range(17)]
    for j in range(8):
        dma("sync", v_uf[:, j * XROWS : (j + 1) * XROWS],
            T["u_dram"][:, j * 128 : (j + 1) * 128], u_deps, [f"uf{j}"], f"uft{j}", transpose=True)
    UFK = [f"uf{j}" for j in range(8)]

    def uf(j, t0, n):
        return v_uf[:, j * XROWS + t0 : j * XROWS + t0 + n]

    dma("sync", SB["wstage"][:, 0:256].rearrange("p (j c) -> p j c", c=32),
        T["wipT"][:, 4352:4384].rearrange("(j p) c -> p j c", p=128), [], ["wstage"], "ld_wdt")
    sop(lambda e: e.activation(SB["wbufA"][:, 0:256], SB["wstage"][:, 0:256], AF.Copy), ["wstage"], ["wbufA"], "cast_wdt")
    for d in range(8):
        for t4 in range(4):
            top((lambda d_, t_: lambda e: e.matmul(
                    psA[0:NH, t_ * 512 : (t_ + 1) * 512],
                    SB["wbufA"][:, d_ * 32 : (d_ + 1) * 32],
                    uf(d_, 128 + t_ * 512, 512),
                    start=(d_ == 0), stop=(d_ == 7), skip_group_check=True))(d, t4),
                ["wbufA"] + UFK, [f"psA{t4}"], f"dtmm{d}_{t4}")
    if "dbg_dt" in dbg:
        T["dbg_dt"] = nc.dram_tensor("dbg_dt", [4 * NH, SEG], F32, kind="ExternalOutput").ap()
    sop(lambda e: e.activation(v_wF[:], psA[0:NH, 0:SEG], AF.Abs, bias=SB["dtb_sb"][:]),
        PSA + ["dtb"], ["wF"], "sp_ab")
    sop(lambda e: e.activation(v_wF[:], v_wF[:], AF.Exp, scale=-1.0), ["wF"], ["wF"], "sp_en")
    sop(lambda e: e.activation(v_wF[:], v_wF[:], AF.Ln, bias=1.0), ["wF"], ["wF"], "sp_l1")
    sop(lambda e: e.activation(v_dtF[:], psA[0:NH, 0:SEG], AF.Relu, bias=SB["dtb_sb"][:]),
        PSA + ["dtb"], ["dtF"], "sp_rl")
    vop(lambda e: e.tensor_add(v_dtF[:], v_dtF[:], v_wF[:]), ["dtF", "wF"], ["dtF"], "dtFadd")
    if "dbg_dt" in dbg:
        dma("sync", T["dbg_dt"][NH : 2 * NH], v_dtF[:], ["dtF"], ["dbgdt1"], "dbg_st1")
        dma("sync", T["dbg_dt"][2 * NH : 3 * NH], v_wF[:], ["wF"], ["dbgdt2"], "dbg_st2")
    for c in range(NCH):
        top((lambda c_: lambda e: e.transpose(psT[:, c_ * NH : (c_ + 1) * NH], v_dtF[:, c_ * Q : (c_ + 1) * Q], SB["identf"][0:NH, 0:NH]))(c),
            ["dtF", "identf"], ["psT"], f"Tdt{c}")
    vop(lambda e: e.tensor_copy(SB["dt_tok"][:], psT[:, 0:512]), ["psT"], ["dt_tok"], "Tdte")
    vop(lambda e: e.tensor_scalar_mul(v_dtF[:], v_dtF[:], SB["A_sb"][:]), ["dtF", "A_sb"], ["dtF"], "dtA")
    vop(lambda e: e.memset(v_wF[:], 1.0), ["wF"], ["wF"], "wFones", late=True)
    vop(lambda e: e.tensor_tensor_scan(v_lcF[:], v_wF[:], v_dtF[:], 0.0, ALU.mult, ALU.add),
        ["wF", "dtF"], ["lcF"], "lcscan", late=True)
    vop(lambda e: e.memset(SB["negoff"][:, 0:1], 0.0), [], ["negoff"], "no0", late=True)
    vop(lambda e: e.tensor_single_scalar(
            SB["negoff"][:, 1:NCH],
            v_lcF.rearrange("h (c q) -> h c q", q=Q)[:, 0 : NCH - 1, Q - 1 : Q].squeeze(2), -1.0, ALU.mult),
        ["lcF"], ["negoff"], "nooff")
    for c in range(NCH):
        top((lambda c_: lambda e: e.transpose(psT[:, c_ * NH : (c_ + 1) * NH], v_lcF[:, c_ * Q : (c_ + 1) * Q], SB["identf"][0:NH, 0:NH]))(c),
            ["lcF", "identf"], ["psT"], f"Tlc{c}")
    vop(lambda e: e.tensor_copy(SB["lc_tok"][:], psT[:, 0:512]), ["psT"], ["lc_tok"], "Tlce")
    if "dbg_dt" in dbg:
        dma("sync", T["dbg_dt"][3 * NH : 4 * NH], v_dtF[:], ["dtF"], ["dbgdt3"], "dbg_st3")
    dma("sync", T["lc_dram"].rearrange("(c h) q -> h c q", h=NH), v_lcF.rearrange("h (c q) -> h c q", q=Q),
        ["lcF"], ["lc_dram"], "st_lc")
    for c in range(NCH):
        sop((lambda c_: lambda e: e.activation(v_wF[:, c_ * Q : (c_ + 1) * Q], v_lcF[:, c_ * Q : (c_ + 1) * Q], AF.Exp, bias=SB["negoff"][:, c_ : c_ + 1]))(c),
            ["lcF", "negoff"], ["wF"], f"eF{c}")
        sop((lambda c_: lambda e: e.activation(SB["PF"][:, c_ : c_ + 1], v_lcF[:, (c_ + 1) * Q - 1 : (c_ + 1) * Q], AF.Exp, bias=SB["negoff"][:, c_ : c_ + 1]))(c),
            ["lcF", "negoff"], ["PF"], f"PF{c}")
    for c in range(NCH):
        top((lambda c_: lambda e: e.transpose(psT[:, c_ * NH : (c_ + 1) * NH], v_wF[:, c_ * Q : (c_ + 1) * Q], SB["identf"][0:NH, 0:NH]))(c),
            ["wF", "identf"], ["psT"], f"Te{c}")
    vop(lambda e: e.tensor_copy(SB["e_tok"][:], psT[:, 0:512]), ["psT"], ["e_tok"], "Tee")
    for c in range(NCH):
        sop((lambda c_: lambda e: e.activation(v_wF[:, c_ * Q : (c_ + 1) * Q], v_lcF[:, c_ * Q : (c_ + 1) * Q], AF.Exp, scale=-1.0, bias=v_lcF[:, (c_ + 1) * Q - 1 : (c_ + 1) * Q]))(c),
            ["lcF"], ["wF"], f"fF{c}")
    for c in range(NCH):
        top((lambda c_: lambda e: e.transpose(psT[:, c_ * NH : (c_ + 1) * NH], v_wF[:, c_ * Q : (c_ + 1) * Q], SB["identf"][0:NH, 0:NH]))(c),
            ["wF", "identf"], ["psT"], f"Tf{c}")
    vop(lambda e: e.tensor_copy(SB["dtf_tok"][:], psT[:, 0:512]), ["psT"], ["dtf_tok"], "Tfe")
    vop(lambda e: e.tensor_mul(SB["dtf_tok"][:], SB["dt_tok"][:], SB["dtf_tok"][:]),
        ["dt_tok", "dtf_tok"], ["dtf_tok"], "dtf")
    sop(lambda e: e.activation(v_wF[:], v_lcF[:], AF.Exp), ["lcF"], ["wF"], "esegF")
    for c in range(NCH):
        top((lambda c_: lambda e: e.transpose(psT[:, c_ * NH : (c_ + 1) * NH], v_wF[:, c_ * Q : (c_ + 1) * Q], SB["identf"][0:NH, 0:NH]))(c),
            ["wF", "identf"], ["psT"], f"Tes{c}")
    vop(lambda e: e.tensor_copy(SB["eseg_tok"][:], psT[:, 0:512]), ["psT"], ["eseg_tok"], "Tese")
    top(lambda e: e.transpose(psT[0:NCH, 0:NH], SB["PF"][:], SB["identf"][0:NH, 0:NH]),
        ["PF", "identf"], ["psT"], "Tpf")
    vop(lambda e: e.tensor_copy(SB["pf_t"][:], psT[0:NCH, 0:NH]), ["psT"], ["pf_t"], "Tpfe")
    dma("sync", T["pf_dram"][:], SB["pf_t"][:], ["pf_t"], ["pf_dram"], "st_pf")

    wk = [("wbufA", SB["wbufA"]), ("wbufB", SB["wbufB"])]
    stg = [("wstage", SB["wstage"]), ("wstage2", SB["wstage2"])]
    for ct in range(18):
        wkey, wbuf = wk[ct % 2]
        skey, sbuf = stg[ct % 2]
        col0 = 2048 + ct * 128
        dma("sync", sbuf[:, 0:1024],
            T["wipA3"][ct * 128 : (ct + 1) * 128, :],
            [], [skey], f"ldw{ct}")
        sop((lambda s_, w_: lambda e: e.activation(w_[:], s_[:, 0:1024], AF.Copy))(sbuf, wbuf),
            [skey], [wkey], f"castw{ct}")
        hbk = f"psB{ct % 2}"
        hb0 = (ct % 2) * 512
        for ts in range(4):
            for d in range(8):
                top((lambda d_, t_, w_, o_: lambda e: e.matmul(
                        psB[:, o_ + t_ * 4 : o_ + t_ * 4 + 3], w_[:, d_ * 128 : (d_ + 1) * 128],
                        uf(d_, 125 + t_ * 512, 3), start=(d_ == 0), stop=(d_ == 7), skip_group_check=True))(d, ts, wbuf, hb0),
                    [wkey] + UFK, [hbk], f"hmm{ct}_{ts}_{d}")
        for ts in range(4):
            pk = f"psA{ts % 2}"
            psl = psA[:, (ts % 2) * 512 : (ts % 2) * 512 + 512]
            for d in range(8):
                top((lambda d_, t_, w_, p_: lambda e: e.matmul(
                        p_, w_[:, d_ * 128 : (d_ + 1) * 128],
                        uf(d_, 128 + t_ * 512, 512),
                        start=(d_ == 0), stop=(d_ == 7)))(d, ts, wbuf, psl),
                    [wkey] + UFK, [pk], f"xmm{ct}_{ts}_{d}")
            wbk = f"work{ts % 2}"
            wbuf2 = v_work[ts % 2]
            if ts == 0:
                sop((lambda w_, o_: lambda e: e.activation(w_[:, 0:3], psB[:, o_ : o_ + 3], AF.Identity, scale=SB["hm"][:]))(wbuf2, hb0),
                    [hbk, "hm"], [wbk], f"evh{ct}_{ts}")
            else:
                sop((lambda w_, t_, o_: lambda e: e.activation(w_[:, 0:3], psB[:, o_ + t_ * 4 : o_ + t_ * 4 + 3], AF.Copy))(wbuf2, ts, hb0),
                    [hbk], [wbk], f"evh{ct}_{ts}")
            sop((lambda w_, p_: lambda e: e.activation(w_[:, 3:515], p_, AF.Copy))(wbuf2, psl),
                [pk], [wbk], f"evac{ct}_{ts}")
            cpk = f"psA{2 + ts % 2}"
            psc = psA[:, (2 + ts % 2) * 512 : (2 + ts % 2) * 512 + 512]
            for k in range(4):
                top((lambda w_, p_, ct_, k_: lambda e: e.matmul(
                        p_, v_diag[:, (ct_ * 4 + k_) * 128 : (ct_ * 4 + k_ + 1) * 128],
                        w_[:, k_ : 512 + k_], start=(k_ == 0), stop=(k_ == 3)))(wbuf2, psc, ct, k),
                    [wbk, "cdiag"], [cpk], f"cnv{ct}_{ts}_{k}")
            if ct < 16:
                so = v_silu[ts % 2]
                sok = f"silu{ts % 2}"
                sop((lambda p_, s_, ct_: lambda e: e.activation(s_[:], p_, AF.Silu, bias=SB["convb"][:, ct_ : ct_ + 1]))(psc, so, ct),
                    [cpk, "convb"], [sok], f"silu{ct}_{ts}")
                dma("scalar", T["xs_dram"][ct * 128 : (ct + 1) * 128, ts * 512 : (ts + 1) * 512],
                    so[:], [sok], [f"xs{ct}_{ts}"], f"stxs{ct}_{ts}")
            else:
                dstk = ("Bf" if ct == 16 else "Cf") + str(ts)
                dst = SB["Bf" if ct == 16 else "Cf"]
                sop((lambda p_, d_, ct_, t_: lambda e: e.activation(d_[:, t_ * 512 : (t_ + 1) * 512], p_, AF.Silu, bias=SB["convb"][:, ct_ : ct_ + 1]))(psc, dst, ct, ts),
                    [cpk, "convb"], [dstk], f"siluBC{ct}_{ts}")

    if "dbg_a3" in dbg:
        pass
    XSK = [f"xs{ct}_{ts}" for ct in range(16) for ts in range(4)]
    P.barrier("barA")

    vop(lambda e: e.memset(SB["S_run"][:], 0.0), [], ["S_run"], "S0", late=True)
    for c in range(NCH):
        bi = c % 2
        xk = f"xst{bi}"
        dma("sync", v_xst[bi][:], T["xs_dram"][:, c * 128 : (c + 1) * 128],
            [f"xs{ct_}_{c // 4}" for ct_ in range(16)], [xk], f"xsT{c}", transpose=True)
        top((lambda c_: lambda e: e.matmul(psT[:, 0:128], SB["Bf"][:, c_ * 128 : (c_ + 1) * 128],
                                           SB["Cf"][:, c_ * 128 : (c_ + 1) * 128], start=True, stop=True))(c),
            [f"Bf{c // 4}", f"Cf{c // 4}"], ["psT"], f"gmm{c}")
        vop((lambda b_: lambda e: e.tensor_tensor(SB["Gm" + str(b_)][:], psT[:, 0:128], SB["caus_sb"][:], ALU.mult))(bi),
            ["psT", "caus"], [f"Gm{bi}"], f"gm{c}")
        for hh in range(2):
            lk = f"lcb{hh}"
            h0 = hh * NH2
            mi = 2 * bi + hh
            dma("scalar", v_lcb[hh][:],
                T["lc_dram"].rearrange("(c h) q -> c (h q)", h=NH)[c : c + 1, h0 * 128 : (h0 + NH2) * 128].broadcast_to([128, NH2 * 128]),
                ["lc_dram"], [lk], f"lcbd{c}_{hh}")
            vop((lambda hh_, c_, h0_: lambda e: e.tensor_tensor(
                    v_lcb[hh_][:].rearrange("p (a b) -> p a b", b=128),
                    v_lcb[hh_][:].rearrange("p (a b) -> p a b", b=128),
                    rep3(SB["lc_tok"][:, c_ * NH + h0_ : c_ * NH + h0_ + NH2], 128),
                    ALU.subtract))(hh, c, h0),
                [lk, "lc_tok"], [lk], f"E{c}_{hh}")
            vop((lambda hh_: lambda e: e.tensor_scalar_min(v_lcb[hh_][:], v_lcb[hh_][:], 0.0))(hh),
                [lk], [lk], f"Emin{c}_{hh}")
            sop((lambda hh_, b_: lambda e: e.activation(v_L[b_][:], v_lcb[hh_][:], AF.Exp))(hh, bi),
                [lk], [f"L{bi}"], f"expL{c}_{hh}")
            vop((lambda mi_, b_: lambda e: e.tensor_tensor(
                    v_Mp[mi_][:].rearrange("p (a b) -> p a b", b=128),
                    v_L[b_][:].rearrange("p (a b) -> p a b", b=128),
                    SB["Gm" + str(b_)][:].unsqueeze(1).broadcast_to([128, NH2, 128]),
                    ALU.mult))(mi, bi),
                [f"L{bi}", f"Gm{bi}"], [f"Mp{mi}"], f"mp{c}_{hh}")
        vop((lambda b_, c_: lambda e: e.tensor_tensor(
                v_xdt[b_][:].rearrange("p (a b) -> p a b", b=HD),
                v_xst[b_][:].rearrange("p (a b) -> p a b", b=HD),
                rep3(SB["dt_tok"][:, c_ * NH : (c_ + 1) * NH], HD),
                ALU.mult))(bi, c),
            [xk, "dt_tok"], [f"xdt{bi}"], f"xdt{c}")
        vop((lambda b_, c_: lambda e: e.tensor_tensor(
                v_xf[b_][:].rearrange("p (a b) -> p a b", b=HD),
                v_xst[b_][:].rearrange("p (a b) -> p a b", b=HD),
                rep3(SB["dtf_tok"][:, c_ * NH : (c_ + 1) * NH], HD),
                ALU.mult))(bi, c),
            [xk, "dtf_tok"], [f"xf{bi}"], f"xf{c}")
        top((lambda c_: lambda e: e.transpose(psTb[:, 0:128], SB["Bf"][:, c_ * 128 : (c_ + 1) * 128], SB["identb"][:]))(c),
            [f"Bf{c // 4}", "identb"], ["psTb"], f"TB{c}")
        vop((lambda b_: lambda e: e.tensor_copy(SB["Btok" + str(b_)][:], psTb[:, 0:128]))(bi),
            ["psTb"], [f"Btok{bi}"], f"TBe{c}")
        sop(lambda e: e.activation(v_srunb[:], SB["S_run"][:], AF.Copy), ["S_run"], ["S_runb"], f"srb{c}")
        for yh in range(2):
            for j2 in range(2):
                top((lambda c_, y_, j_: lambda e: e.matmul(
                        psB[:, j_ * 512 : (j_ + 1) * 512], SB["Cf"][:, c_ * 128 : (c_ + 1) * 128],
                        v_srunb[:, y_ * 1024 + j_ * 512 : y_ * 1024 + (j_ + 1) * 512], start=True, stop=True))(c, yh, j2),
                    [f"Cf{c // 4}", "S_runb"], [f"psB{j2}"], f"yint{c}_{yh}_{j2}")
            vop((lambda b_, c_, y_: lambda e: e.tensor_tensor(
                    v_ysb[b_][:, y_ * 1024 : (y_ + 1) * 1024].rearrange("p (a b) -> p a b", b=HD),
                    psB[:, 0:1024].rearrange("p (a b) -> p a b", b=HD),
                    rep3(SB["e_tok"][:, c_ * NH + y_ * NH2 : c_ * NH + (y_ + 1) * NH2], HD),
                    ALU.mult))(bi, c, yh),
                ["psB0", "psB1", "e_tok"], [f"ysb{bi}_{yh}"], f"yie{c}_{yh}")
            for h in range(yh * NH2, (yh + 1) * NH2):
                top((lambda h_, mi_, b_, y_: lambda e: e.matmul(
                        psB[:, (h_ % NH2) * HD : (h_ % NH2 + 1) * HD], v_Mp[mi_][:, (h_ % NH2) * 128 : (h_ % NH2 + 1) * 128],
                        v_xdt[b_][:, h_ * HD : (h_ + 1) * HD], start=True, stop=True))(h, 2 * bi + yh, bi, yh),
                    [f"Mp{2 * bi + yh}", f"xdt{bi}"], [f"psB{(h % NH2) // 8}"], f"yin{c}_{h}")
            vop((lambda b_, y_: lambda e: e.tensor_add(
                    v_ysb[b_][:, y_ * 1024 : (y_ + 1) * 1024],
                    v_ysb[b_][:, y_ * 1024 : (y_ + 1) * 1024], psB[:, 0:1024]))(bi, yh),
                [f"ysb{bi}_{yh}", "psB0", "psB1"], [f"ysb{bi}_{yh}"], f"yadd{c}_{yh}")
        for j4 in range(4):
            top((lambda b_, j_: lambda e: e.matmul(
                    psA[:, j_ * 512 : (j_ + 1) * 512], SB["Btok" + str(b_)][:],
                    v_xf[b_][:, j_ * 512 : (j_ + 1) * 512], start=True, stop=True))(bi, j4),
                [f"Btok{bi}", f"xf{bi}"], [f"psA{j4}"], f"smm{c}_{j4}")
        dma("scalar", SB["Pb"][:],
            T["pf_dram"][c : c + 1, :].broadcast_to([128, NH]),
            ["pf_dram"], ["Pb"], f"pb{c}")
        vop(lambda e: e.tensor_tensor(
                SB["S_tmp"][:].rearrange("p (a b) -> p a b", b=HD),
                SB["S_run"][:].rearrange("p (a b) -> p a b", b=HD),
                rep3(SB["Pb"][:], HD), ALU.mult),
            ["S_run", "Pb"], ["S_tmp"], f"stmp{c}")
        vop(lambda e: e.tensor_add(SB["S_run"][:], psA[:, 0:2048], SB["S_tmp"][:]),
            PSA + ["S_tmp"], ["S_run"], f"supd{c}")
        vop((lambda b_: lambda e: e.tensor_tensor(
                v_tmpb[:].rearrange("p (a b) -> p a b", b=HD),
                v_xst[b_][:].rearrange("p (a b) -> p a b", b=HD),
                rep3(SB["Drep"][:], HD), ALU.mult))(bi),
            [xk, "Drep"], ["tmpb"], f"dxs{c}")
        vop((lambda b_: lambda e: e.tensor_add(v_ysb[b_][:], v_ysb[b_][:], v_tmpb[:]))(bi),
            [f"ysb{bi}_0", f"ysb{bi}_1", "tmpb"], [f"ysb{bi}_0", f"ysb{bi}_1"], f"yD{c}")
        sop((lambda b_: lambda e: e.activation(v_ylb[:], v_ysb[b_][:], AF.Copy))(bi),
            [f"ysb{bi}_0", f"ysb{bi}_1"], ["ylb"], f"ycast{c}")
        dma("gpsimd", T["yl_dram"][c * 128 : (c + 1) * 128, :], v_ylb[:], ["ylb"], [f"yl{c}"], f"styl{c}")

    vop(lambda e: e.memset(SB["zrow"][:], 0.0), [], ["zrow"], "zrow", late=True)
    dma("gpsimd", T["cc_in"][128:129, :], SB["zrow"][:], ["zrow"], ["cc_in"], "st_zrow")
    dma("gpsimd", T["cc_in"][0:128, :], SB["S_run"][:], ["S_run"], ["cc_in"], "st_sseg")
    dma("gpsimd", T["cc_in"][128:129, 0:NH], SB["lc_tok"][127:128, 15 * NH : 16 * NH], ["lc_tok"], ["cc_in"], "st_lp")
    P.op("gpsimd",
         lambda e: e.collective_compute(
             "AllGather", ALU.bypass, replica_groups=[list(range(CORES))],
             ins=[T["cc_in"].opt()], outs=[T["cc_out"].opt()]),
         reads=["cc_in"], writes=["cc_out"], tag="allgather")
    CDCLR = ["xst0", "xst1", "xdt0", "xdt1", "xf0", "xf1", "ysb0_0", "ysb0_1"]
    for d in range(8):
        for half in range(2):
            skey, sbuf = stg[half]
            dma("sync", sbuf[:, 0:1024],
                T["wipT"][:, 0:2048][d * 128 : (d + 1) * 128, half * 1024 : (half + 1) * 1024],
                [], [skey], f"ldwz{d}_{half}")
            sop((lambda d_, h_, s_: lambda e: e.activation(v_wz[:, d_ * 2048 + h_ * 1024 : d_ * 2048 + (h_ + 1) * 1024], s_[:, 0:1024], AF.Copy))(d, half, sbuf),
                [skey], ["wz", CDCLR[d]], f"castwz{d}_{half}")
    for j in range(8):
        dma("gpsimd", SB["lp_sb"][j : j + 1, :], T["cc_out"][j * 129 + 128 : j * 129 + 129, 0:NH],
            ["cc_out"], ["lp_sb"], f"ldlp{j}")
    top(lambda e: e.matmul(psB[0:8, 0:NH], SB["m2t_sb"][:], SB["lp_sb"][:], start=True, stop=True),
        ["m2t", "lp_sb"], ["psB0"], "alphamm")
    sop(lambda e: e.activation(SB["alpha"][:], psB[0:8, 0:NH], AF.Exp), ["psB0"], ["alpha"], "alphaexp")
    vop(lambda e: e.tensor_scalar_mul(SB["alpha"][:], SB["alpha"][:], SB["sel_sb"][:]),
        ["alpha", "sel"], ["alpha"], "alphasel", late=True)
    dma("gpsimd", T["al_dram"][:], SB["alpha"][:], ["alpha"], ["al_dram"], "st_alpha")
    vop(lambda e: e.memset(SB["S_run"][:], 0.0), ["cc_in"], ["S_run"], "sacc0", late=True)
    for j in range(8):
        dma("gpsimd", SB["S_j"][:], T["cc_out"][j * 129 : j * 129 + 128, :], ["cc_out"], ["S_j"], f"ldsj{j}")
        dma("scalar", SB["Pb"][:], T["al_dram"][j : j + 1, :].broadcast_to([128, NH]), ["al_dram"], ["Pb"], f"ldal{j}")
        vop(lambda e: e.tensor_tensor(
                SB["S_tmp"][:].rearrange("p (a b) -> p a b", b=HD),
                SB["S_j"][:].rearrange("p (a b) -> p a b", b=HD),
                rep3(SB["Pb"][:], HD), ALU.mult),
            ["S_j", "Pb"], ["S_tmp"], f"sjw{j}")
        vop(lambda e: e.tensor_add(SB["S_run"][:], SB["S_run"][:], SB["S_tmp"][:]),
            ["S_run", "S_tmp"], ["S_run"], f"sacc{j}")
    vop(lambda e: e.tensor_copy(v_saccb[:], SB["S_run"][:]), ["S_run"], ["S_accb", "L1"], "saccb")

    for c in range(NCH):
        bi = c % 2
        szk = f"sz{bi}"
        ylk = f"yl2{bi}"
        y1k, g0k = f"y1{bi}", f"g0{bi}"
        y1b, g0b = v_y1[bi], v_g0[bi]
        st1 = "stat1" if bi == 0 else "stat3"
        st2 = "stat2" if bi == 0 else "stat4"
        s1, s2 = SB[st1], SB[st2]
        for d in range(8):
            for j4 in range(4):
                top((lambda c_, d_, j_: lambda e: e.matmul(
                        psA[:, j_ * 512 : (j_ + 1) * 512],
                        uf(d_, 128 + c_ * 128, 128),
                        v_wz[:, d_ * 2048 + j_ * 512 : d_ * 2048 + (j_ + 1) * 512],
                        start=(d_ == 0), stop=(d_ == 7), skip_group_check=True))(c, d, j4),
                    ["wz"] + UFK, [f"psA{j4}"], f"zmm{c}_{d}_{j4}")
        sop((lambda b_, c_: lambda e: e.activation(v_sz[b_][:], psA[:, 0:2048], AF.Silu))(bi, c),
            PSA, [szk, f"Mp{bi}"], f"zsilu{c}")
        dma("gpsimd", v_yl2[bi][:], T["yl_dram"][c * 128 : (c + 1) * 128, :],
            [f"yl{c}"], [ylk, f"ysb1_{bi}"], f"ldyl{c}")
        yal = ["tmpb", "ylb"] if bi == 0 else ["Mp2", "Mp3"]
        for h in range(2):
            for j2 in range(2):
                top((lambda c_, h_, j_: lambda e: e.matmul(
                        psB[:, j_ * 512 : (j_ + 1) * 512], SB["Cf"][:, c_ * 128 : (c_ + 1) * 128],
                        v_saccb[:, h_ * 1024 + j_ * 512 : h_ * 1024 + (j_ + 1) * 512],
                        start=True, stop=True))(c, h, j2),
                    [f"Cf{c // 4}", "S_accb"], [f"psB{j2}"], f"corr{c}_{h}_{j2}")
            vop((lambda c_, h_, y_: lambda e: e.tensor_tensor(
                    y_[:, h_ * 1024 : (h_ + 1) * 1024].rearrange("p (a b) -> p a b", b=HD),
                    psB[:, 0:1024].rearrange("p (a b) -> p a b", b=HD),
                    rep3(SB["eseg_tok"][:, c_ * NH + h_ * NH2 : c_ * NH + (h_ + 1) * NH2], HD),
                    ALU.mult))(c, h, y1b),
                ["psB0", "psB1", "eseg_tok"], [y1k] + yal, f"ycorr{c}_{h}")
        vop((lambda b_, y_: lambda e: e.tensor_add(y_[:], y_[:], v_yl2[b_][:]))(bi, y1b),
            [y1k, ylk], [y1k], f"yfull{c}")
        vop((lambda b_, y_, g_: lambda e: e.tensor_tensor(g_[:], y_[:], v_sz[b_][:], ALU.mult))(bi, y1b, g0b),
            [y1k, szk], [g0k, "L0" if bi == 0 else "lcb0"], f"gate{c}")
        sop((lambda y_, g_, s_: lambda e: e.activation(y_[:], g_[:], AF.Square, accum_out=s_[:]))(y1b, g0b, s1),
            [g0k], [st1, y1k], f"rmssq{c}", late=True)
        sop((lambda s1_, s2_: lambda e: e.activation(s2_[:], s1_[:], AF.Ln, scale=1.0 / DI, bias=EPS))(s1, s2),
            [st1], [st2], f"rmsln{c}")
        sop((lambda s2_: lambda e: e.activation(s2_[:], s2_[:], AF.Exp, scale=-0.5))(s2),
            [st2], [st2], f"rmsexp{c}")
        sop((lambda g_, s2_: lambda e: e.activation(g_[:], g_[:], AF.Identity, scale=s2_[:]))(g0b, s2),
            [g0k, st2], [g0k], f"rmsapp{c}")
        dma("gpsimd", T["g_dram"][c * 128 : (c + 1) * 128, :], g0b[:], [g0k], [f"g{c}"], f"stg{c}")

    GK = [f"g{c}" for c in range(NCH)]
    P.barrier("barPOST")

    for j in range(16):
        skey, sbuf = stg[j % 2]
        dma("sync", sbuf[:, 0:1024], T["wopT"][j * 128 : (j + 1) * 128, :], [], [skey], f"ldwo{j}")
        sop((lambda j_, s_: lambda e: e.activation(v_wo[:, j_ * 1024 : (j_ + 1) * 1024], s_[:, 0:1024], AF.Identity, scale=SB["gamma_sb"][:, j_ : j_ + 1]))(j, sbuf),
            [skey, "gamma"], ["wo"], f"castwo{j}")
    dma("sync", v_W1r[:], T["W1"].rearrange("a b -> (a b)").unsqueeze(0).broadcast_to([128, 1024]), [], ["W1r"], "ld_W1")
    dma("sync", v_b1r[:], T["b1"].rearrange("a b -> (a b)").unsqueeze(0).broadcast_to([128, 1024]), [], ["b1r"], "ld_b1")
    dma("sync", v_W2r[:], T["W2"].rearrange("a b -> (a b)").unsqueeze(0).broadcast_to([128, 1024]), [], ["W2r"], "ld_W2")

    GFK = [f"gfh{j}" for j in range(16)]

    def plain_ln(src_ap, src_keys, dst_ap, dst_key, tag, co, bi):
        Lx = SB["lnsc"]
        bsk, agk = f"bs{bi}", f"ag{bi}"
        bs, ag = SB[bsk], SB[agk]
        vop((lambda s_, bs_: lambda e: e.bn_stats(bs_[:, 0:6], s_[:, 0:512]))(src_ap, bs),
            src_keys, [bsk], f"{tag}bna", late=True)
        vop((lambda s_, bs_: lambda e: e.bn_stats(bs_[:, 6:12], s_[:, 512:1024]))(src_ap, bs),
            src_keys, [bsk], f"{tag}bnb", late=True)
        vop((lambda bs_, ag_: lambda e: e.bn_aggr(ag_[:], bs_[:]))(bs, ag),
            [bsk], [agk], f"{tag}bng", late=True)
        sop((lambda c_, ag_: lambda e: e.activation(Lx[:, c_ + 3 : c_ + 4], ag_[:, 1:2], AF.Ln, bias=EPS))(co, ag),
            [agk], [f"{tag}lnv"], f"{tag}lnv")
        sop((lambda c_: lambda e: e.activation(Lx[:, c_ + 4 : c_ + 5], Lx[:, c_ + 3 : c_ + 4], AF.Exp, scale=-0.5))(co),
            [f"{tag}lnv"], [f"{tag}r"], f"{tag}r")
        vop((lambda s_, d_, c_, ag_: lambda e: e.tensor_scalar(d_[:], s_[:], ag_[:, 0:1], Lx[:, c_ + 4 : c_ + 5], ALU.subtract, ALU.mult))(src_ap, dst_ap, co, ag),
            src_keys + [f"{tag}r", agk], [dst_key], f"{tag}app", late=True)

    for th in range(2):
        for j in range(16):
            dma("sync", v_gfh[j][:],
                T["g_dram"][th * 1024 : (th + 1) * 1024, j * 128 : (j + 1) * 128],
                [f"g{c_}" for c_ in range(th * 8, th * 8 + 8)], [f"gfh{j}"], f"gfT{th}_{j}", transpose=True)
        for r in range(th * 8, th * 8 + 8):
            bi = r % 2
            rr = r - th * 8
            xk = f"xfin{bi}"
            ok = f"vout{bi}"
            co = bi * 8
            st1 = "stat1" if bi == 0 else "stat3"
            st2 = "stat2" if bi == 0 else "stat4"
            dma("sync", v_xfin[bi][:], T["x"][128 + r * 128 : 128 + (r + 1) * 128, :], [], [xk], f"ldx2_{r}")
            for j in range(16):
                for d2 in range(2):
                    top((lambda rr_, j_, d_: lambda e: e.matmul(
                            psA[:, d_ * 512 : (d_ + 1) * 512], v_gfh[j_][:, rr_ * 128 : (rr_ + 1) * 128],
                            v_wo[:, j_ * 1024 + d_ * 512 : j_ * 1024 + (d_ + 1) * 512],
                            start=(j_ == 0), stop=(j_ == 15), skip_group_check=True))(rr, j, d2),
                        GFK + ["wo"], [f"psA{d2}"], f"opmm{r}_{j}_{d2}")
            sop((lambda b_, r_: lambda e: e.activation(v_moh[b_][:], v_xfin[b_][:], AF.Identity, scale=SB["rstd0"][:, r_ + 1 : r_ + 2], bias=SB["nm0r0"][:, r_ + 1 : r_ + 2]))(bi, r),
                [xk, "rstd0", "nm0r0"], [f"moh{bi}"], f"u0_{r}")
            vop((lambda b_: lambda e: e.tensor_add(v_moh[b_][:], v_moh[b_][:], psA[:, 0:1024]))(bi),
                [f"moh{bi}", "psA0", "psA1"], [f"moh{bi}"], f"moadd{r}")
            plain_ln(v_moh[bi], [f"moh{bi}"], v_moh[bi], f"moh{bi}", f"ln2_{r}_", co, bi)
            vop((lambda b_: lambda e: e.tensor_add(v_x2[b_][:], v_xfin[b_][:], v_moh[b_][:]))(bi),
                [xk, f"moh{bi}"], [f"x2{bi}"], f"x2_{r}")
            plain_ln(v_x2[bi], [f"x2{bi}"], v_u3[bi], f"u3{bi}", f"ln3_{r}_", co, bi)
            for j in range(8):
                top((lambda j_, b_: lambda e: e.transpose(psTb[:, j_ * 128 : (j_ + 1) * 128], v_u3[b_][:, j_ * 128 : (j_ + 1) * 128], SB["identb"][:]))(j, bi),
                    [f"u3{bi}", "identb"], ["psTb"], f"Tu3_{r}_{j}")
            sop((lambda b_: lambda e: e.activation(v_u3f[b_][:], psTb[:], AF.Copy))(bi),
                ["psTb"], [f"u3f{bi}"], f"Tu3e_{r}", late=True)
            for d in range(8):
                top((lambda d_, b_: lambda e: e.matmul(psB[:, 0:64], v_u3f[b_][:, d_ * 128 : (d_ + 1) * 128],
                                                       SB["Win_bf"][:, d_ * 64 : (d_ + 1) * 64], start=(d_ == 0), stop=False))(d, bi),
                    [f"u3f{bi}", "win_bf"], ["psB0"], f"ffs{r}_{d}")
            top(lambda e: e.matmul(psB[:, 0:64], SB["onesK"][:], SB["binrow"][:], start=False, stop=True),
                ["onesK", "binrow"], ["psB0"], f"ffsb{r}")
            vop((lambda b_: lambda e: e.tensor_tensor(
                    v_moh[b_][:].rearrange("p (a b) -> p a b", b=16),
                    psB[:, 0:64].unsqueeze(2).broadcast_to([128, 64, 16]),
                    v_W1r[:].rearrange("p (a b) -> p a b", b=16), ALU.mult))(bi),
                ["psB0", "W1r"], [f"moh{bi}"], f"h1_{r}")
            vop((lambda b_: lambda e: e.tensor_add(v_moh[b_][:], v_moh[b_][:], v_b1r[:]))(bi),
                [f"moh{bi}", "b1r"], [f"moh{bi}"], f"h2_{r}")
            sop((lambda b_: lambda e: e.activation(v_hb[b_][:], v_moh[b_][:], AF.Relu))(bi),
                [f"moh{bi}"], [f"hb{bi}"], f"h3_{r}")
            vop((lambda b_: lambda e: e.tensor_tensor(v_moh[b_][:], v_hb[b_][:], v_W2r[:], ALU.mult))(bi),
                [f"hb{bi}", "W2r"], [f"moh{bi}"], f"hw_{r}")
            vop((lambda b_: lambda e: e.tensor_reduce(v_g2[b_][:], v_moh[b_][:].rearrange("p (a b) -> p a b", b=16), AXX, ALU.add))(bi),
                [f"moh{bi}"], [f"g2{bi}"], f"g2_{r}", late=True)
            vop((lambda b_: lambda e: e.tensor_add(v_g2[b_][:], v_g2[b_][:], SB["b2row"][:]))(bi),
                [f"g2{bi}", "b2r"], [f"g2{bi}"], f"g2b_{r}")
            sop((lambda b_: lambda e: e.activation(v_g2b[b_][:], v_g2[b_][:], AF.Copy))(bi),
                [f"g2{bi}"], [f"g2b{bi}"], f"g2c_{r}", late=True)
            top((lambda b_: lambda e: e.transpose(psTb[0:64, 0:128], v_g2b[b_][:], SB["identb"][:]))(bi),
                [f"g2b{bi}", "identb"], ["psTb"], f"Tg2_{r}")
            sop((lambda b_: lambda e: e.activation(v_u3f[b_][0:64, 0:128], psTb[0:64, 0:128], AF.Copy))(bi),
                ["psTb"], [f"u3f{bi}"], f"Tg2e_{r}")
            for d2 in range(2):
                top((lambda d_, b_: lambda e: e.matmul(psB[:, d_ * 512 : (d_ + 1) * 512], v_u3f[b_][0:64, 0:128],
                                                       SB["Wout2_bf"][:, d_ * 512 : (d_ + 1) * 512], start=True, stop=False, skip_group_check=True))(d2, bi),
                    [f"u3f{bi}", "wo2_bf"], [f"psB{d2}"], f"fo{r}_{d2}")
            for d2 in range(2):
                top((lambda d_: lambda e: e.matmul(psB[:, d_ * 512 : (d_ + 1) * 512], SB["onesK"][:],
                                                   SB["boutrow"][:, d_ * 512 : (d_ + 1) * 512], start=False, stop=True, skip_group_check=True))(d2),
                    ["onesK", "boutrow"], [f"psB{d2}"], f"fob{r}_{d2}")
            sop((lambda b_: lambda e: e.activation(v_moh[b_][:], psB[:, 0:1024], AF.Copy))(bi),
                ["psB0", "psB1"], [f"moh{bi}"], f"foevac{r}")
            plain_ln(v_moh[bi], [f"moh{bi}"], v_moh[bi], f"moh{bi}", f"ln4_{r}_", co, bi)
            vop((lambda b_: lambda e: e.tensor_add(v_o[b_][:], v_x2[b_][:], v_moh[b_][:]))(bi),
                [f"x2{bi}", f"moh{bi}"], [ok], f"out_{r}")
            dma("sync", T["out"][r * 128 : (r + 1) * 128, :], v_o[bi][:], [ok], [f"outd{r}"], f"stout{r}")

    P.barrier("barEND")

    P.resolve()
    global LAST_PLAN
    LAST_PLAN = P
    from contextlib import ExitStack
    es = ExitStack()
    semmap = {}
    for e in ENGINES:
        semmap["prog_" + e] = es.enter_context(nc.semaphore("prog_" + e))
    for sk in P.dma_slot_sems:
        semmap[sk] = es.enter_context(nc.semaphore(sk))
    block = es.enter_context(nc.Block())

    SEQ = [P.ops[i] for i in P.order]

    def emit_engine(ename):
        def run(eng_obj):
            for o in SEQ:
                if o.engine != ename:
                    continue
                for sk, v in o.waits:
                    eng_obj.wait_ge(semmap[sk], v)
                inst = o.fn(eng_obj)
                for sk, v in o.incs:
                    if inst is None:
                        raise RuntimeError(f"op {o.tag} has incs but returned no instruction")
                    inst = inst.then_inc(semmap[sk], v)
        return run

    block.tensor(emit_engine("tensor"))
    block.vector(emit_engine("vector"))
    block.scalar(emit_engine("scalar"))
    block.sync(emit_engine("sync"))
    block.gpsimd(emit_engine("gpsimd"))
    es.close()
    return nc


def make_in_maps(inputs):
    f32 = np.float32
    x = np.asarray(inputs["L0_attn_x"], f32).reshape(16384, DM)
    xg = np.concatenate([np.zeros((128, DM), f32), x], axis=0)
    wipT = np.ascontiguousarray(np.asarray(inputs["L0_attn_in_proj_w"], f32).T)
    wipA3 = np.ascontiguousarray(
        np.asarray(inputs["L0_attn_in_proj_w"], f32)[2048:4352]
        .reshape(18, 128, 8, 128)
        .transpose(0, 3, 2, 1)
        .reshape(18 * 128, 1024))
    wopT = np.ascontiguousarray(np.asarray(inputs["L0_attn_out_proj_w"], f32).T)
    ident_bf = np.eye(128, dtype=ml_dtypes.bfloat16)
    ident_f32 = np.eye(128, dtype=f32)
    caus = np.triu(np.ones((128, 128), f32))
    in_maps = []
    for i in range(CORES):
        sel = (np.arange(8)[:, None] < i).astype(f32)
        m2t = np.zeros((8, 8), f32)
        for k in range(8):
            for j in range(8):
                m2t[k, j] = 1.0 if (j < k < i) else 0.0
        in_maps.append({
            "x": np.ascontiguousarray(xg[i * SEG : i * SEG + XROWS]),
            "wipT": wipT,
            "wipA3": wipA3,
            "wopT": wopT,
            "conv_w": np.asarray(inputs["L0_attn_conv_w"], f32),
            "conv_wT": np.ascontiguousarray(np.asarray(inputs["L0_attn_conv_w"], f32).T),
            "conv_b": np.asarray(inputs["L0_attn_conv_b"], f32),
            "A_log": np.asarray(inputs["L0_attn_A_log"], f32),
            "Dv": np.asarray(inputs["L0_attn_D"], f32),
            "dt_bias": np.asarray(inputs["L0_attn_dt_bias"], f32),
            "gamma": np.asarray(inputs["L0_attn_gamma"], f32),
            "ln_w": np.asarray(inputs["L0_attn_ln_w"], f32),
            "ln_b": np.asarray(inputs["L0_attn_ln_b"], f32),
            "Win": np.asarray(inputs["L0_ffn_Win"], f32),
            "bin": np.asarray(inputs["L0_ffn_bin_"], f32),
            "W1": np.asarray(inputs["L0_ffn_W1"], f32),
            "b1": np.asarray(inputs["L0_ffn_b1"], f32),
            "W2": np.asarray(inputs["L0_ffn_W2"], f32),
            "b2": np.asarray(inputs["L0_ffn_b2"], f32),
            "Wout2": np.asarray(inputs["L0_ffn_Wout"], f32),
            "bout": np.asarray(inputs["L0_ffn_bout"], f32),
            "ident_bf": ident_bf,
            "ident_f32": ident_f32,
            "caus": caus,
            "sel": sel,
            "m2t": m2t,
            "hmask": np.full((1, 1), 0.0 if i == 0 else 1.0, f32),
        })
    return in_maps


def kernel(**inputs):
    nc = build_program()
    in_maps = make_in_maps(inputs)
    res = run_bass_kernel_spmd(nc, in_maps, core_ids=list(range(CORES)))
    out = np.concatenate([r["out"] for r in res.results], axis=0)
    return out[None]


# version 17
# speedup vs baseline: 1.0597x; 1.0028x over previous
import numpy as np
import ml_dtypes
import concourse.bass as bass
import concourse.mybir as mybir
from concourse.bass_utils import run_bass_kernel_spmd

F32 = mybir.dt.float32
BF16 = mybir.dt.bfloat16
U8 = mybir.dt.uint8
AF = mybir.ActivationFunctionType
ALU = mybir.AluOpType
AXX = mybir.AxisListType.X

EPS = 1e-5
DM = 1024
DI = 2048
DS = 128
NH = 32
HD = 64
CDIM = 2304
SEG = 2048
Q = 128
NCH = SEG // Q
NT = 16
XROWS = SEG + 128
CORES = 8
NH2 = 16

ENGINES = ("tensor", "vector", "scalar", "sync", "gpsimd")


class Op:
    __slots__ = ("idx", "engine", "fn", "deps", "is_dma", "tag", "waits", "incs", "flagged", "late", "dma_idx")

    def __init__(self, idx, engine, fn, deps, is_dma, tag):
        self.idx = idx
        self.engine = engine
        self.fn = fn
        self.deps = deps
        self.is_dma = is_dma
        self.tag = tag
        self.waits = []
        self.incs = []
        self.flagged = False
        self.late = False
        self.dma_idx = -1


class Plan:
    NSLOT = 10

    def __init__(self):
        self.ops = []
        self.last_write = {}
        self.readers = {}
        self.dma_counts = {}

    def op(self, engine, fn, reads=(), writes=(), dma=False, tag="", after=(), late=False):
        deps = set()
        for k in reads:
            if k in self.last_write:
                deps.add(self.last_write[k])
        for k in writes:
            if k in self.last_write:
                deps.add(self.last_write[k])
            for r in self.readers.get(k, ()):
                deps.add(r)
        for o in after:
            deps.add(o.idx)
        o = Op(len(self.ops), engine, fn, deps, dma, tag)
        o.late = late
        if dma:
            o.dma_idx = self.dma_counts.get(engine, 0)
            self.dma_counts[engine] = o.dma_idx + 1
        self.ops.append(o)
        for k in reads:
            self.readers.setdefault(k, []).append(o.idx)
        for k in writes:
            self.last_write[k] = o.idx
            self.readers[k] = []
        return o

    def barrier(self, tag="bar"):
        lasts = {}
        last_dma = {}
        for o in self.ops:
            lasts[o.engine] = o
            if o.is_dma:
                last_dma[(o.engine, o.dma_idx % self.NSLOT)] = o
        deps = list(lasts.values()) + list(last_dma.values())
        for e in ENGINES:
            self.op(e, _mknop(e), after=deps, tag=tag + "_" + e)


    def schedule(self):
        n = len(self.ops)
        dur = [0.0] * n
        for o in self.ops:
            t = o.tag
            if o.is_dma:
                dur[o.idx] = 4.0 if t.startswith(("xsT", "gfT", "uft")) else 2.5
            elif o.engine == "tensor":
                if t.startswith(("xmm", "zmm", "opmm", "dtmm", "yint", "smm", "corr")):
                    dur[o.idx] = 0.25
                elif t.startswith("cnv"):
                    dur[o.idx] = 0.5
                elif t.startswith("hmm"):
                    dur[o.idx] = 0.12
                elif t.startswith("yin"):
                    dur[o.idx] = 0.12
                else:
                    dur[o.idx] = 0.35
            elif o.engine == "vector":
                if t.startswith(("E", "yie", "yadd", "supd", "stmp", "ycorr", "yfull", "sjw", "sacc")):
                    dur[o.idx] = 2.2
                elif t.startswith(("xdt", "xf", "mp", "gate", "dxs", "yD", "ycast", "srb", "Emin")):
                    dur[o.idx] = 1.2
                else:
                    dur[o.idx] = 0.5
            elif o.engine == "scalar":
                if t.startswith(("expL", "nrelu")):
                    dur[o.idx] = 3.6
                elif t.startswith(("zsilu", "ycast", "srb")):
                    dur[o.idx] = 1.8
                elif t.startswith(("silu", "castw", "uhat", "sq", "sp_", "evac", "evh", "foevac", "Tu3e", "g2c", "Tg2e")):
                    dur[o.idx] = 1.0
                else:
                    dur[o.idx] = 0.3
            elif t == "allgather":
                dur[o.idx] = 60.0
            else:
                dur[o.idx] = 0.3
        succ = [[] for _ in range(n)]
        for o in self.ops:
            for d in o.deps:
                succ[d].append(o.idx)
        prio = [0.0] * n
        for i in range(n - 1, -1, -1):
            m = 0.0
            for s in succ[i]:
                if prio[s] > m:
                    m = prio[s]
            prio[i] = dur[i] + m
        regions = []
        i = 0
        ops = self.ops
        while i < n:
            isbar = ops[i].tag.startswith("bar")
            j = i
            while j < n and ops[j].tag.startswith("bar") == isbar:
                j += 1
            regions.append(("bar" if isbar else "work", list(range(i, j))))
            i = j
        import heapq
        order = []
        fin = [0.0] * n
        t_now = 0.0
        done = [False] * n
        for kind, idxs in regions:
            if kind == "bar":
                for i2 in idxs:
                    order.append(i2)
                    fin[i2] = t_now
                    done[i2] = True
                continue
            remaining = set(idxs)
            indeg = {}
            ready = []
            for i2 in idxs:
                k = sum(1 for d in self.ops[i2].deps if d in remaining)
                indeg[i2] = k
                if k == 0:
                    heapq.heappush(ready, (-prio[i2], i2))
            eng_free = {e: t_now for e in ENGINES}
            while remaining:
                progressed = False
                for e in ENGINES:
                    if not ready:
                        break
                    best = None
                    tmp = []
                    while ready:
                        p, i2 = heapq.heappop(ready)
                        if self.ops[i2].engine == e and best is None:
                            best = (p, i2)
                        else:
                            tmp.append((p, i2))
                    for x in tmp:
                        heapq.heappush(ready, x)
                    if best is None:
                        continue
                    p, i2 = best
                    dep_fin = t_now
                    for d in self.ops[i2].deps:
                        if fin[d] > dep_fin:
                            dep_fin = fin[d]
                    st = max(eng_free[e], dep_fin)
                    if self.ops[i2].is_dma:
                        eng_free[e] = st + 0.08
                        fin[i2] = st + dur[i2]
                    else:
                        eng_free[e] = st + dur[i2]
                        fin[i2] = eng_free[e]
                    done[i2] = True
                    order.append(i2)
                    remaining.discard(i2)
                    progressed = True
                    for s2 in succ[i2]:
                        if s2 in indeg:
                            indeg[s2] -= 1
                            if indeg[s2] == 0:
                                heapq.heappush(ready, (-prio[s2], s2))
                if not progressed:
                    i2 = min(remaining)
                    if indeg.get(i2, 0) != 0:
                        raise RuntimeError("scheduler stuck")
                    order.append(i2)
                    done[i2] = True
                    remaining.discard(i2)
            t_now = max(fin[i2] for i2 in idxs) if idxs else t_now
        assert len(order) == n
        self.order = order
        self.est_span = t_now

    def resolve(self):
        self.schedule()
        seq = [self.ops[i] for i in self.order]
        lasts = {}
        last_dma_all = []
        for o in seq:
            if o.tag.startswith("bar"):
                o.deps = set(x.idx for x in lasts.values()) | set(x.idx for x in last_dma_all[-80:])
            else:
                lasts[o.engine] = o
                if o.is_dma:
                    last_dma_all.append(o)
        for o in self.ops:
            for d in o.deps:
                do = self.ops[d]
                if not do.is_dma:
                    do.flagged = True
        comp_count = {e: 0 for e in ENGINES}
        slot_count = {}
        events = [None] * len(self.ops)
        pre_waits = [None] * len(self.ops)
        dma_seq = {}
        for o in seq:
            if o.is_dma:
                o.dma_idx = dma_seq.get(o.engine, 0)
                dma_seq[o.engine] = o.dma_idx + 1
        for o in seq:
            if o.is_dma:
                slot = o.dma_idx % self.NSLOT
                sk = f"dmaq_{o.engine}{slot}"
                n = slot_count.get(sk, 0)
                pre_waits[o.idx] = (sk, n * 16)
                slot_count[sk] = n + 1
                o.incs.append((sk, 16))
                events[o.idx] = (sk, (n + 1) * 16)
            elif o.flagged:
                comp_count[o.engine] += 1
                o.incs.append(("prog_" + o.engine, 1))
                events[o.idx] = ("prog_" + o.engine, comp_count[o.engine])
            else:
                events[o.idx] = ("prog_" + o.engine, 0)
        water = {e: {} for e in ENGINES}
        self.dma_slot_sems = sorted(slot_count)
        self.dma_slot_finals = dict(slot_count)
        for o in seq:
            need = {}
            if pre_waits[o.idx] is not None:
                sk, v = pre_waits[o.idx]
                if v > 0:
                    need[sk] = v
            for d in sorted(o.deps):
                do = self.ops[d]
                sk, v = events[d]
                if v <= 0:
                    continue
                need[sk] = max(need.get(sk, 0), v)
            for sk, v in sorted(need.items()):
                if water[o.engine].get(sk, 0) >= v:
                    continue
                o.waits.append((sk, v))
                water[o.engine][sk] = v


def _mknop(ename):
    def f(eng):
        if hasattr(eng, "engine_nop"):
            return eng.engine_nop()
        return eng.nop()
    return f


def rep3(ap_small, reps):
    p, n = ap_small.shape
    return ap_small.unsqueeze(2).broadcast_to([p, n, reps])


def build_program(debug=()):
    nc = bass.Bass("TRN2", target_bir_lowering=False, debug=False)
    dbg = set(debug)

    def okind(name):
        return "ExternalOutput" if name in dbg else "Internal"

    T = {}
    for name, shape, dt_, kind in [
        ("x", [XROWS, DM], F32, "ExternalInput"),
        ("wipT", [DM, 4384], F32, "ExternalInput"),
        ("wipA3", [18 * 128, 1024], F32, "ExternalInput"),
        ("wopT", [DI, DM], F32, "ExternalInput"),
        ("conv_w", [CDIM, 4], F32, "ExternalInput"),
        ("conv_wT", [4, CDIM], F32, "ExternalInput"),
        ("conv_b", [CDIM], F32, "ExternalInput"),
        ("A_log", [NH], F32, "ExternalInput"),
        ("Dv", [NH], F32, "ExternalInput"),
        ("dt_bias", [NH], F32, "ExternalInput"),
        ("gamma", [DI], F32, "ExternalInput"),
        ("ln_w", [DM], F32, "ExternalInput"),
        ("ln_b", [DM], F32, "ExternalInput"),
        ("Win", [DM, 64], F32, "ExternalInput"),
        ("bin", [64], F32, "ExternalInput"),
        ("W1", [64, 16], F32, "ExternalInput"),
        ("b1", [64, 16], F32, "ExternalInput"),
        ("W2", [64, 16], F32, "ExternalInput"),
        ("b2", [64], F32, "ExternalInput"),
        ("Wout2", [64, DM], F32, "ExternalInput"),
        ("bout", [DM], F32, "ExternalInput"),
        ("ident_bf", [128, 128], BF16, "ExternalInput"),
        ("ident_f32", [128, 128], F32, "ExternalInput"),
        ("caus", [128, 128], F32, "ExternalInput"),
        ("sel", [8, 1], F32, "ExternalInput"),
        ("m2t", [8, 8], F32, "ExternalInput"),
        ("hmask", [1, 1], F32, "ExternalInput"),
        ("out", [SEG, DM], F32, "ExternalOutput"),
    ]:
        T[name] = nc.dram_tensor(name, shape, dt_, kind=kind).ap()
    T["cc_out"] = nc.dram_tensor("cc_out", [8 * 129, DI], F32, kind=okind("cc_out"), addr_space="Shared").ap()
    for name, shape, dt_ in [
        ("u_dram", [XROWS, DM], BF16),
        ("xs_dram", [DI, SEG], BF16),
        ("yl_dram", [SEG, DI], BF16),
        ("sz_dram", [SEG, DI], BF16),
        ("g_dram", [SEG, DI], BF16),
        ("lc_dram", [NCH * NH, 128], F32),
        ("pf_dram", [NCH, NH], F32),
        ("al_dram", [8, NH], F32),
        ("cc_in", [129, DI], F32),

    ]:
        T[name] = nc.dram_tensor(name, shape, dt_, kind=okind(name)).ap()

    if "dbg_a3" in dbg:
        T["dbg_a3"] = nc.dram_tensor("dbg_a3", [1024, 516], F32, kind="ExternalOutput").ap()

    SB = {}

    def sball(name, shape, dtype):
        SB[name] = nc.alloc_sbuf_tensor(name, shape, dtype).ap()
        return SB[name]

    sball("cpool", [128, 6], F32)
    sball("ones_c", [NH, 1], F32)
    sball("zcol", [NH, 1], F32)
    sball("A_sb", [NH, 1], F32)
    sball("dtb_sb", [NH, 1], F32)
    sball("negoff", [NH, NCH], F32)
    sball("PF", [NH, NCH], F32)
    sball("dt_tok", [128, NT * NH], F32)
    sball("dtf_tok", [128, NT * NH], F32)
    sball("e_tok", [128, NT * NH], F32)
    sball("eseg_tok", [128, NT * NH], F32)
    sball("lc_tok", [128, NT * NH], F32)
    sball("Bf", [128, SEG], BF16)
    sball("Cf", [128, SEG], BF16)
    sball("caus_sb", [128, 128], F32)
    sball("identb", [128, 128], BF16)
    sball("identf", [128, 128], F32)
    sball("Drep", [128, NH], F32)
    sball("Pb", [128, NH], F32)
    sball("rstd0", [128, NT + 1], F32)
    sball("nm0r0", [128, NT + 1], F32)
    sball("convw", [128, 18 * 4], F32)
    sball("cwr", [72, 128], F32)
    sball("cbr", [18, 128], F32)
    sball("gmr", [16, 128], F32)
    sball("pf_t", [16, NH], F32)
    sball("convb", [128, 18], F32)
    sball("hm", [128, 1], F32)
    sball("gamma_sb", [128, 16], F32)
    sball("b2row", [128, 64], F32)
    sball("binrow", [1, 64], BF16)
    sball("boutrow", [1, DM], BF16)
    sball("onesK", [1, 128], BF16)
    sball("Win_bf", [128, 8 * 64], BF16)
    sball("Wout2_bf", [64, DM], BF16)
    sball("lnsc", [128, 32], F32)
    sball("bs2", [128, 12], F32)
    sball("ag2", [128, 2], F32)
    sball("stat5", [128, 1], F32)
    sball("stat6", [128, 1], F32)
    sball("m2t_sb", [8, 8], F32)
    sball("sel_sb", [8, 1], F32)
    sball("lp_sb", [8, NH], F32)
    sball("alpha", [8, NH], F32)
    sball("stat1", [128, 1], F32)
    sball("stat2", [128, 1], F32)
    sball("stat3", [128, 1], F32)
    sball("stat4", [128, 1], F32)
    sball("bs0", [128, 12], F32)
    sball("bs1", [128, 12], F32)
    sball("ag0", [128, 2], F32)
    sball("ag1", [128, 2], F32)
    sball("scr", [128, DM], F32)
    sball("wstage", [128, 8 * 128], F32)
    sball("wstage2", [128, 8 * 128], F32)
    sball("wbufA", [128, 8 * 128], BF16)
    sball("wbufB", [128, 8 * 128], BF16)
    sball("halo3", [128, 18 * 4], BF16)
    sball("S_run", [128, DI], F32)
    sball("S_j", [128, DI], F32)
    sball("S_tmp", [128, DI], F32)
    sball("Btok0", [128, 128], BF16)
    sball("Btok1", [128, 128], BF16)
    sball("zrow", [1, DI], F32)
    sball("Gm0", [128, 128], BF16)
    sball("Gm1", [128, 128], BF16)

    ARN = 133120
    AR = nc.alloc_sbuf_tensor("arena", [128, ARN], U8).ap()

    def ar(off, p, f, dtype):
        nbytes = f * (2 if dtype == BF16 else 4)
        assert off % 4 == 0 and off + nbytes <= ARN, (off, nbytes)
        return AR[0:p, off : off + nbytes].bitcast(dtype)

    v_uf = ar(0, 128, 8 * XROWS, BF16)
    v_xsb = [ar(34816, 128, 1024, F32), ar(38912, 128, 1024, F32)]
    v_usr = [ar(43008, 128, 1024, F32), ar(47104, 128, 1024, F32)]
    v_utok = [ar(51200, 128, 1024, BF16), ar(53248, 128, 1024, BF16)]
    v_dtF = ar(34816, NH, SEG, F32)
    v_lcF = ar(43008, NH, SEG, F32)
    v_wF = ar(51200, NH, SEG, F32)
    v_work = [ar(59392, 128, 516, BF16), ar(60448, 128, 516, BF16)]
    v_cvo = [ar(61504, 128, 512, BF16), ar(62528, 128, 512, BF16)]
    v_silu = [ar(63616, 128, 512, BF16), ar(64640, 128, 512, BF16)]
    v_wrow = ar(69760, 128, DM, F32)
    v_brow = ar(73856, 128, DM, F32)
    v_xst = [ar(34816, 128, 2048, BF16), ar(38912, 128, 2048, BF16)]
    v_xdt = [ar(43008, 128, 2048, BF16), ar(47104, 128, 2048, BF16)]
    v_xf = [ar(51200, 128, 2048, BF16), ar(55296, 128, 2048, BF16)]
    v_ysb = [ar(59392, 128, 2048, F32), ar(67584, 128, 2048, F32)]
    v_tmpb = ar(75776, 128, 2048, BF16)
    v_ylb = ar(79872, 128, 2048, BF16)
    v_L = [ar(83968, 128, 2048, BF16), ar(88064, 128, 2048, BF16)]
    v_Mp = [ar(92160, 128, 2048, BF16), ar(96256, 128, 2048, BF16),
            ar(100352, 128, 2048, BF16), ar(104448, 128, 2048, BF16)]
    v_lcb = [ar(108544, 128, 2048, F32), ar(116736, 128, 2048, F32)]
    v_diag = ar(108544, 128, 72 * 128, BF16)
    v_srunb = ar(124928, 128, 2048, BF16)
    v_wz = ar(34816, 128, 8 * DI, BF16)
    v_yl2 = [ar(67584, 128, 2048, BF16), ar(71680, 128, 2048, BF16)]
    v_y1 = [ar(75776, 128, 2048, F32), ar(100352, 128, 2048, F32)]
    v_g0 = [ar(83968, 128, 2048, BF16), ar(108544, 128, 2048, BF16)]
    v_saccb = ar(88064, 128, 2048, F32) if False else ar(88064, 128, 2048, BF16)
    v_sz = [ar(92160, 128, 2048, BF16), ar(96256, 128, 2048, BF16)]
    v_gfh = [ar(j * 2048, 128, 1024, BF16) for j in range(16)]
    v_wo = ar(32768, 128, 16 * 1024, BF16)
    v_W1r = ar(65536, 128, 1024, F32)
    v_b1r = ar(69632, 128, 1024, F32)
    v_W2r = ar(73728, 128, 1024, F32)
    _FB = [77824 + k * 16768 for k in range(3)]
    v_xfin = [ar(b, 128, 1024, F32) for b in _FB]
    v_x2 = [ar(b + 4096, 128, 1024, F32) for b in _FB]
    v_moh = [ar(b + 8192, 128, 1024, F32) for b in _FB]
    v_u3 = [ar(b + 12288, 128, 1024, BF16) for b in _FB]
    v_hb = [ar(b + 12288, 128, 1024, BF16) for b in _FB]
    v_u3f = [ar(b + 14336, 128, 1024, BF16) for b in _FB]
    v_g2 = [ar(b + 16384, 128, 64, F32) for b in _FB]
    v_g2b = [ar(b + 16640, 128, 64, BF16) for b in _FB]

    psA = nc.alloc_psum_tensor("psA", [128, 2048], F32).ap()
    psB = nc.alloc_psum_tensor("psB", [128, 1024], F32).ap()
    psT = nc.alloc_psum_tensor("psT", [128, 512], F32).ap()
    psTb = nc.alloc_psum_tensor("psTb", [128, 1024], BF16).ap()
    PSA = ["psA0", "psA1", "psA2", "psA3"]

    P = Plan()

    def dma(eng, out, in_, reads, writes, tag="", transpose=False):
        return P.op(
            eng,
            (lambda o_, i_, tr: (lambda e: e.dma_start(out=o_, in_=i_, transpose=tr)))(out, in_, transpose),
            reads=reads, writes=writes, dma=True, tag=tag,
        )

    def vop(fn, reads, writes, tag="", late=False):
        return P.op("vector", fn, reads=reads, writes=writes, tag=tag, late=late)

    def sop(fn, reads, writes, tag="", late=False):
        return P.op("scalar", fn, reads=reads, writes=writes, tag=tag, late=late)

    def top(fn, reads, writes, tag=""):
        return P.op("tensor", fn, reads=reads, writes=writes, tag=tag)

    cp = SB["cpool"]
    _cvals = [(EPS, 0), (-0.5, 1), (-1.0, 2), (1.0 / DI, 3), (1.0 + EPS, 4), (EPS * EPS, 5)]
    for val, col in _cvals:
        nc.const_aps.register(F32, val, cp[:, col : col + 1])
        vop((lambda v_, c_: lambda e: e.memset(cp[:, c_ : c_ + 1], v_))(val, col), [], ["constap"], f"cst{col}", late=True)

    dma("sync", SB["caus_sb"][:], T["caus"][:], [], ["caus"], "ld_caus")
    dma("sync", SB["identb"][:], T["ident_bf"][:], [], ["identb"], "ld_idb")
    dma("sync", SB["identf"][:], T["ident_f32"][:], [], ["identf"], "ld_idf")
    dma("sync", SB["Drep"][:], T["Dv"].unsqueeze(0).broadcast_to([128, NH]), [], ["Drep"], "ld_D")
    dma("sync", v_wrow[:], T["ln_w"].unsqueeze(0).broadcast_to([128, DM]), [], ["wrow"], "ld_w")
    dma("sync", v_brow[:], T["ln_b"].unsqueeze(0).broadcast_to([128, DM]), [], ["brow"], "ld_b")
    dma("sync", SB["cwr"][:], T["conv_wT"].rearrange("k (ct p) -> (k ct) p", p=128), [], ["cwr"], "ld_cw")
    dma("sync", SB["cbr"][:], T["conv_b"].rearrange("(ct p) -> ct p", p=128), [], ["cbr"], "ld_cb")
    dma("sync", SB["hm"][:], T["hmask"].broadcast_to([128, 1]), [], ["hm"], "ld_hm")
    dma("sync", SB["A_sb"][:], T["A_log"].unsqueeze(1), [], ["A_sb"], "ld_A")
    dma("sync", SB["dtb_sb"][:], T["dt_bias"].unsqueeze(1), [], ["dtb"], "ld_dtb")
    dma("sync", SB["gmr"][:], T["gamma"].rearrange("(j p) -> j p", p=128), [], ["gmr"], "ld_gm")
    dma("sync", SB["b2row"][:], T["b2"].unsqueeze(0).broadcast_to([128, 64]), [], ["b2r"], "ld_b2")
    dma("sync", SB["m2t_sb"][:], T["m2t"][:], [], ["m2t"], "ld_m2t")
    dma("sync", SB["sel_sb"][:], T["sel"][:], [], ["sel"], "ld_sel")

    vop(lambda e: e.memset(SB["ones_c"][:], 1.0), [], ["ones_c"], "ones_c", late=True)
    vop(lambda e: e.memset(SB["zcol"][:], 0.0), [], ["zcol"], "zcol", late=True)
    vop(lambda e: e.memset(SB["onesK"][:], 1.0), [], ["onesK"], "onesK", late=True)
    sop(lambda e: e.activation(SB["A_sb"][:], SB["A_sb"][:], AF.Exp), ["A_sb", "constap"], ["A_sb"], "A_exp")
    vop(lambda e: e.tensor_single_scalar(SB["A_sb"][:], SB["A_sb"][:], -1.0, ALU.mult), ["A_sb"], ["A_sb"], "A_neg")
    dma("sync", SB["wstage"][:, 0:512].rearrange("p (j m) -> p j m", m=64), T["Win"].rearrange("(j p) m -> p j m", p=128), [], ["wstage"], "ld_win")
    sop(lambda e: e.activation(SB["Win_bf"][:], SB["wstage"][:, 0:512], AF.Copy), ["wstage"], ["win_bf"], "cast_win")
    dma("sync", SB["wstage2"][0:64, 0:1024], T["Wout2"][:], [], ["wstage2"], "ld_wo2")
    sop(lambda e: e.activation(SB["Wout2_bf"][:], SB["wstage2"][0:64, 0:1024], AF.Copy), ["wstage2"], ["wo2_bf"], "cast_wo2")
    dma("sync", SB["wstage"][0:1, 512:576], T["bin"].unsqueeze(0), ["win_bf"], ["wstage"], "ld_bin")
    sop(lambda e: e.activation(SB["binrow"][:], SB["wstage"][0:1, 512:576], AF.Copy), ["wstage"], ["binrow"], "cast_bin")
    dma("sync", SB["wstage2"][0:1, 0:1024], T["bout"].unsqueeze(0), ["wo2_bf"], ["wstage2"], "ld_bout")
    sop(lambda e: e.activation(SB["boutrow"][:], SB["wstage2"][0:1, 0:1024], AF.Copy), ["wstage2"], ["boutrow"], "cast_bout")

    top(lambda e: e.transpose(psT[:, 0:72], SB["cwr"][:], SB["identf"][0:72, 0:72]),
        ["cwr", "identf"], ["psT"], "Tcw")
    vop(lambda e: e.tensor_copy(SB["convw"][:].rearrange("p (k ct) -> p k ct", ct=18), psT[:, 0:72].rearrange("p (k ct) -> p k ct", ct=18)),
        ["psT"], ["convw"], "Tcwe")
    top(lambda e: e.transpose(psT[:, 0:18], SB["cbr"][:], SB["identf"][0:18, 0:18]),
        ["cbr", "identf"], ["psT"], "Tcb")
    vop(lambda e: e.tensor_copy(SB["convb"][:], psT[:, 0:18]), ["psT"], ["convb"], "Tcbe")
    for ct_ in range(18):
        for k_ in range(4):
            vop((lambda c2, k2: lambda e: e.tensor_scalar_mul(
                    v_diag[:, (c2 * 4 + k2) * 128 : (c2 * 4 + k2 + 1) * 128],
                    SB["identb"][:], SB["convw"][:, k2 * 18 + c2 : k2 * 18 + c2 + 1]))(ct_, k_),
                ["identb", "convw"], ["cdiag"], f"dg{ct_}_{k_}")
    top(lambda e: e.transpose(psT[:, 0:16], SB["gmr"][:], SB["identf"][0:16, 0:16]),
        ["gmr", "identf"], ["psT"], "Tgm")
    vop(lambda e: e.tensor_copy(SB["gamma_sb"][:], psT[:, 0:16]), ["psT"], ["gamma"], "Tgme")
    P.barrier("barC0")

    for r in range(17):
        bi = r % 2
        L = SB["lnsc"][:, bi * 8 : bi * 8 + 8]
        bsk, agk = f"bs{bi}", f"ag{bi}"
        bs, ag = SB[bsk], SB[agk]
        xk, uk, utk = f"xsb{bi}", f"usr{bi}", f"utok{bi}"
        dma("sync", v_xsb[bi][:], T["x"][r * 128 : (r + 1) * 128, :], [], [xk], f"ldx{r}")
        vop((lambda b_, bs_: lambda e: e.bn_stats(bs_[:, 0:6], v_xsb[b_][:, 0:512]))(bi, bs),
            [xk], [bsk], f"bna{r}", late=True)
        vop((lambda b_, bs_: lambda e: e.bn_stats(bs_[:, 6:12], v_xsb[b_][:, 512:1024]))(bi, bs),
            [xk], [bsk], f"bnb{r}", late=True)
        vop((lambda bs_, ag_: lambda e: e.bn_aggr(ag_[:], bs_[:]))(bs, ag),
            [bsk], [agk], f"bng{r}", late=True)
        sop((lambda L_, ag_: lambda e: e.activation(L_[:, 3:4], ag_[:, 1:2], AF.Ln, bias=EPS))(L, ag),
            [agk], ["lnt1" + str(bi)], f"t1{r}")
        sop((lambda r_, L_: lambda e: e.activation(SB["rstd0"][:, r_ : r_ + 1], L_[:, 3:4], AF.Exp, scale=-0.5))(r, L),
            ["lnt1" + str(bi)], ["rstd0"], f"r0{r}")
        sop((lambda L_, ag_: lambda e: e.activation(L_[:, 4:5], ag_[:, 1:2], AF.Ln, scale=1.0 + EPS, bias=EPS * EPS))(L, ag),
            [agk], ["lnt12" + str(bi)], f"t12{r}")
        sop((lambda L_: lambda e: e.activation(L_[:, 5:6], L_[:, 4:5], AF.Exp, scale=-0.5))(L),
            ["lnt12" + str(bi)], ["lnq" + str(bi)], f"q{r}")
        vop((lambda r_, L_, ag_: lambda e: e.scalar_tensor_tensor(SB["nm0r0"][:, r_ : r_ + 1], SB["rstd0"][:, r_ : r_ + 1], -1.0, ag_[:, 0:1], ALU.mult, ALU.mult))(r, L, ag),
            ["rstd0", agk], ["nm0r0"], f"nm0r{r}", late=True)
        vop((lambda b_, L_, ag_: lambda e: e.tensor_scalar(v_usr[b_][:], v_xsb[b_][:], ag_[:, 0:1], L_[:, 5:6], ALU.subtract, ALU.mult))(bi, L, ag),
            [xk, agk, "lnq" + str(bi)], [uk], f"uhat{r}")
        vop((lambda b_: lambda e: e.tensor_mul(v_usr[b_][:], v_usr[b_][:], v_wrow[:]))(bi),
            [uk, "wrow"], [uk], f"uw{r}")
        vop((lambda b_: lambda e: e.tensor_add(v_utok[b_][:], v_usr[b_][:], v_brow[:]))(bi),
            [uk, "brow"], [utk], f"ub{r}")
        dma("sync", T["u_dram"][r * 128 : (r + 1) * 128, :], v_utok[bi][:], [utk], [f"u_dram{r}"], f"stu{r}")

    u_deps = [f"u_dram{r}" for r in range(17)]
    for j in range(8):
        dma("sync", v_uf[:, j * XROWS : (j + 1) * XROWS],
            T["u_dram"][:, j * 128 : (j + 1) * 128], u_deps, [f"uf{j}"], f"uft{j}", transpose=True)
    UFK = [f"uf{j}" for j in range(8)]

    def uf(j, t0, n):
        return v_uf[:, j * XROWS + t0 : j * XROWS + t0 + n]

    dma("sync", SB["wstage"][:, 0:256].rearrange("p (j c) -> p j c", c=32),
        T["wipT"][:, 4352:4384].rearrange("(j p) c -> p j c", p=128), [], ["wstage"], "ld_wdt")
    sop(lambda e: e.activation(SB["wbufA"][:, 0:256], SB["wstage"][:, 0:256], AF.Copy), ["wstage"], ["wbufA"], "cast_wdt")
    for d in range(8):
        for t4 in range(4):
            top((lambda d_, t_: lambda e: e.matmul(
                    psA[0:NH, t_ * 512 : (t_ + 1) * 512],
                    SB["wbufA"][:, d_ * 32 : (d_ + 1) * 32],
                    uf(d_, 128 + t_ * 512, 512),
                    start=(d_ == 0), stop=(d_ == 7), skip_group_check=True))(d, t4),
                ["wbufA"] + UFK, [f"psA{t4}"], f"dtmm{d}_{t4}")
    if "dbg_dt" in dbg:
        T["dbg_dt"] = nc.dram_tensor("dbg_dt", [4 * NH, SEG], F32, kind="ExternalOutput").ap()
    sop(lambda e: e.activation(v_wF[:], psA[0:NH, 0:SEG], AF.Abs, bias=SB["dtb_sb"][:]),
        PSA + ["dtb"], ["wF"], "sp_ab")
    sop(lambda e: e.activation(v_wF[:], v_wF[:], AF.Exp, scale=-1.0), ["wF"], ["wF"], "sp_en")
    sop(lambda e: e.activation(v_wF[:], v_wF[:], AF.Ln, bias=1.0), ["wF"], ["wF"], "sp_l1")
    sop(lambda e: e.activation(v_dtF[:], psA[0:NH, 0:SEG], AF.Relu, bias=SB["dtb_sb"][:]),
        PSA + ["dtb"], ["dtF"], "sp_rl")
    vop(lambda e: e.tensor_add(v_dtF[:], v_dtF[:], v_wF[:]), ["dtF", "wF"], ["dtF"], "dtFadd")
    if "dbg_dt" in dbg:
        dma("sync", T["dbg_dt"][NH : 2 * NH], v_dtF[:], ["dtF"], ["dbgdt1"], "dbg_st1")
        dma("sync", T["dbg_dt"][2 * NH : 3 * NH], v_wF[:], ["wF"], ["dbgdt2"], "dbg_st2")
    for c in range(NCH):
        top((lambda c_: lambda e: e.transpose(psT[:, c_ * NH : (c_ + 1) * NH], v_dtF[:, c_ * Q : (c_ + 1) * Q], SB["identf"][0:NH, 0:NH]))(c),
            ["dtF", "identf"], ["psT"], f"Tdt{c}")
    vop(lambda e: e.tensor_copy(SB["dt_tok"][:], psT[:, 0:512]), ["psT"], ["dt_tok"], "Tdte")
    vop(lambda e: e.tensor_scalar_mul(v_dtF[:], v_dtF[:], SB["A_sb"][:]), ["dtF", "A_sb"], ["dtF"], "dtA")
    vop(lambda e: e.memset(v_wF[:], 1.0), ["wF"], ["wF"], "wFones", late=True)
    vop(lambda e: e.tensor_tensor_scan(v_lcF[:], v_wF[:], v_dtF[:], 0.0, ALU.mult, ALU.add),
        ["wF", "dtF"], ["lcF"], "lcscan", late=True)
    vop(lambda e: e.memset(SB["negoff"][:, 0:1], 0.0), [], ["negoff"], "no0", late=True)
    vop(lambda e: e.tensor_single_scalar(
            SB["negoff"][:, 1:NCH],
            v_lcF.rearrange("h (c q) -> h c q", q=Q)[:, 0 : NCH - 1, Q - 1 : Q].squeeze(2), -1.0, ALU.mult),
        ["lcF"], ["negoff"], "nooff")
    for c in range(NCH):
        top((lambda c_: lambda e: e.transpose(psT[:, c_ * NH : (c_ + 1) * NH], v_lcF[:, c_ * Q : (c_ + 1) * Q], SB["identf"][0:NH, 0:NH]))(c),
            ["lcF", "identf"], ["psT"], f"Tlc{c}")
    vop(lambda e: e.tensor_copy(SB["lc_tok"][:], psT[:, 0:512]), ["psT"], ["lc_tok"], "Tlce")
    if "dbg_dt" in dbg:
        dma("sync", T["dbg_dt"][3 * NH : 4 * NH], v_dtF[:], ["dtF"], ["dbgdt3"], "dbg_st3")
    dma("sync", T["lc_dram"].rearrange("(c h) q -> h c q", h=NH), v_lcF.rearrange("h (c q) -> h c q", q=Q),
        ["lcF"], ["lc_dram"], "st_lc")
    for c in range(NCH):
        sop((lambda c_: lambda e: e.activation(v_wF[:, c_ * Q : (c_ + 1) * Q], v_lcF[:, c_ * Q : (c_ + 1) * Q], AF.Exp, bias=SB["negoff"][:, c_ : c_ + 1]))(c),
            ["lcF", "negoff"], ["wF"], f"eF{c}")
        sop((lambda c_: lambda e: e.activation(SB["PF"][:, c_ : c_ + 1], v_lcF[:, (c_ + 1) * Q - 1 : (c_ + 1) * Q], AF.Exp, bias=SB["negoff"][:, c_ : c_ + 1]))(c),
            ["lcF", "negoff"], ["PF"], f"PF{c}")
    for c in range(NCH):
        top((lambda c_: lambda e: e.transpose(psT[:, c_ * NH : (c_ + 1) * NH], v_wF[:, c_ * Q : (c_ + 1) * Q], SB["identf"][0:NH, 0:NH]))(c),
            ["wF", "identf"], ["psT"], f"Te{c}")
    vop(lambda e: e.tensor_copy(SB["e_tok"][:], psT[:, 0:512]), ["psT"], ["e_tok"], "Tee")
    for c in range(NCH):
        sop((lambda c_: lambda e: e.activation(v_wF[:, c_ * Q : (c_ + 1) * Q], v_lcF[:, c_ * Q : (c_ + 1) * Q], AF.Exp, scale=-1.0, bias=v_lcF[:, (c_ + 1) * Q - 1 : (c_ + 1) * Q]))(c),
            ["lcF"], ["wF"], f"fF{c}")
    for c in range(NCH):
        top((lambda c_: lambda e: e.transpose(psT[:, c_ * NH : (c_ + 1) * NH], v_wF[:, c_ * Q : (c_ + 1) * Q], SB["identf"][0:NH, 0:NH]))(c),
            ["wF", "identf"], ["psT"], f"Tf{c}")
    vop(lambda e: e.tensor_copy(SB["dtf_tok"][:], psT[:, 0:512]), ["psT"], ["dtf_tok"], "Tfe")
    vop(lambda e: e.tensor_mul(SB["dtf_tok"][:], SB["dt_tok"][:], SB["dtf_tok"][:]),
        ["dt_tok", "dtf_tok"], ["dtf_tok"], "dtf")
    sop(lambda e: e.activation(v_wF[:], v_lcF[:], AF.Exp), ["lcF"], ["wF"], "esegF")
    for c in range(NCH):
        top((lambda c_: lambda e: e.transpose(psT[:, c_ * NH : (c_ + 1) * NH], v_wF[:, c_ * Q : (c_ + 1) * Q], SB["identf"][0:NH, 0:NH]))(c),
            ["wF", "identf"], ["psT"], f"Tes{c}")
    vop(lambda e: e.tensor_copy(SB["eseg_tok"][:], psT[:, 0:512]), ["psT"], ["eseg_tok"], "Tese")
    top(lambda e: e.transpose(psT[0:NCH, 0:NH], SB["PF"][:], SB["identf"][0:NH, 0:NH]),
        ["PF", "identf"], ["psT"], "Tpf")
    vop(lambda e: e.tensor_copy(SB["pf_t"][:], psT[0:NCH, 0:NH]), ["psT"], ["pf_t"], "Tpfe")
    dma("sync", T["pf_dram"][:], SB["pf_t"][:], ["pf_t"], ["pf_dram"], "st_pf")

    wk = [("wbufA", SB["wbufA"]), ("wbufB", SB["wbufB"])]
    stg = [("wstage", SB["wstage"]), ("wstage2", SB["wstage2"])]
    for ct in range(18):
        wkey, wbuf = wk[ct % 2]
        skey, sbuf = stg[ct % 2]
        col0 = 2048 + ct * 128
        dma("sync", sbuf[:, 0:1024],
            T["wipA3"][ct * 128 : (ct + 1) * 128, :],
            [], [skey], f"ldw{ct}")
        sop((lambda s_, w_: lambda e: e.activation(w_[:], s_[:, 0:1024], AF.Copy))(sbuf, wbuf),
            [skey], [wkey], f"castw{ct}")
        hbk = f"psB{ct % 2}"
        hb0 = (ct % 2) * 512
        for ts in range(4):
            for d in range(8):
                top((lambda d_, t_, w_, o_: lambda e: e.matmul(
                        psB[:, o_ + t_ * 4 : o_ + t_ * 4 + 3], w_[:, d_ * 128 : (d_ + 1) * 128],
                        uf(d_, 125 + t_ * 512, 3), start=(d_ == 0), stop=(d_ == 7), skip_group_check=True))(d, ts, wbuf, hb0),
                    [wkey] + UFK, [hbk], f"hmm{ct}_{ts}_{d}")
        for ts in range(4):
            pk = f"psA{ts % 2}"
            psl = psA[:, (ts % 2) * 512 : (ts % 2) * 512 + 512]
            for d in range(8):
                top((lambda d_, t_, w_, p_: lambda e: e.matmul(
                        p_, w_[:, d_ * 128 : (d_ + 1) * 128],
                        uf(d_, 128 + t_ * 512, 512),
                        start=(d_ == 0), stop=(d_ == 7)))(d, ts, wbuf, psl),
                    [wkey] + UFK, [pk], f"xmm{ct}_{ts}_{d}")
            wbk = f"work{ts % 2}"
            wbuf2 = v_work[ts % 2]
            if ts == 0:
                sop((lambda w_, o_: lambda e: e.activation(w_[:, 0:3], psB[:, o_ : o_ + 3], AF.Identity, scale=SB["hm"][:]))(wbuf2, hb0),
                    [hbk, "hm"], [wbk], f"evh{ct}_{ts}")
            else:
                sop((lambda w_, t_, o_: lambda e: e.activation(w_[:, 0:3], psB[:, o_ + t_ * 4 : o_ + t_ * 4 + 3], AF.Copy))(wbuf2, ts, hb0),
                    [hbk], [wbk], f"evh{ct}_{ts}")
            sop((lambda w_, p_: lambda e: e.activation(w_[:, 3:515], p_, AF.Copy))(wbuf2, psl),
                [pk], [wbk], f"evac{ct}_{ts}")
            cpk = f"psA{2 + ts % 2}"
            psc = psA[:, (2 + ts % 2) * 512 : (2 + ts % 2) * 512 + 512]
            for k in range(4):
                top((lambda w_, p_, ct_, k_: lambda e: e.matmul(
                        p_, v_diag[:, (ct_ * 4 + k_) * 128 : (ct_ * 4 + k_ + 1) * 128],
                        w_[:, k_ : 512 + k_], start=(k_ == 0), stop=(k_ == 3)))(wbuf2, psc, ct, k),
                    [wbk, "cdiag"], [cpk], f"cnv{ct}_{ts}_{k}")
            if ct < 16:
                so = v_silu[ts % 2]
                sok = f"silu{ts % 2}"
                sop((lambda p_, s_, ct_: lambda e: e.activation(s_[:], p_, AF.Silu, bias=SB["convb"][:, ct_ : ct_ + 1]))(psc, so, ct),
                    [cpk, "convb"], [sok], f"silu{ct}_{ts}")
                dma("scalar", T["xs_dram"][ct * 128 : (ct + 1) * 128, ts * 512 : (ts + 1) * 512],
                    so[:], [sok], [f"xs{ct}_{ts}"], f"stxs{ct}_{ts}")
            else:
                dstk = ("Bf" if ct == 16 else "Cf") + str(ts)
                dst = SB["Bf" if ct == 16 else "Cf"]
                sop((lambda p_, d_, ct_, t_: lambda e: e.activation(d_[:, t_ * 512 : (t_ + 1) * 512], p_, AF.Silu, bias=SB["convb"][:, ct_ : ct_ + 1]))(psc, dst, ct, ts),
                    [cpk, "convb"], [dstk], f"siluBC{ct}_{ts}")

    if "dbg_a3" in dbg:
        pass
    XSK = [f"xs{ct}_{ts}" for ct in range(16) for ts in range(4)]
    P.barrier("barA")

    vop(lambda e: e.memset(SB["S_run"][:], 0.0), [], ["S_run"], "S0", late=True)
    for c in range(NCH):
        bi = c % 2
        xk = f"xst{bi}"
        dma("sync", v_xst[bi][:], T["xs_dram"][:, c * 128 : (c + 1) * 128],
            [f"xs{ct_}_{c // 4}" for ct_ in range(16)], [xk], f"xsT{c}", transpose=True)
        top((lambda c_: lambda e: e.matmul(psT[:, 0:128], SB["Bf"][:, c_ * 128 : (c_ + 1) * 128],
                                           SB["Cf"][:, c_ * 128 : (c_ + 1) * 128], start=True, stop=True))(c),
            [f"Bf{c // 4}", f"Cf{c // 4}"], ["psT"], f"gmm{c}")
        vop((lambda b_: lambda e: e.tensor_tensor(SB["Gm" + str(b_)][:], psT[:, 0:128], SB["caus_sb"][:], ALU.mult))(bi),
            ["psT", "caus"], [f"Gm{bi}"], f"gm{c}")
        for hh in range(2):
            lk = f"lcb{hh}"
            h0 = hh * NH2
            mi = 2 * bi + hh
            dma("scalar", v_lcb[hh][:],
                T["lc_dram"].rearrange("(c h) q -> c (h q)", h=NH)[c : c + 1, h0 * 128 : (h0 + NH2) * 128].broadcast_to([128, NH2 * 128]),
                ["lc_dram"], [lk], f"lcbd{c}_{hh}")
            vop((lambda hh_, c_, h0_: lambda e: e.tensor_tensor(
                    v_lcb[hh_][:].rearrange("p (a b) -> p a b", b=128),
                    v_lcb[hh_][:].rearrange("p (a b) -> p a b", b=128),
                    rep3(SB["lc_tok"][:, c_ * NH + h0_ : c_ * NH + h0_ + NH2], 128),
                    ALU.subtract))(hh, c, h0),
                [lk, "lc_tok"], [lk], f"E{c}_{hh}")
            vop((lambda hh_: lambda e: e.tensor_scalar_min(v_lcb[hh_][:], v_lcb[hh_][:], 0.0))(hh),
                [lk], [lk], f"Emin{c}_{hh}")
            sop((lambda hh_, b_: lambda e: e.activation(v_L[b_][:], v_lcb[hh_][:], AF.Exp))(hh, bi),
                [lk], [f"L{bi}"], f"expL{c}_{hh}")
            vop((lambda mi_, b_: lambda e: e.tensor_tensor(
                    v_Mp[mi_][:].rearrange("p (a b) -> p a b", b=128),
                    v_L[b_][:].rearrange("p (a b) -> p a b", b=128),
                    SB["Gm" + str(b_)][:].unsqueeze(1).broadcast_to([128, NH2, 128]),
                    ALU.mult))(mi, bi),
                [f"L{bi}", f"Gm{bi}"], [f"Mp{mi}"], f"mp{c}_{hh}")
        vop((lambda b_, c_: lambda e: e.tensor_tensor(
                v_xdt[b_][:].rearrange("p (a b) -> p a b", b=HD),
                v_xst[b_][:].rearrange("p (a b) -> p a b", b=HD),
                rep3(SB["dt_tok"][:, c_ * NH : (c_ + 1) * NH], HD),
                ALU.mult))(bi, c),
            [xk, "dt_tok"], [f"xdt{bi}"], f"xdt{c}")
        vop((lambda b_, c_: lambda e: e.tensor_tensor(
                v_xf[b_][:].rearrange("p (a b) -> p a b", b=HD),
                v_xst[b_][:].rearrange("p (a b) -> p a b", b=HD),
                rep3(SB["dtf_tok"][:, c_ * NH : (c_ + 1) * NH], HD),
                ALU.mult))(bi, c),
            [xk, "dtf_tok"], [f"xf{bi}"], f"xf{c}")
        top((lambda c_: lambda e: e.transpose(psTb[:, 0:128], SB["Bf"][:, c_ * 128 : (c_ + 1) * 128], SB["identb"][:]))(c),
            [f"Bf{c // 4}", "identb"], ["psTb"], f"TB{c}")
        vop((lambda b_: lambda e: e.tensor_copy(SB["Btok" + str(b_)][:], psTb[:, 0:128]))(bi),
            ["psTb"], [f"Btok{bi}"], f"TBe{c}")
        sop(lambda e: e.activation(v_srunb[:], SB["S_run"][:], AF.Copy), ["S_run"], ["S_runb"], f"srb{c}")
        for yh in range(2):
            for j2 in range(2):
                top((lambda c_, y_, j_: lambda e: e.matmul(
                        psB[:, j_ * 512 : (j_ + 1) * 512], SB["Cf"][:, c_ * 128 : (c_ + 1) * 128],
                        v_srunb[:, y_ * 1024 + j_ * 512 : y_ * 1024 + (j_ + 1) * 512], start=True, stop=True))(c, yh, j2),
                    [f"Cf{c // 4}", "S_runb"], [f"psB{j2}"], f"yint{c}_{yh}_{j2}")
            vop((lambda b_, c_, y_: lambda e: e.tensor_tensor(
                    v_ysb[b_][:, y_ * 1024 : (y_ + 1) * 1024].rearrange("p (a b) -> p a b", b=HD),
                    psB[:, 0:1024].rearrange("p (a b) -> p a b", b=HD),
                    rep3(SB["e_tok"][:, c_ * NH + y_ * NH2 : c_ * NH + (y_ + 1) * NH2], HD),
                    ALU.mult))(bi, c, yh),
                ["psB0", "psB1", "e_tok"], [f"ysb{bi}_{yh}"], f"yie{c}_{yh}")
            for h in range(yh * NH2, (yh + 1) * NH2):
                top((lambda h_, mi_, b_, y_: lambda e: e.matmul(
                        psB[:, (h_ % NH2) * HD : (h_ % NH2 + 1) * HD], v_Mp[mi_][:, (h_ % NH2) * 128 : (h_ % NH2 + 1) * 128],
                        v_xdt[b_][:, h_ * HD : (h_ + 1) * HD], start=True, stop=True))(h, 2 * bi + yh, bi, yh),
                    [f"Mp{2 * bi + yh}", f"xdt{bi}"], [f"psB{(h % NH2) // 8}"], f"yin{c}_{h}")
            vop((lambda b_, y_: lambda e: e.tensor_add(
                    v_ysb[b_][:, y_ * 1024 : (y_ + 1) * 1024],
                    v_ysb[b_][:, y_ * 1024 : (y_ + 1) * 1024], psB[:, 0:1024]))(bi, yh),
                [f"ysb{bi}_{yh}", "psB0", "psB1"], [f"ysb{bi}_{yh}"], f"yadd{c}_{yh}")
        for j4 in range(4):
            top((lambda b_, j_: lambda e: e.matmul(
                    psA[:, j_ * 512 : (j_ + 1) * 512], SB["Btok" + str(b_)][:],
                    v_xf[b_][:, j_ * 512 : (j_ + 1) * 512], start=True, stop=True))(bi, j4),
                [f"Btok{bi}", f"xf{bi}"], [f"psA{j4}"], f"smm{c}_{j4}")
        dma("scalar", SB["Pb"][:],
            T["pf_dram"][c : c + 1, :].broadcast_to([128, NH]),
            ["pf_dram"], ["Pb"], f"pb{c}")
        vop(lambda e: e.tensor_tensor(
                SB["S_tmp"][:].rearrange("p (a b) -> p a b", b=HD),
                SB["S_run"][:].rearrange("p (a b) -> p a b", b=HD),
                rep3(SB["Pb"][:], HD), ALU.mult),
            ["S_run", "Pb"], ["S_tmp"], f"stmp{c}")
        vop(lambda e: e.tensor_add(SB["S_run"][:], psA[:, 0:2048], SB["S_tmp"][:]),
            PSA + ["S_tmp"], ["S_run"], f"supd{c}")
        vop((lambda b_: lambda e: e.tensor_tensor(
                v_tmpb[:].rearrange("p (a b) -> p a b", b=HD),
                v_xst[b_][:].rearrange("p (a b) -> p a b", b=HD),
                rep3(SB["Drep"][:], HD), ALU.mult))(bi),
            [xk, "Drep"], ["tmpb"], f"dxs{c}")
        vop((lambda b_: lambda e: e.tensor_add(v_ysb[b_][:], v_ysb[b_][:], v_tmpb[:]))(bi),
            [f"ysb{bi}_0", f"ysb{bi}_1", "tmpb"], [f"ysb{bi}_0", f"ysb{bi}_1"], f"yD{c}")
        sop((lambda b_: lambda e: e.activation(v_ylb[:], v_ysb[b_][:], AF.Copy))(bi),
            [f"ysb{bi}_0", f"ysb{bi}_1"], ["ylb"], f"ycast{c}")
        dma("gpsimd", T["yl_dram"][c * 128 : (c + 1) * 128, :], v_ylb[:], ["ylb"], [f"yl{c}"], f"styl{c}")

    vop(lambda e: e.memset(SB["zrow"][:], 0.0), [], ["zrow"], "zrow", late=True)
    dma("gpsimd", T["cc_in"][128:129, :], SB["zrow"][:], ["zrow"], ["cc_in"], "st_zrow")
    dma("gpsimd", T["cc_in"][0:128, :], SB["S_run"][:], ["S_run"], ["cc_in"], "st_sseg")
    dma("gpsimd", T["cc_in"][128:129, 0:NH], SB["lc_tok"][127:128, 15 * NH : 16 * NH], ["lc_tok"], ["cc_in"], "st_lp")
    P.op("gpsimd",
         lambda e: e.collective_compute(
             "AllGather", ALU.bypass, replica_groups=[list(range(CORES))],
             ins=[T["cc_in"].opt()], outs=[T["cc_out"].opt()]),
         reads=["cc_in"], writes=["cc_out"], tag="allgather")
    CDCLR = ["xst0", "xst1", "xdt0", "xdt1", "xf0", "xf1", "ysb0_0", "ysb0_1"]
    for d in range(8):
        for half in range(2):
            skey, sbuf = stg[half]
            dma("sync", sbuf[:, 0:1024],
                T["wipT"][:, 0:2048][d * 128 : (d + 1) * 128, half * 1024 : (half + 1) * 1024],
                [], [skey], f"ldwz{d}_{half}")
            sop((lambda d_, h_, s_: lambda e: e.activation(v_wz[:, d_ * 2048 + h_ * 1024 : d_ * 2048 + (h_ + 1) * 1024], s_[:, 0:1024], AF.Copy))(d, half, sbuf),
                [skey], ["wz", CDCLR[d]], f"castwz{d}_{half}")
    for j in range(8):
        dma("gpsimd", SB["lp_sb"][j : j + 1, :], T["cc_out"][j * 129 + 128 : j * 129 + 129, 0:NH],
            ["cc_out"], ["lp_sb"], f"ldlp{j}")
    top(lambda e: e.matmul(psB[0:8, 0:NH], SB["m2t_sb"][:], SB["lp_sb"][:], start=True, stop=True),
        ["m2t", "lp_sb"], ["psB0"], "alphamm")
    sop(lambda e: e.activation(SB["alpha"][:], psB[0:8, 0:NH], AF.Exp), ["psB0"], ["alpha"], "alphaexp")
    vop(lambda e: e.tensor_scalar_mul(SB["alpha"][:], SB["alpha"][:], SB["sel_sb"][:]),
        ["alpha", "sel"], ["alpha"], "alphasel", late=True)
    dma("gpsimd", T["al_dram"][:], SB["alpha"][:], ["alpha"], ["al_dram"], "st_alpha")
    vop(lambda e: e.memset(SB["S_run"][:], 0.0), ["cc_in"], ["S_run"], "sacc0", late=True)
    for j in range(8):
        dma("gpsimd", SB["S_j"][:], T["cc_out"][j * 129 : j * 129 + 128, :], ["cc_out"], ["S_j"], f"ldsj{j}")
        dma("scalar", SB["Pb"][:], T["al_dram"][j : j + 1, :].broadcast_to([128, NH]), ["al_dram"], ["Pb"], f"ldal{j}")
        vop(lambda e: e.tensor_tensor(
                SB["S_tmp"][:].rearrange("p (a b) -> p a b", b=HD),
                SB["S_j"][:].rearrange("p (a b) -> p a b", b=HD),
                rep3(SB["Pb"][:], HD), ALU.mult),
            ["S_j", "Pb"], ["S_tmp"], f"sjw{j}")
        vop(lambda e: e.tensor_add(SB["S_run"][:], SB["S_run"][:], SB["S_tmp"][:]),
            ["S_run", "S_tmp"], ["S_run"], f"sacc{j}")
    vop(lambda e: e.tensor_copy(v_saccb[:], SB["S_run"][:]), ["S_run"], ["S_accb", "L1"], "saccb")

    for c in range(NCH):
        bi = c % 2
        szk = f"sz{bi}"
        ylk = f"yl2{bi}"
        y1k, g0k = f"y1{bi}", f"g0{bi}"
        y1b, g0b = v_y1[bi], v_g0[bi]
        st1 = "stat1" if bi == 0 else "stat3"
        st2 = "stat2" if bi == 0 else "stat4"
        s1, s2 = SB[st1], SB[st2]
        for d in range(8):
            for j4 in range(4):
                top((lambda c_, d_, j_: lambda e: e.matmul(
                        psA[:, j_ * 512 : (j_ + 1) * 512],
                        uf(d_, 128 + c_ * 128, 128),
                        v_wz[:, d_ * 2048 + j_ * 512 : d_ * 2048 + (j_ + 1) * 512],
                        start=(d_ == 0), stop=(d_ == 7), skip_group_check=True))(c, d, j4),
                    ["wz"] + UFK, [f"psA{j4}"], f"zmm{c}_{d}_{j4}")
        sop((lambda b_, c_: lambda e: e.activation(v_sz[b_][:], psA[:, 0:2048], AF.Silu))(bi, c),
            PSA, [szk, f"Mp{bi}"], f"zsilu{c}")
        dma("gpsimd", v_yl2[bi][:], T["yl_dram"][c * 128 : (c + 1) * 128, :],
            [f"yl{c}"], [ylk, f"ysb1_{bi}"], f"ldyl{c}")
        yal = ["tmpb", "ylb"] if bi == 0 else ["Mp2", "Mp3"]
        for h in range(2):
            for j2 in range(2):
                top((lambda c_, h_, j_: lambda e: e.matmul(
                        psB[:, j_ * 512 : (j_ + 1) * 512], SB["Cf"][:, c_ * 128 : (c_ + 1) * 128],
                        v_saccb[:, h_ * 1024 + j_ * 512 : h_ * 1024 + (j_ + 1) * 512],
                        start=True, stop=True))(c, h, j2),
                    [f"Cf{c // 4}", "S_accb"], [f"psB{j2}"], f"corr{c}_{h}_{j2}")
            vop((lambda c_, h_, y_: lambda e: e.tensor_tensor(
                    y_[:, h_ * 1024 : (h_ + 1) * 1024].rearrange("p (a b) -> p a b", b=HD),
                    psB[:, 0:1024].rearrange("p (a b) -> p a b", b=HD),
                    rep3(SB["eseg_tok"][:, c_ * NH + h_ * NH2 : c_ * NH + (h_ + 1) * NH2], HD),
                    ALU.mult))(c, h, y1b),
                ["psB0", "psB1", "eseg_tok"], [y1k] + yal, f"ycorr{c}_{h}")
        vop((lambda b_, y_: lambda e: e.tensor_add(y_[:], y_[:], v_yl2[b_][:]))(bi, y1b),
            [y1k, ylk], [y1k], f"yfull{c}")
        vop((lambda b_, y_, g_: lambda e: e.tensor_tensor(g_[:], y_[:], v_sz[b_][:], ALU.mult))(bi, y1b, g0b),
            [y1k, szk], [g0k, "L0" if bi == 0 else "lcb0"], f"gate{c}")
        sop((lambda y_, g_, s_: lambda e: e.activation(y_[:], g_[:], AF.Square, accum_out=s_[:]))(y1b, g0b, s1),
            [g0k], [st1, y1k], f"rmssq{c}", late=True)
        sop((lambda s1_, s2_: lambda e: e.activation(s2_[:], s1_[:], AF.Ln, scale=1.0 / DI, bias=EPS))(s1, s2),
            [st1], [st2], f"rmsln{c}")
        sop((lambda s2_: lambda e: e.activation(s2_[:], s2_[:], AF.Exp, scale=-0.5))(s2),
            [st2], [st2], f"rmsexp{c}")
        sop((lambda g_, s2_: lambda e: e.activation(g_[:], g_[:], AF.Identity, scale=s2_[:]))(g0b, s2),
            [g0k, st2], [g0k], f"rmsapp{c}")
        dma("gpsimd", T["g_dram"][c * 128 : (c + 1) * 128, :], g0b[:], [g0k], [f"g{c}"], f"stg{c}")

    GK = [f"g{c}" for c in range(NCH)]
    P.barrier("barPOST")

    for j in range(16):
        skey, sbuf = stg[j % 2]
        dma("sync", sbuf[:, 0:1024], T["wopT"][j * 128 : (j + 1) * 128, :], [], [skey], f"ldwo{j}")
        sop((lambda j_, s_: lambda e: e.activation(v_wo[:, j_ * 1024 : (j_ + 1) * 1024], s_[:, 0:1024], AF.Identity, scale=SB["gamma_sb"][:, j_ : j_ + 1]))(j, sbuf),
            [skey, "gamma"], ["wo"], f"castwo{j}")
    dma("sync", v_W1r[:], T["W1"].rearrange("a b -> (a b)").unsqueeze(0).broadcast_to([128, 1024]), [], ["W1r"], "ld_W1")
    dma("sync", v_b1r[:], T["b1"].rearrange("a b -> (a b)").unsqueeze(0).broadcast_to([128, 1024]), [], ["b1r"], "ld_b1")
    dma("sync", v_W2r[:], T["W2"].rearrange("a b -> (a b)").unsqueeze(0).broadcast_to([128, 1024]), [], ["W2r"], "ld_W2")

    GFK = [f"gfh{j}" for j in range(16)]

    def plain_ln(src_ap, src_keys, dst_ap, dst_key, tag, co, bi):
        Lx = SB["lnsc"]
        bsk, agk = f"bs{bi}", f"ag{bi}"
        if bi == 2:
            bsk, agk = "bs2", "ag2"
        bs, ag = SB[bsk], SB[agk]
        vop((lambda s_, bs_: lambda e: e.bn_stats(bs_[:, 0:6], s_[:, 0:512]))(src_ap, bs),
            src_keys, [bsk], f"{tag}bna", late=True)
        vop((lambda s_, bs_: lambda e: e.bn_stats(bs_[:, 6:12], s_[:, 512:1024]))(src_ap, bs),
            src_keys, [bsk], f"{tag}bnb", late=True)
        vop((lambda bs_, ag_: lambda e: e.bn_aggr(ag_[:], bs_[:]))(bs, ag),
            [bsk], [agk], f"{tag}bng", late=True)
        sop((lambda c_, ag_: lambda e: e.activation(Lx[:, c_ + 3 : c_ + 4], ag_[:, 1:2], AF.Ln, bias=EPS))(co, ag),
            [agk], [f"{tag}lnv"], f"{tag}lnv")
        sop((lambda c_: lambda e: e.activation(Lx[:, c_ + 4 : c_ + 5], Lx[:, c_ + 3 : c_ + 4], AF.Exp, scale=-0.5))(co),
            [f"{tag}lnv"], [f"{tag}r"], f"{tag}r")
        vop((lambda s_, d_, c_, ag_: lambda e: e.tensor_scalar(d_[:], s_[:], ag_[:, 0:1], Lx[:, c_ + 4 : c_ + 5], ALU.subtract, ALU.mult))(src_ap, dst_ap, co, ag),
            src_keys + [f"{tag}r", agk], [dst_key], f"{tag}app", late=True)

    for th in range(2):
        for j in range(16):
            dma("sync", v_gfh[j][:],
                T["g_dram"][th * 1024 : (th + 1) * 1024, j * 128 : (j + 1) * 128],
                [f"g{c_}" for c_ in range(th * 8, th * 8 + 8)], [f"gfh{j}"], f"gfT{th}_{j}", transpose=True)
        for r in range(th * 8, th * 8 + 8):
            bi = r % 3
            rr = r - th * 8
            xk = f"xfin{bi}"
            co = bi * 8
            st1 = ["stat1", "stat3", "stat5"][bi]
            st2 = ["stat2", "stat4", "stat6"][bi]
            dma("sync", v_xfin[bi][:], T["x"][128 + r * 128 : 128 + (r + 1) * 128, :], [], [xk], f"ldx2_{r}")
            for j in range(16):
                for d2 in range(2):
                    top((lambda rr_, j_, d_: lambda e: e.matmul(
                            psA[:, d_ * 512 : (d_ + 1) * 512], v_gfh[j_][:, rr_ * 128 : (rr_ + 1) * 128],
                            v_wo[:, j_ * 1024 + d_ * 512 : j_ * 1024 + (d_ + 1) * 512],
                            start=(j_ == 0), stop=(j_ == 15), skip_group_check=True))(rr, j, d2),
                        GFK + ["wo"], [f"psA{d2}"], f"opmm{r}_{j}_{d2}")
            sop((lambda b_, r_: lambda e: e.activation(v_moh[b_][:], v_xfin[b_][:], AF.Identity, scale=SB["rstd0"][:, r_ + 1 : r_ + 2], bias=SB["nm0r0"][:, r_ + 1 : r_ + 2]))(bi, r),
                [xk, "rstd0", "nm0r0"], [f"moh{bi}"], f"u0_{r}")
            vop((lambda b_: lambda e: e.tensor_add(v_moh[b_][:], v_moh[b_][:], psA[:, 0:1024]))(bi),
                [f"moh{bi}", "psA0", "psA1"], [f"moh{bi}"], f"moadd{r}")
            plain_ln(v_moh[bi], [f"moh{bi}"], v_moh[bi], f"moh{bi}", f"ln2_{r}_", co, bi)
            vop((lambda b_: lambda e: e.tensor_add(v_x2[b_][:], v_xfin[b_][:], v_moh[b_][:]))(bi),
                [xk, f"moh{bi}"], [f"x2{bi}"], f"x2_{r}")
            plain_ln(v_x2[bi], [f"x2{bi}"], v_u3[bi], f"u3{bi}", f"ln3_{r}_", co, bi)
            for j in range(8):
                top((lambda j_, b_: lambda e: e.transpose(psTb[:, j_ * 128 : (j_ + 1) * 128], v_u3[b_][:, j_ * 128 : (j_ + 1) * 128], SB["identb"][:]))(j, bi),
                    [f"u3{bi}", "identb"], ["psTb"], f"Tu3_{r}_{j}")
            sop((lambda b_: lambda e: e.activation(v_u3f[b_][:], psTb[:], AF.Copy))(bi),
                ["psTb"], [f"u3f{bi}"], f"Tu3e_{r}", late=True)
            for d in range(8):
                top((lambda d_, b_: lambda e: e.matmul(psB[:, 0:64], v_u3f[b_][:, d_ * 128 : (d_ + 1) * 128],
                                                       SB["Win_bf"][:, d_ * 64 : (d_ + 1) * 64], start=(d_ == 0), stop=False))(d, bi),
                    [f"u3f{bi}", "win_bf"], ["psB0"], f"ffs{r}_{d}")
            top(lambda e: e.matmul(psB[:, 0:64], SB["onesK"][:], SB["binrow"][:], start=False, stop=True),
                ["onesK", "binrow"], ["psB0"], f"ffsb{r}")
            vop((lambda b_: lambda e: e.tensor_tensor(
                    v_moh[b_][:].rearrange("p (a b) -> p a b", b=16),
                    psB[:, 0:64].unsqueeze(2).broadcast_to([128, 64, 16]),
                    v_W1r[:].rearrange("p (a b) -> p a b", b=16), ALU.mult))(bi),
                ["psB0", "W1r"], [f"moh{bi}"], f"h1_{r}")
            vop((lambda b_: lambda e: e.tensor_add(v_moh[b_][:], v_moh[b_][:], v_b1r[:]))(bi),
                [f"moh{bi}", "b1r"], [f"moh{bi}"], f"h2_{r}")
            sop((lambda b_: lambda e: e.activation(v_hb[b_][:], v_moh[b_][:], AF.Relu))(bi),
                [f"moh{bi}"], [f"hb{bi}", f"u3{bi}"], f"h3_{r}")
            vop((lambda b_: lambda e: e.tensor_tensor(v_moh[b_][:], v_hb[b_][:], v_W2r[:], ALU.mult))(bi),
                [f"hb{bi}", f"u3{bi}", "W2r"], [f"moh{bi}"], f"hw_{r}")
            vop((lambda b_: lambda e: e.tensor_reduce(v_g2[b_][:], v_moh[b_][:].rearrange("p (a b) -> p a b", b=16), AXX, ALU.add))(bi),
                [f"moh{bi}"], [f"g2{bi}"], f"g2_{r}", late=True)
            vop((lambda b_: lambda e: e.tensor_add(v_g2[b_][:], v_g2[b_][:], SB["b2row"][:]))(bi),
                [f"g2{bi}", "b2r"], [f"g2{bi}"], f"g2b_{r}")
            sop((lambda b_: lambda e: e.activation(v_g2b[b_][:], v_g2[b_][:], AF.Copy))(bi),
                [f"g2{bi}"], [f"g2b{bi}"], f"g2c_{r}", late=True)
            top((lambda b_: lambda e: e.transpose(psTb[0:64, 0:128], v_g2b[b_][:], SB["identb"][:]))(bi),
                [f"g2b{bi}", "identb"], ["psTb"], f"Tg2_{r}")
            sop((lambda b_: lambda e: e.activation(v_u3f[b_][0:64, 0:128], psTb[0:64, 0:128], AF.Copy))(bi),
                ["psTb"], [f"u3f{bi}"], f"Tg2e_{r}")
            for d2 in range(2):
                top((lambda d_, b_: lambda e: e.matmul(psB[:, d_ * 512 : (d_ + 1) * 512], v_u3f[b_][0:64, 0:128],
                                                       SB["Wout2_bf"][:, d_ * 512 : (d_ + 1) * 512], start=True, stop=False, skip_group_check=True))(d2, bi),
                    [f"u3f{bi}", "wo2_bf"], [f"psB{d2}"], f"fo{r}_{d2}")
            for d2 in range(2):
                top((lambda d_: lambda e: e.matmul(psB[:, d_ * 512 : (d_ + 1) * 512], SB["onesK"][:],
                                                   SB["boutrow"][:, d_ * 512 : (d_ + 1) * 512], start=False, stop=True, skip_group_check=True))(d2),
                    ["onesK", "boutrow"], [f"psB{d2}"], f"fob{r}_{d2}")
            sop((lambda b_: lambda e: e.activation(v_moh[b_][:], psB[:, 0:1024], AF.Copy))(bi),
                ["psB0", "psB1"], [f"moh{bi}"], f"foevac{r}")
            plain_ln(v_moh[bi], [f"moh{bi}"], v_moh[bi], f"moh{bi}", f"ln4_{r}_", co, bi)
            vop((lambda b_: lambda e: e.tensor_add(v_xfin[b_][:], v_x2[b_][:], v_moh[b_][:]))(bi),
                [f"x2{bi}", f"moh{bi}"], [xk], f"out_{r}")
            dma("sync", T["out"][r * 128 : (r + 1) * 128, :], v_xfin[bi][:], [xk], [f"outd{r}"], f"stout{r}")

    P.barrier("barEND")

    P.resolve()
    global LAST_PLAN
    LAST_PLAN = P
    from contextlib import ExitStack
    es = ExitStack()
    semmap = {}
    for e in ENGINES:
        semmap["prog_" + e] = es.enter_context(nc.semaphore("prog_" + e))
    for sk in P.dma_slot_sems:
        semmap[sk] = es.enter_context(nc.semaphore(sk))
    block = es.enter_context(nc.Block())

    SEQ = [P.ops[i] for i in P.order]

    def emit_engine(ename):
        def run(eng_obj):
            for o in SEQ:
                if o.engine != ename:
                    continue
                for sk, v in o.waits:
                    eng_obj.wait_ge(semmap[sk], v)
                inst = o.fn(eng_obj)
                for sk, v in o.incs:
                    if inst is None:
                        raise RuntimeError(f"op {o.tag} has incs but returned no instruction")
                    inst = inst.then_inc(semmap[sk], v)
        return run

    block.tensor(emit_engine("tensor"))
    block.vector(emit_engine("vector"))
    block.scalar(emit_engine("scalar"))
    block.sync(emit_engine("sync"))
    block.gpsimd(emit_engine("gpsimd"))
    es.close()
    return nc


def make_in_maps(inputs):
    f32 = np.float32
    x = np.asarray(inputs["L0_attn_x"], f32).reshape(16384, DM)
    xg = np.concatenate([np.zeros((128, DM), f32), x], axis=0)
    wipT = np.ascontiguousarray(np.asarray(inputs["L0_attn_in_proj_w"], f32).T)
    wipA3 = np.ascontiguousarray(
        np.asarray(inputs["L0_attn_in_proj_w"], f32)[2048:4352]
        .reshape(18, 128, 8, 128)
        .transpose(0, 3, 2, 1)
        .reshape(18 * 128, 1024))
    wopT = np.ascontiguousarray(np.asarray(inputs["L0_attn_out_proj_w"], f32).T)
    ident_bf = np.eye(128, dtype=ml_dtypes.bfloat16)
    ident_f32 = np.eye(128, dtype=f32)
    caus = np.triu(np.ones((128, 128), f32))
    in_maps = []
    for i in range(CORES):
        sel = (np.arange(8)[:, None] < i).astype(f32)
        m2t = np.zeros((8, 8), f32)
        for k in range(8):
            for j in range(8):
                m2t[k, j] = 1.0 if (j < k < i) else 0.0
        in_maps.append({
            "x": np.ascontiguousarray(xg[i * SEG : i * SEG + XROWS]),
            "wipT": wipT,
            "wipA3": wipA3,
            "wopT": wopT,
            "conv_w": np.asarray(inputs["L0_attn_conv_w"], f32),
            "conv_wT": np.ascontiguousarray(np.asarray(inputs["L0_attn_conv_w"], f32).T),
            "conv_b": np.asarray(inputs["L0_attn_conv_b"], f32),
            "A_log": np.asarray(inputs["L0_attn_A_log"], f32),
            "Dv": np.asarray(inputs["L0_attn_D"], f32),
            "dt_bias": np.asarray(inputs["L0_attn_dt_bias"], f32),
            "gamma": np.asarray(inputs["L0_attn_gamma"], f32),
            "ln_w": np.asarray(inputs["L0_attn_ln_w"], f32),
            "ln_b": np.asarray(inputs["L0_attn_ln_b"], f32),
            "Win": np.asarray(inputs["L0_ffn_Win"], f32),
            "bin": np.asarray(inputs["L0_ffn_bin_"], f32),
            "W1": np.asarray(inputs["L0_ffn_W1"], f32),
            "b1": np.asarray(inputs["L0_ffn_b1"], f32),
            "W2": np.asarray(inputs["L0_ffn_W2"], f32),
            "b2": np.asarray(inputs["L0_ffn_b2"], f32),
            "Wout2": np.asarray(inputs["L0_ffn_Wout"], f32),
            "bout": np.asarray(inputs["L0_ffn_bout"], f32),
            "ident_bf": ident_bf,
            "ident_f32": ident_f32,
            "caus": caus,
            "sel": sel,
            "m2t": m2t,
            "hmask": np.full((1, 1), 0.0 if i == 0 else 1.0, f32),
        })
    return in_maps


def kernel(**inputs):
    nc = build_program()
    in_maps = make_in_maps(inputs)
    res = run_bass_kernel_spmd(nc, in_maps, core_ids=list(range(CORES)))
    out = np.concatenate([r["out"] for r in res.results], axis=0)
    return out[None]
